# version 02 (final)
# baseline (speedup 1.0000x reference)
import math
from contextlib import ExitStack

import numpy as np
import ml_dtypes

import concourse.bass as bass
import concourse.mybir as mybir
from concourse.bass_utils import run_bass_kernel_spmd

F32 = mybir.dt.float32
BF16 = mybir.dt.bfloat16
I32 = mybir.dt.int32
U32 = mybir.dt.uint32
AF = mybir.ActivationFunctionType
ALU = mybir.AluOpType
AX = mybir.AxisListType

COMPUTE_ENGINES = ("gpsimd", "vector", "scalar", "tensor")
ALL_ENGINES = ("sync",) + COMPUTE_ENGINES

NCORES = 8
DEV_SKIP = set()
B_LOC = 2
T = 2048
NT = B_LOC * T
D = 1024
NTILES = NT // 128


class _Op:
    __slots__ = ("idx", "eng", "fn", "reads", "writes", "inc", "sem", "cum",
                 "waits", "is_barrier", "tag")

    def __init__(self, idx, eng, fn, reads, writes, inc, tag):
        self.idx = idx
        self.eng = eng
        self.fn = fn
        self.reads = reads
        self.writes = writes
        self.inc = inc
        self.sem = None
        self.cum = None
        self.waits = []
        self.is_barrier = False
        self.tag = tag


class Prog:
    def __init__(self, nc):
        self.nc = nc
        self.ops = []
        self.res = {}
        self.eng_sems = {e: f"S_{e}" for e in COMPUTE_ENGINES}
        self.sem_handles = {}
        self.sem_cum = {}
        for e in COMPUTE_ENGINES:
            self._get_sem(self.eng_sems[e])
        self.dma_pool = {"hw": [f"DH_{i}" for i in range(48)],
                         "sw": [f"DS_{i}" for i in range(32)]}
        for cls in self.dma_pool:
            for name in self.dma_pool[cls]:
                self._get_sem(name)
        self.dma_free = {c: list(p) for c, p in self.dma_pool.items()}
        self.dma_bind = {}
        self.watermark = {e: {} for e in ALL_ENGINES}
        self.dma_waiters = {}
        self.n_barriers = 0

    def _get_sem(self, name):
        if name not in self.sem_handles:
            self.sem_handles[name] = self.nc.alloc_semaphore(name)
            self.sem_cum[name] = 0
        return self.sem_handles[name]

    def op(self, eng, fn, reads=(), writes=(), tag=""):
        assert eng in COMPUTE_ENGINES
        return self._record(eng, fn, reads, writes, 1, tag)

    def dma(self, eng, fn, reads=(), writes=(), tag=""):
        assert eng in ("sync", "gpsimd", "scalar")
        assert len(writes) >= 1, "dma must declare its target buffer first in writes"
        return self._record(eng, fn, reads, writes, 16, tag)

    def _record(self, eng, fn, reads, writes, inc, tag):
        reads = list(reads)
        writes = list(writes)
        o = _Op(len(self.ops), eng, fn, reads, writes, inc, tag)
        if inc == 16:
            cls = "sw" if eng == "gpsimd" else "hw"
            base = (cls, writes[0].split(":", 1)[0])
            semname = self.dma_bind.get(base)
            if semname is None:
                assert self.dma_free[cls], f"{cls} DMA sem pool exhausted"
                semname = self.dma_free[cls].pop()
                self.dma_bind[base] = semname
            o.sem = semname
        else:
            o.sem = self.eng_sems[eng]
        dep_ops = set()
        if inc == 16:
            dep_ops.update(self.dma_waiters.get(o.sem, ()))
        for k in reads:
            st = self.res.get(k)
            if st and st["w"] is not None:
                dep_ops.add(st["w"])
        for k in writes:
            st = self.res.get(k)
            if st:
                if st["w"] is not None:
                    dep_ops.add(st["w"])
                dep_ops.update(st["r"])
        wm = self.watermark[eng]
        needed = {}
        for d in sorted(dep_ops):
            dop = self.ops[d]
            if dop.eng == "tensor" and eng == "tensor":
                continue
            v = self.sem_cum[dop.sem] if dop.inc == 16 else dop.cum
            if wm.get(dop.sem, 0) >= v:
                continue
            if needed.get(dop.sem, 0) < v:
                needed[dop.sem] = v
        for semname, v in needed.items():
            o.waits.append((semname, v))
            wm[semname] = v
            if semname not in self.eng_sems.values():
                self.dma_waiters.setdefault(semname, []).append(o.idx)
        self.sem_cum[o.sem] += inc
        o.cum = self.sem_cum[o.sem]
        for k in reads:
            st = self.res.setdefault(k, {"w": None, "r": []})
            st["r"].append(o.idx)
        for k in writes:
            self.res[k] = {"w": o.idx, "r": []}
        self.ops.append(o)
        return o

    def barrier(self):
        targets = [(s, c) for s, c in self.sem_cum.items() if c > 0]
        for e in ALL_ENGINES:
            o = _Op(len(self.ops), e, None, [], [], 0, f"barrier{self.n_barriers}")
            o.is_barrier = True
            wm = self.watermark[e]
            for s, c in targets:
                if wm.get(s, 0) < c:
                    o.waits.append((s, c))
                    wm[s] = c
            self.ops.append(o)
        self.res = {}
        self.dma_bind = {}
        self.dma_waiters = {}
        self.dma_free = {c: list(p) for c, p in self.dma_pool.items()}
        self.n_barriers += 1

    def finalize(self):
        nc = self.nc
        ops_by_eng = {e: [o for o in self.ops if o.eng == e] for e in ALL_ENGINES}

        with nc.Block() as block:
            def emit_for(engobj, eng):
                for o in ops_by_eng[eng]:
                    for semname, v in o.waits:
                        engobj.wait_ge(self.sem_handles[semname], v)
                    if o.fn is not None:
                        inst = o.fn(engobj)
                        inst.then_inc(self.sem_handles[o.sem], o.inc)

            @block.sync
            def _(sync):
                emit_for(sync, "sync")

            @block.gpsimd
            def _(g):
                emit_for(g, "gpsimd")

            @block.vector
            def _(v):
                emit_for(v, "vector")

            @block.scalar
            def _(sc):
                emit_for(sc, "scalar")

            @block.tensor
            def _(t):
                emit_for(t, "tensor")

    def stats(self):
        n = len(self.ops)
        nw = sum(len(o.waits) for o in self.ops)
        per = {e: sum(1 for o in self.ops if o.eng == e) for e in ALL_ENGINES}
        return f"ops={n} waits={nw} {per}"


def K(t, sub=None):
    name = t if isinstance(t, str) else t.tensor.name if isinstance(t, bass.AP) else t.name
    return name if sub is None else f"{name}:{sub}"


def ceil_div(a, b):
    return (a + b - 1) // b


class Ctx:

    def __init__(self, nc, prog, consts, stack):
        self.nc = nc
        self.P = prog
        self.c = consts
        self.stack = stack
        self.dram = {}
        self._uid = 0

    def uname(self, name):
        self._uid += 1
        return f"{name}__{self._uid}"

    def dram_scratch(self, name, shape, dtype):
        if name not in self.dram:
            self.dram[name] = self.nc.dram_tensor(name, shape, dtype, kind="Internal").ap()
        return self.dram[name]

    def sbuf(self, stack, name, shape, dtype):
        return stack.enter_context(self.nc.sbuf_tensor(self.uname(name), shape, dtype))

    def psum(self, stack, name, shape, dtype=F32):
        return stack.enter_context(self.nc.psum_tensor(self.uname(name), shape, dtype))


def _bf(x):
    return np.ascontiguousarray(x).astype(ml_dtypes.bfloat16)


def make_consts():
    c = {}
    c["ident_bf"] = np.eye(128, dtype=ml_dtypes.bfloat16)
    c["ident_f32"] = np.eye(128, dtype=np.float32)
    idlo = np.zeros((64, 128), dtype=np.float32)
    idlo[np.arange(64), np.arange(64)] = 1.0
    idhi = np.zeros((64, 128), dtype=np.float32)
    idhi[np.arange(64), 64 + np.arange(64)] = 1.0
    half = np.concatenate([idlo, idhi], axis=1)
    c["idstack_bf"] = np.concatenate([half, half], axis=0).astype(ml_dtypes.bfloat16)
    c["ones_f32"] = np.ones((1, 128), dtype=np.float32)
    tril = np.tril(np.ones((128, 128), np.float32))
    c["mask_tril_add"] = ((1.0 - tril) * -1e30).astype(np.float32)
    c["tril01_bf"] = _bf(tril)
    c["triu01_bf"] = _bf(np.triu(np.ones((128, 128), np.float32)))
    c["triu01s_bf"] = _bf(np.triu(np.ones((128, 128), np.float32), 1))
    c["tril01s_bf"] = _bf(np.tril(np.ones((128, 128), np.float32), -1))
    for r in range(3):
        m = (np.arange(128)[:, None] <= 3 * np.arange(43)[None, :] + r)
        c[f"stair{r}_bf"] = _bf(m.astype(np.float32))
    hd = 64
    inv_freq = 1.0 / 10000.0 ** (np.arange(0, hd, 2, dtype=np.float32) / hd)
    pos = np.arange(T, dtype=np.float32)
    emb = np.repeat(pos[:, None] * inv_freq[None, :], 2, axis=-1)
    c["rope0_cos"] = np.tile(np.cos(emb), (1, 16)).astype(np.float32)
    c["rope0_sin"] = np.tile(np.sin(emb), (1, 16)).astype(np.float32)
    c["iota_row"] = np.arange(T, dtype=np.float32).reshape(1, T)
    c["mask_triu_add"] = ((1.0 - np.triu(np.ones((128, 128), np.float32))) * -1e30
                          ).astype(np.float32)
    hi = np.arange(0, 64, 2, dtype=np.float32)
    invf = 1.0 / 10000.0 ** (hi / 64.0)
    tt_ = np.arange(T, dtype=np.float32)
    freqs = np.concatenate([tt_[:, None] * invf[None, :]] * 2, axis=-1)
    xbase = (hi + 0.4 * 64) / (1.4 * 64)
    xpow = (tt_ - T // 2) / 256.0
    xscale = np.concatenate([xbase[None, :] ** xpow[:, None]] * 2, axis=-1
                            ).astype(np.float32)
    c["xpos_qc"] = (np.cos(freqs) * xscale).astype(np.float32)
    c["xpos_qs"] = (np.sin(freqs) * xscale).astype(np.float32)
    c["xpos_kc"] = (np.cos(freqs) / xscale).astype(np.float32)
    c["xpos_ks"] = (np.sin(freqs) / xscale).astype(np.float32)
    return c


def prep_inputs(inp):
    d = {}
    d["x_in"] = np.ascontiguousarray(inp["L0_attn_x"])

    for w in ("Wq", "Wk", "Wv", "Wo"):
        d[f"l0_{w}t"] = np.ascontiguousarray(inp[f"L0_attn_{w}"].T)
    d["l0_router_w"] = np.ascontiguousarray(inp["L0_ffn_router_w"])
    d["l0_w1"] = np.ascontiguousarray(inp["L0_ffn_w1"])
    d["l0_b1"] = np.ascontiguousarray(inp["L0_ffn_b1"])
    d["l0_w2"] = np.ascontiguousarray(inp["L0_ffn_w2"])
    d["l0_b2"] = np.ascontiguousarray(inp["L0_ffn_b2"])

    for w in ("Wq", "Wk", "Wv"):
        d[f"l2_{w}"] = np.ascontiguousarray(
            inp[f"L2_attn_{w}"].transpose(1, 0, 2).reshape(D, D))
    d["l2_proj_wt"] = np.ascontiguousarray(inp["L2_attn_proj_w"].T)
    d["l2_proj_b"] = np.ascontiguousarray(inp["L2_attn_proj_b"])
    d["l2_Wrt"] = np.ascontiguousarray(inp["L2_ffn_Wr"].T)
    d["l2_We"] = np.ascontiguousarray(inp["L2_ffn_We"])

    for w in ("Wq", "Wk", "Wv", "Wo", "G1"):
        d[f"l1_{w}"] = np.ascontiguousarray(inp[f"L1_attn_{w}"])
    d["l1_G2"] = np.ascontiguousarray(inp["L1_attn_G2"])
    d["l1_gamma"] = np.ascontiguousarray(inp["L1_attn_gamma"])
    d["l1_beta"] = np.ascontiguousarray(inp["L1_attn_beta"])
    d["l1_theta"] = np.ascontiguousarray(inp["L1_attn_theta"])

    d["l1f_wqkv"] = np.ascontiguousarray(inp["L1_ffn_wqkv"])
    d["l1f_q_scale"] = np.ascontiguousarray(inp["L1_ffn_q_scale"])
    d["l1f_k_scale"] = np.ascontiguousarray(inp["L1_ffn_k_scale"])
    d["l1f_wo"] = np.ascontiguousarray(inp["L1_ffn_wo"])
    d["l1f_ln_g"] = np.ascontiguousarray(inp["L1_ffn_ln_g"])
    d["l1f_ln_b"] = np.ascontiguousarray(inp["L1_ffn_ln_b"])
    w1 = inp["L1_ffn_w1"]
    w1p = np.zeros((2, 1024, 5632), np.float32)
    w1p[:, :, 0:2730] = w1[:, :, 0:2730]
    w1p[:, :, 2816:5546] = w1[:, :, 2730:5460]
    d["l1f_w1p"] = w1p
    b1 = inp["L1_ffn_b1"]
    b1p = np.zeros((2, 5632), np.float32)
    b1p[:, 0:2730] = b1[:, 0:2730]
    b1p[:, 2816:5546] = b1[:, 2730:5460]
    d["l1f_b1p"] = b1p
    w2 = inp["L1_ffn_w2"]
    w2p = np.zeros((2, 2816, 1024), np.float32)
    w2p[:, 0:2730, :] = w2
    d["l1f_w2p"] = w2p
    d["l1f_b2"] = np.ascontiguousarray(inp["L1_ffn_b2"])

    for w in ("Wq", "Wk", "Wv", "Wb", "Wa", "Wg", "Wo"):
        d[f"l3a_{w}"] = np.ascontiguousarray(inp[f"L3_attn_{w}"])
    for w in ("conv_q", "conv_k", "conv_v"):
        d[f"l3a_{w}"] = np.ascontiguousarray(inp[f"L3_attn_{w}"])
    d["l3a_A_log"] = np.ascontiguousarray(inp["L3_attn_A_log"])
    d["l3a_dt_bias"] = np.ascontiguousarray(inp["L3_attn_dt_bias"])
    d["l3a_rmsw"] = np.ascontiguousarray(inp["L3_attn_rms_weight"])

    d["l3f_gwt"] = np.ascontiguousarray(inp["L3_ffn_gate_w"].T)
    d["l3f_gb"] = np.ascontiguousarray(inp["L3_ffn_gate_b"])
    d["l3f_w1t"] = np.ascontiguousarray(inp["L3_ffn_w1"].transpose(0, 2, 1))
    d["l3f_b1"] = np.ascontiguousarray(inp["L3_ffn_b1"])
    d["l3f_w2t"] = np.ascontiguousarray(inp["L3_ffn_w2"].transpose(0, 2, 1))
    d["l3f_b2"] = np.ascontiguousarray(inp["L3_ffn_b2"])
    return d


def load_weight_bf16(ctx, stack, wname, src_ap, kchunks, ncols, tag=""):
    P = ctx.P
    t = ctx.sbuf(stack, f"w_{wname}", [128, kchunks * ncols], BF16)
    for c in range(kchunks):
        P.dma("gpsimd",
              lambda g, t=t, c=c: g.dma_start(
                  out=t[:, c * ncols:(c + 1) * ncols],
                  in_=src_ap[c * 128:(c + 1) * 128, 0:ncols]),
              writes=[K(t, c)], tag=f"ldw_{wname}{c}")
    return t


def sbuf_const(ctx, stack, name, ncols=None, dtype=None):
    P = ctx.P
    if name == "ones_f32":
        t = ctx.sbuf(stack, "c_ones_f32", [128, 128], F32)
        P.op("vector", lambda v, t=t: v.memset(t[:], 1.0), writes=[K(t)],
             tag="mkones")
        return t
    ap = ctx.c[name]
    shape = list(ap.shape)
    t = ctx.sbuf(stack, f"c_{name}", shape, ap.dtype)
    P.dma("sync", lambda s, t=t, ap=ap: s.dma_start(out=t[:], in_=ap[:]),
          writes=[K(t)], tag=f"ldc_{name}")
    return t


class RotBuf:

    def __init__(self, ctx, stack, name, n, shape, dtype):
        self.ts = [ctx.sbuf(stack, f"{name}{i}", shape, dtype) for i in range(n)]
        self.n = n

    def __getitem__(self, i):
        return self.ts[i % self.n]


def colize(ctx, stack, name, src_d, n, ident_f):
    P = ctx.P
    m = n // 128
    rows = ctx.sbuf(stack, f"{name}_rows", [m, 128], F32)
    ps = ctx.psum(stack, f"{name}_ps", [128, 512], F32)
    cols = ctx.sbuf(stack, f"{name}_cols", [128, m], F32)
    P.dma("sync", lambda s: s.dma_start(
        out=rows[:], in_=src_d.rearrange("(m p) -> m p", p=128)),
        writes=[K(rows)], tag=f"{name}_ld")
    P.op("tensor", lambda t: t.transpose(ps[:, 0:m], rows[:], ident_f[0:m, 0:m]),
         reads=[K(rows), K(ident_f)], writes=[K(ps)], tag=f"{name}_tr")
    P.op("vector", lambda v: v.tensor_copy(cols[:], ps[:, 0:m]),
         reads=[K(ps)], writes=[K(cols)], tag=f"{name}_ev")
    return cols


def bcast_row(ctx, stack, name, row_sbuf, ncols, ones_f, psum=None):
    P = ctx.P
    own = psum is None
    ps = ctx.psum(stack, f"{name}_ps", [128, ceil_div(ncols, 512) * 512], F32) if own else psum
    out = ctx.sbuf(stack, f"{name}_bc", [128, ncols], F32)
    nch = ceil_div(ncols, 512)
    for c in range(nch):
        c0, c1 = c * 512, min(ncols, (c + 1) * 512)
        P.op("tensor", lambda t, c0=c0, c1=c1: t.matmul(
            ps[:, c0:c1], ones_f[0:1, 0:128], row_sbuf[0:1, c0:c1],
            start=True, stop=True),
            reads=[K(row_sbuf), K(ones_f)], writes=[K(ps)], tag=f"{name}_bc{c}")
    P.op("vector", lambda v: v.tensor_copy(out[:], ps[:, 0:ncols]),
         reads=[K(ps)], writes=[K(out)], tag=f"{name}_ev")
    return out


def phase_ln(ctx, x_src_d, xn_fm_d, xn_f32_d=None, xn_fm32_d=None, tag="ln"):
    P, nc = ctx.P, ctx.nc
    with ExitStack() as st:
        xb = RotBuf(ctx, st, f"{tag}_x", 3, [128, D], F32)
        xcb = RotBuf(ctx, st, f"{tag}_xc", 2, [128, D], F32)
        sq = RotBuf(ctx, st, f"{tag}_sq", 2, [128, D], F32)
        xn = RotBuf(ctx, st, f"{tag}_xn", 2, [128, D], F32)
        xnb = RotBuf(ctx, st, f"{tag}_xnb", 2, [128, D], BF16)
        stc = RotBuf(ctx, st, f"{tag}_st", 2, [128, 8], F32)
        fmst = RotBuf(ctx, st, f"{tag}_fm", 2, [128, 8 * 512], BF16)
        fmst32 = RotBuf(ctx, st, f"{tag}_fm32", 2, [128, 1024], F32) \
            if xn_fm32_d is not None else None
        ident_b = sbuf_const(ctx, st, "ident_bf")
        ident_f = sbuf_const(ctx, st, "ident_f32")
        ps_t = ctx.psum(st, f"{tag}_pst", [128, 1024], BF16)
        ps_t32 = ctx.psum(st, f"{tag}_pst32", [128, 1024], F32) \
            if xn_fm32_d is not None else None
        eps = ctx.sbuf(st, f"{tag}_eps", [128, 1], F32)
        P.op("vector", lambda v: v.memset(eps[:], 1e-5), writes=[K(eps)])
        for i in range(NTILES):
            x_s, xc_s, sq_s, xn_s, xnb_s, c = xb[i], xcb[i], sq[i], xn[i], xnb[i], stc[i]
            tc, tt = i // 4, i % 4
            fm_s = fmst[tc]
            r = i * 128
            P.dma("sync", lambda s, x_s=x_s, r=r: s.dma_start(
                out=x_s[:], in_=x_src_d[r:r + 128, :]),
                writes=[K(x_s)], tag=f"{tag}_ldx{i}")
            P.op("vector", lambda v, x_s=x_s, c=c: v.reduce_sum(
                c[:, 0:1], x_s[:], axis=AX.X),
                reads=[K(x_s)], writes=[K(c, "sum")], tag=f"{tag}_sum{i}")
            P.op("vector", lambda v, c=c: v.tensor_scalar_mul(c[:, 2:3], c[:, 0:1], 1.0 / D),
                 reads=[K(c, "sum")], writes=[K(c, "mu")], tag=f"{tag}_mu{i}")
            P.op("vector", lambda v, x_s=x_s, xc_s=xc_s, c=c: v.tensor_scalar(
                xc_s[:], x_s[:], c[:, 2:3], None, ALU.subtract),
                reads=[K(x_s), K(c, "mu")], writes=[K(xc_s)], tag=f"{tag}_xc{i}")
            P.op("scalar", lambda s, xc_s=xc_s, sq_s=sq_s, c=c: s.activation(
                sq_s[:], xc_s[:], AF.Square, accum_out=c[:, 1:2]),
                reads=[K(xc_s)], writes=[K(sq_s), K(c, "sq")], tag=f"{tag}_sq{i}")
            P.op("vector", lambda v, c=c: v.tensor_scalar_mul(c[:, 3:4], c[:, 1:2], 1.0 / D),
                 reads=[K(c, "sq")], writes=[K(c, "var")], tag=f"{tag}_var{i}")
            P.op("scalar", lambda s, c=c: s.activation(c[:, 6:7], c[:, 3:4], AF.Sqrt,
                                                       bias=eps[:, 0:1]),
                 reads=[K(c, "var"), K(eps)], writes=[K(c, "sd")], tag=f"{tag}_sd{i}")
            P.op("vector", lambda v, c=c: v.reciprocal(c[:, 7:8], c[:, 6:7]),
                 reads=[K(c, "sd")], writes=[K(c, "rstd")], tag=f"{tag}_rstd{i}")
            P.op("vector", lambda v, xc_s=xc_s, xn_s=xn_s, c=c: v.tensor_scalar(
                xn_s[:], xc_s[:], c[:, 7:8], None, ALU.mult),
                 reads=[K(xc_s), K(c, "rstd")], writes=[K(xn_s)],
                 tag=f"{tag}_xn{i}")
            P.op("scalar", lambda s, xn_s=xn_s, xnb_s=xnb_s: s.copy(xnb_s[:], xn_s[:]),
                 reads=[K(xn_s)], writes=[K(xnb_s)], tag=f"{tag}_cast{i}")
            def tr8(t, xnb_s=xnb_s):
                for k in range(8):
                    inst = t.transpose(ps_t[:, k * 128:(k + 1) * 128],
                                       xnb_s[:, k * 128:(k + 1) * 128], ident_b[:])
                return inst
            P.op("tensor", tr8, reads=[K(xnb_s), K(ident_b)], writes=[K(ps_t)],
                 tag=f"{tag}_tr{i}")
            P.op("vector", lambda v, fm_s=fm_s, tt=tt: v.tensor_copy(
                fm_s[:].rearrange("p (k t) -> p k t", k=8)[:, :, tt * 128:(tt + 1) * 128],
                ps_t[:].rearrange("p (k t) -> p k t", k=8)),
                reads=[K(ps_t)], writes=[K(fm_s, tt)], tag=f"{tag}_fmev{i}")
            if tt == 3:
                for k in range(8):
                    P.dma("sync", lambda s, fm_s=fm_s, k=k, tc=tc: s.dma_start(
                        out=xn_fm_d[k * 128:(k + 1) * 128, tc * 512:(tc + 1) * 512],
                        in_=fm_s[:, k * 512:(k + 1) * 512]),
                        reads=[K(fm_s, 0), K(fm_s, 1), K(fm_s, 2), K(fm_s, 3)],
                        writes=[K(xn_fm_d, (tc, k))], tag=f"{tag}_fmst{tc}_{k}")
            if xn_fm32_d is not None:
                fm32_s = fmst32[i]
                for hk in range(2):
                    def tr4f(t, xn_s=xn_s, hk=hk):
                        for k in range(4):
                            inst = t.transpose(
                                ps_t32[:, k * 128:(k + 1) * 128],
                                xn_s[:, (hk * 4 + k) * 128:(hk * 4 + k) * 128 + 128],
                                ident_f[:])
                        return inst
                    P.op("tensor", tr4f, reads=[K(xn_s), K(ident_f)],
                         writes=[K(ps_t32)], tag=f"{tag}_tr32{i}_{hk}")
                    P.op("vector", lambda v, fm32_s=fm32_s, hk=hk: v.tensor_copy(
                        fm32_s[:, hk * 512:(hk + 1) * 512], ps_t32[:, 0:512]),
                        reads=[K(ps_t32)], writes=[K(fm32_s, hk)],
                        tag=f"{tag}_fm32ev{i}_{hk}")
                    for k in range(4):
                        kk = hk * 4 + k
                        P.dma("gpsimd", lambda s, fm32_s=fm32_s, hk=hk, k=k, kk=kk, i=i: s.dma_start(
                            out=xn_fm32_d[kk * 128:(kk + 1) * 128,
                                          i * 128:(i + 1) * 128],
                            in_=fm32_s[:, hk * 512 + k * 128: hk * 512 + (k + 1) * 128]),
                            reads=[K(fm32_s, hk)],
                            writes=[K(xn_fm32_d, (i, kk))],
                            tag=f"{tag}_fm32st{i}_{kk}")
            if xn_f32_d is not None:
                P.dma("gpsimd", lambda g, xn_s=xn_s, r=r: g.dma_start(
                    out=xn_f32_d[r:r + 128, :], in_=xn_s[:]),
                    reads=[K(xn_s)], writes=[K(xn_f32_d, i)], tag=f"{tag}_stf{i}")
    P.barrier()


def phase_combine(ctx, x_src_d, s_d, g_d, x_dst_d, tag="cmb"):
    P = ctx.P
    with ExitStack() as st:
        xb = RotBuf(ctx, st, f"{tag}_x", 2, [128, D], F32)
        sb_ = RotBuf(ctx, st, f"{tag}_s", 2, [128, D], F32)
        gb = RotBuf(ctx, st, f"{tag}_g", 2, [128, D], F32)
        ob = RotBuf(ctx, st, f"{tag}_o", 2, [128, D], F32)
        for i in range(NTILES):
            x_s, s_s, g_s, o_s = xb[i], sb_[i], gb[i], ob[i]
            r = i * 128
            P.dma("sync", lambda s, x_s=x_s, r=r: s.dma_start(out=x_s[:], in_=x_src_d[r:r + 128, :]),
                  writes=[K(x_s)], tag=f"{tag}_ldx{i}")
            P.dma("sync", lambda s, s_s=s_s, r=r: s.dma_start(out=s_s[:], in_=s_d[r:r + 128, :]),
                  writes=[K(s_s)], tag=f"{tag}_lds{i}")
            P.dma("sync", lambda s, g_s=g_s, r=r: s.dma_start(out=g_s[:], in_=g_d[r:r + 128, :]),
                  writes=[K(g_s)], tag=f"{tag}_ldg{i}")
            P.op("vector", lambda v, s_s=s_s, g_s=g_s, o_s=o_s: v.tensor_mul(o_s[:], s_s[:], g_s[:]),
                 reads=[K(s_s), K(g_s)], writes=[K(o_s)], tag=f"{tag}_sg{i}")
            P.op("vector", lambda v, x_s=x_s, o_s=o_s: v.tensor_add(o_s[:], o_s[:], x_s[:]),
                 reads=[K(x_s)], writes=[K(o_s)], tag=f"{tag}_add{i}")
            P.dma("sync", lambda s, o_s=o_s, r=r: s.dma_start(out=x_dst_d[r:r + 128, :], in_=o_s[:]),
                  reads=[K(o_s)], writes=[K(x_dst_d, i)], tag=f"{tag}_st{i}")
    P.barrier()


C1_GELU = float(np.sqrt(2.0 / np.pi))


def l0_ffn(ctx, xn_fm_d, g_d, weights):
    P, nc = ctx.P, ctx.nc
    hid_d = ctx.dram_scratch("l0_hid", [4096, NT], BF16)
    fac_d = ctx.dram_scratch("l0_fac", [NT, 1], F32)

    with ExitStack() as st:
        w1_s = load_weight_bf16(ctx, st, "l0w1", weights["l0_w1"], 8, 4096)
        rw_s = ctx.sbuf(st, "l0rw", [128, 8 * 8], BF16)
        for c in range(8):
            P.dma("gpsimd", lambda g, c=c: g.dma_start(
                out=rw_s[:, c * 8:(c + 1) * 8],
                in_=weights["l0_router_w"][c * 128:(c + 1) * 128, :]),
                writes=[K(rw_s, c)])
        ident_f = sbuf_const(ctx, st, "ident_f32")
        b1c = colize(ctx, st, "l0b1", weights["l0_b1"], 4096, ident_f)
        xnf = RotBuf(ctx, st, "l0f_xn", 2, [128, 8 * 512], BF16)
        hbuf = RotBuf(ctx, st, "l0f_h", 3, [128, 512], F32)
        gel = RotBuf(ctx, st, "l0f_ge", 3, [128, 512], F32)
        gelb = RotBuf(ctx, st, "l0f_gb", 3, [128, 512], BF16)
        lgt = RotBuf(ctx, st, "l0f_lg", 2, [128, 24], F32)
        facb = RotBuf(ctx, st, "l0f_fac", 2, [128, 1], F32)
        ps_h = ctx.psum(st, "l0f_ps", [128, 512], F32)
        ps_h2 = ctx.psum(st, "l0f_ps2", [128, 512], F32)
        ps_l = ctx.psum(st, "l0f_psl", [128, 512], F32)
        for tc in range(8):
            xn_s = xnf[tc]
            for k in range(8):
                P.dma("sync", lambda s, xn_s=xn_s, k=k, tc=tc: s.dma_start(
                    out=xn_s[:, k * 512:(k + 1) * 512],
                    in_=xn_fm_d[k * 128:(k + 1) * 128, tc * 512:(tc + 1) * 512]),
                    writes=[K(xn_s, k)], tag=f"l0f_ldxn{tc}_{k}")
            xn_keys = [K(xn_s, k) for k in range(8)]
            for tt in range(4):
                lg = lgt[tt]
                fac = facb[tt]
                tok0 = tc * 512 + tt * 128
                P.op("tensor", lambda t, xn_s=xn_s, tt=tt: [
                    t.matmul(ps_l[:, 0:8], xn_s[:, k * 512 + tt * 128: k * 512 + tt * 128 + 128],
                             rw_s[:, k * 8:(k + 1) * 8], start=(k == 0), stop=(k == 7))
                    for k in range(8)][-1],
                    reads=xn_keys + [K(rw_s, k) for k in range(8)],
                    writes=[K(ps_l)], tag=f"l0f_lg{tc}_{tt}")
                P.op("vector", lambda v, lg=lg: v.reduce_max(lg[:, 8:9], ps_l[:, 0:8], axis=AX.X),
                     reads=[K(ps_l)], writes=[K(lg, "m")], tag=f"l0f_lgm{tc}_{tt}")
                P.op("vector", lambda v, lg=lg: v.tensor_scalar_mul(lg[:, 9:10], lg[:, 8:9], -1.0),
                     reads=[K(lg, "m")], writes=[K(lg, "nm")])
                P.op("scalar", lambda s, lg=lg: s.activation(
                    lg[:, 0:8], ps_l[:, 0:8], AF.Exp, bias=lg[:, 9:10]),
                    reads=[K(ps_l), K(lg, "nm")], writes=[K(lg, "e")], tag=f"l0f_exp{tc}_{tt}")
                P.op("vector", lambda v, lg=lg: v.max(lg[:, 10:18], lg[:, 0:8]),
                     reads=[K(lg, "e")], writes=[K(lg, "top")], tag=f"l0f_top{tc}_{tt}")
                P.op("vector", lambda v, lg=lg: v.tensor_add(lg[:, 18:19], lg[:, 10:11], lg[:, 11:12]),
                     reads=[K(lg, "top")], writes=[K(lg, "u")])
                P.op("vector", lambda v, lg=lg: v.reciprocal(lg[:, 19:20], lg[:, 18:19]),
                     reads=[K(lg, "u")], writes=[K(lg, "ru")])
                P.op("vector", lambda v, lg=lg, fac=fac: v.tensor_mul(fac[:], lg[:, 18:19], lg[:, 19:20]),
                     reads=[K(lg, "u"), K(lg, "ru")], writes=[K(fac)], tag=f"l0f_fac{tc}_{tt}")
                P.dma("sync", lambda s, fac=fac, tok0=tok0: s.dma_start(
                    out=fac_d[tok0:tok0 + 128, :], in_=fac[:]),
                    reads=[K(fac)], writes=[K(fac_d, tok0)], tag=f"l0f_facst{tc}_{tt}")
            for m in range(32):
                ps = ps_h if m % 2 == 0 else ps_h2
                h_s, ge_s, gb_s = hbuf[m], gel[m], gelb[m]
                P.op("tensor", lambda t, xn_s=xn_s, m=m, ps=ps: [
                    t.matmul(ps[:], w1_s[:, k * 4096 + m * 128: k * 4096 + m * 128 + 128],
                             xn_s[:, k * 512:(k + 1) * 512], start=(k == 0), stop=(k == 7))
                    for k in range(8)][-1],
                    reads=xn_keys + [K(w1_s, k) for k in range(8)],
                    writes=[K(ps)], tag=f"l0f_mm{tc}_{m}")
                P.op("vector", lambda v, h_s=h_s, ps=ps, m=m: v.tensor_scalar(
                    h_s[:], ps[:], b1c[:, m:m + 1], None, ALU.add),
                    reads=[K(ps), K(b1c)], writes=[K(h_s)], tag=f"l0f_b1{tc}_{m}")
                P.op("scalar", lambda s, h_s=h_s, ge_s=ge_s: s.activation(ge_s[:], h_s[:], AF.Square),
                     reads=[K(h_s)], writes=[K(ge_s)])
                P.op("vector", lambda v, h_s=h_s, ge_s=ge_s: v.tensor_mul(ge_s[:], ge_s[:], h_s[:]),
                     reads=[K(ge_s), K(h_s)], writes=[K(ge_s)])
                P.op("vector", lambda v, h_s=h_s, ge_s=ge_s: v.scalar_tensor_tensor(
                    ge_s[:], ge_s[:], 0.044715, h_s[:], ALU.mult, ALU.add),
                    reads=[K(ge_s), K(h_s)], writes=[K(ge_s)])
                P.op("scalar", lambda s, ge_s=ge_s: s.activation(ge_s[:], ge_s[:], AF.Tanh,
                                                                scale=C1_GELU),
                     reads=[K(ge_s)], writes=[K(ge_s)])
                P.op("vector", lambda v, ge_s=ge_s: v.tensor_scalar(
                    ge_s[:], ge_s[:], 1.0, 0.5, ALU.add, ALU.mult),
                    reads=[K(ge_s)], writes=[K(ge_s)])
                P.op("vector", lambda v, h_s=h_s, ge_s=ge_s, gb_s=gb_s: v.tensor_mul(
                    gb_s[:], ge_s[:], h_s[:]),
                    reads=[K(ge_s), K(h_s)], writes=[K(gb_s)], tag=f"l0f_gel{tc}_{m}")
                P.dma("gpsimd", lambda g, gb_s=gb_s, m=m, tc=tc: g.dma_start(
                    out=hid_d[m * 128:(m + 1) * 128, tc * 512:(tc + 1) * 512], in_=gb_s[:]),
                    reads=[K(gb_s)], writes=[K(hid_d, (tc, m))], tag=f"l0f_sth{tc}_{m}")
    P.barrier()

    with ExitStack() as st:
        w2_s = load_weight_bf16(ctx, st, "l0w2", weights["l0_w2"], 32, 1024)
        ones_f = sbuf_const(ctx, st, "ones_f32")
        b2row = ctx.sbuf(st, "l0b2r", [1, D], F32)
        P.dma("sync", lambda s: s.dma_start(
            out=b2row[:], in_=weights["l0_b2"].rearrange("(o d) -> o d", o=1)),
              writes=[K(b2row)])
        b2b = bcast_row(ctx, st, "l0b2", b2row, D, ones_f)
        hh = RotBuf(ctx, st, "l0g_h", 2, [128, 32 * 128], BF16)
        yb = RotBuf(ctx, st, "l0g_y", 2, [128, D], F32)
        fb = RotBuf(ctx, st, "l0g_f", 2, [128, 1], F32)
        ps_y = ctx.psum(st, "l0g_ps", [128, 512], F32)
        ps_y2 = ctx.psum(st, "l0g_ps2", [128, 512], F32)
        for tt in range(NTILES):
            h_s, y_s, f_s = hh[tt], yb[tt], fb[tt]
            tok0 = tt * 128
            for kc in range(32):
                P.dma("sync", lambda s, h_s=h_s, kc=kc, tok0=tok0: s.dma_start(
                    out=h_s[:, kc * 128:(kc + 1) * 128],
                    in_=hid_d[kc * 128:(kc + 1) * 128, tok0:tok0 + 128]),
                    writes=[K(h_s, kc)], tag=f"l0g_ldh{tt}_{kc}")
            P.dma("sync", lambda s, f_s=f_s, tok0=tok0: s.dma_start(
                out=f_s[:], in_=fac_d[tok0:tok0 + 128, :]),
                writes=[K(f_s)], tag=f"l0g_ldf{tt}")
            h_keys = [K(h_s, kc) for kc in range(32)]
            for oc in range(2):
                ps = ps_y if oc == 0 else ps_y2
                P.op("tensor", lambda t, h_s=h_s, oc=oc, ps=ps: [
                    t.matmul(ps[:], h_s[:, kc * 128:(kc + 1) * 128],
                             w2_s[:, kc * 1024 + oc * 512: kc * 1024 + oc * 512 + 512],
                             start=(kc == 0), stop=(kc == 31))
                    for kc in range(32)][-1],
                    reads=h_keys + [K(w2_s, kc) for kc in range(32)],
                    writes=[K(ps)], tag=f"l0g_mm{tt}_{oc}")
                P.op("vector", lambda v, y_s=y_s, oc=oc, ps=ps: v.tensor_add(
                    y_s[:, oc * 512:(oc + 1) * 512], ps[:], b2b[:, oc * 512:(oc + 1) * 512]),
                    reads=[K(ps), K(b2b)], writes=[K(y_s, oc)], tag=f"l0g_b2{tt}_{oc}")
            P.op("vector", lambda v, y_s=y_s, f_s=f_s: v.tensor_scalar(
                y_s[:], y_s[:], f_s[:, 0:1], None, ALU.mult),
                reads=[K(f_s)], writes=[K(y_s, 0), K(y_s, 1)],
                tag=f"l0g_fac{tt}")
            P.op("scalar", lambda s, y_s=y_s: s.activation(y_s[:], y_s[:], AF.Silu),
                 reads=[], writes=[K(y_s, 0), K(y_s, 1)], tag=f"l0g_silu{tt}")
            P.dma("sync", lambda s, y_s=y_s, tok0=tok0: s.dma_start(
                out=g_d[tok0:tok0 + 128, :], in_=y_s[:]),
                reads=[K(y_s, 0), K(y_s, 1)], writes=[K(g_d, tt)], tag=f"l0g_st{tt}")
    P.barrier()


def l0_attn(ctx, xn_fm_d, s_d, weights):
    P, nc = ctx.P, ctx.nc
    qf_d = ctx.dram_scratch("l0_qfm", [D, NT], BF16)
    kf_d = ctx.dram_scratch("l0_kfm", [D, NT], BF16)
    v_d = ctx.dram_scratch("l0_vtm", [NT, D], BF16)

    with ExitStack() as st:
        wq_s = load_weight_bf16(ctx, st, "l0wq", weights["l0_Wqt"], 8, D)
        wk_s = load_weight_bf16(ctx, st, "l0wk", weights["l0_Wkt"], 8, D)
        wv_s = load_weight_bf16(ctx, st, "l0wv", weights["l0_Wvt"], 8, D)
        ident_b = sbuf_const(ctx, st, "ident_bf")
        xnf = RotBuf(ctx, st, "l0a_xn", 2, [128, 8 * 512], BF16)
        cosb = RotBuf(ctx, st, "l0a_cos", 2, [128, D], F32)
        sinb = RotBuf(ctx, st, "l0a_sin", 2, [128, D], F32)
        t1b = RotBuf(ctx, st, "l0a_t1", 2, [128, 512], F32)
        rotb = RotBuf(ctx, st, "l0a_rot", 2, [128, 512], F32)
        outb = RotBuf(ctx, st, "l0a_o", 3, [128, 512], BF16)
        qst = RotBuf(ctx, st, "l0a_qst", 2, [128, 8 * 512], BF16)
        kst = RotBuf(ctx, st, "l0a_kst", 2, [128, 8 * 512], BF16)
        ps = [ctx.psum(st, f"l0a_ps{i}", [128, 512], F32) for i in range(4)]
        ps_t = ctx.psum(st, "l0a_pst", [128, 1024], BF16)
        for tc in range(8):
            xn_s = xnf[tc]
            for k in range(8):
                P.dma("sync", lambda s, xn_s=xn_s, k=k, tc=tc: s.dma_start(
                    out=xn_s[:, k * 512:(k + 1) * 512],
                    in_=xn_fm_d[k * 128:(k + 1) * 128, tc * 512:(tc + 1) * 512]),
                    writes=[K(xn_s, k)], tag=f"l0a_ldxn{tc}_{k}")
            xn_keys = [K(xn_s, k) for k in range(8)]
            for tt in range(4):
                ti = tc * 4 + tt
                tok0 = ti * 128
                cos_s, sin_s = cosb[ti], sinb[ti]
                trow = tok0 % T
                P.dma("sync", lambda s, cos_s=cos_s, trow=trow: s.dma_start(
                    out=cos_s[:], in_=ctx.c["rope0_cos"][trow:trow + 128, :]),
                    writes=[K(cos_s)], tag=f"l0a_ldc{ti}")
                P.dma("sync", lambda s, sin_s=sin_s, trow=trow: s.dma_start(
                    out=sin_s[:], in_=ctx.c["rope0_sin"][trow:trow + 128, :]),
                    writes=[K(sin_s)], tag=f"l0a_lds{ti}")
                for pi, (w_s, rope) in enumerate(
                        ((wq_s, True), (wk_s, True), (wv_s, False))):
                    w_keys = [K(w_s, k) for k in range(8)]
                    for oc in range(2):
                        psb = ps[(pi * 2 + oc) % 4]
                        P.op("tensor", lambda t, xn_s=xn_s, w_s=w_s, tt=tt, oc=oc, psb=psb: [
                            t.matmul(psb[:],
                                     xn_s[:, k * 512 + tt * 128: k * 512 + tt * 128 + 128],
                                     w_s[:, k * 1024 + oc * 512: k * 1024 + oc * 512 + 512],
                                     start=(k == 0), stop=(k == 7))
                            for k in range(8)][-1],
                            reads=xn_keys + w_keys, writes=[K(psb)],
                            tag=f"l0a_mm{ti}_{pi}{oc}")
                        o_s = outb[ti * 6 + pi * 2 + oc]
                        if rope:
                            t1, rot = t1b[pi * 2 + oc], rotb[pi * 2 + oc]
                            cs = (cos_s[:, oc * 512:(oc + 1) * 512],
                                  sin_s[:, oc * 512:(oc + 1) * 512])
                            P.op("vector", lambda v, t1=t1, psb=psb, c=cs[0]: v.tensor_mul(
                                t1[:], psb[:], c),
                                reads=[K(psb), K(cos_s)], writes=[K(t1)])
                            P.op("vector", lambda v, rot=rot, psb=psb: v.tensor_scalar_mul(
                                rot[:, 0::2], psb[:, 1::2], -1.0),
                                reads=[K(psb)], writes=[K(rot)])
                            P.op("vector", lambda v, rot=rot, psb=psb: v.tensor_copy(
                                rot[:, 1::2], psb[:, 0::2]),
                                reads=[K(psb)], writes=[K(rot)])
                            P.op("vector", lambda v, rot=rot, s_=cs[1]: v.tensor_mul(
                                rot[:], rot[:], s_),
                                reads=[K(sin_s)], writes=[K(rot)])
                            P.op("vector", lambda v, o_s=o_s, t1=t1, rot=rot: v.tensor_add(
                                o_s[:], t1[:], rot[:]),
                                reads=[K(t1), K(rot)], writes=[K(o_s)],
                                tag=f"l0a_rope{ti}_{pi}{oc}")
                            stg = qst[tc] if pi == 0 else kst[tc]

                            def tr4(t, o_s=o_s):
                                for j in range(4):
                                    inst = t.transpose(
                                        ps_t[:, j * 128:(j + 1) * 128],
                                        o_s[:, j * 128:(j + 1) * 128], ident_b[:])
                                return inst
                            P.op("tensor", tr4, reads=[K(o_s), K(ident_b)],
                                 writes=[K(ps_t)], tag=f"l0a_qtr{ti}_{pi}{oc}")
                            P.op("vector", lambda v, stg=stg, oc=oc, tt=tt: v.tensor_copy(
                                stg[:].rearrange("p (k t) -> p k t", k=8)[
                                    :, oc * 4:(oc + 1) * 4, tt * 128:(tt + 1) * 128],
                                ps_t[:, 0:512].rearrange("p (k t) -> p k t", k=4)),
                                reads=[K(ps_t)], writes=[K(stg, (oc, tt))],
                                tag=f"l0a_qev{ti}_{pi}{oc}")
                        else:
                            P.op("vector", lambda v, o_s=o_s, psb=psb: v.tensor_copy(
                                o_s[:], psb[:]),
                                reads=[K(psb)], writes=[K(o_s)], tag=f"l0a_vc{ti}_{oc}")
                            P.dma("gpsimd", lambda g, o_s=o_s, tok0=tok0, oc=oc: g.dma_start(
                                out=v_d[tok0:tok0 + 128, oc * 512:(oc + 1) * 512], in_=o_s[:]),
                                reads=[K(o_s)], writes=[K(v_d, (tok0, oc))],
                                tag=f"l0a_vst{ti}_{oc}")
            for (stg, dst) in ((qst[tc], qf_d), (kst[tc], kf_d)):
                stg_keys = [K(stg, (oc, tt)) for oc in range(2) for tt in range(4)]
                for k in range(8):
                    P.dma("gpsimd", lambda g, stg=stg, dst=dst, k=k, tc=tc: g.dma_start(
                        out=dst[k * 128:(k + 1) * 128, tc * 512:(tc + 1) * 512],
                        in_=stg[:, k * 512:(k + 1) * 512]),
                        reads=stg_keys, writes=[K(dst, (tc, k))],
                        tag=f"l0a_fmst{tc}_{k}")
    P.barrier()

    _mha_core_proj(ctx, qf_d, kf_d, v_d, s_d, weights["l0_Wot"], scale=64.0 ** -0.5,
                   nheads=16, tag="l0c")


def _mha_core(ctx, st0, q_d, k_d, v_d, ofm, scale, nheads, tag,
              seq_len=T, nbatch=B_LOC, v_mode="rows"):
    P, nc = ctx.P, ctx.nc
    HD = 64
    n_qt = seq_len // 128
    with ExitStack() as st:
        ident_b = sbuf_const(ctx, st, "ident_bf")
        mask_add = sbuf_const(ctx, st, "mask_tril_add")
        v_b = ctx.sbuf(st, f"{tag}_vb", [128, (seq_len // 128) * nheads * HD], BF16)
        qf = RotBuf(ctx, st, f"{tag}_qf", 2, [64, seq_len], BF16)
        kf = RotBuf(ctx, st, f"{tag}_kf", 2, [64, seq_len], BF16)
        pb = RotBuf(ctx, st, f"{tag}_p", 2, [128, seq_len], BF16)
        stt_ = RotBuf(ctx, st, f"{tag}_c", 2, [128, 4], F32)
        ob = RotBuf(ctx, st, f"{tag}_ob", 2, [128, 64], BF16)
        ps_s = ctx.psum(st, f"{tag}_pss", [128, 2048], F32)
        ps_t = ctx.psum(st, f"{tag}_pst", [128, 2048], BF16)
        ps_o = ctx.psum(st, f"{tag}_pso", [128, 512], F32)
        ps_o2 = ctx.psum(st, f"{tag}_pso2", [128, 512], BF16)
        for b in range(nbatch):
            if v_mode == "rows":
                for kt in range(seq_len // 128):
                    P.dma("sync", lambda s, b=b, kt=kt: s.dma_start(
                        out=v_b[:, kt * nheads * HD:(kt + 1) * nheads * HD],
                        in_=v_d[b * seq_len + kt * 128: b * seq_len + (kt + 1) * 128, :]),
                        writes=[K(v_b, kt)], tag=f"{tag}_ldv{b}_{kt}")
            else:
                for kt in range(seq_len // 128):
                    for c in range(nheads):
                        P.dma("sync", lambda s, b=b, kt=kt, c=c: s.dma_start(
                            out=v_b[:, kt * nheads * HD + c * HD:
                                    kt * nheads * HD + (c + 1) * HD],
                            in_=v_d[(b * nheads + c) * seq_len + kt * 128:
                                    (b * nheads + c) * seq_len + (kt + 1) * 128, :]),
                            writes=[K(v_b, kt)], tag=f"{tag}_ldv{b}_{kt}_{c}")
            for h in range(nheads):
                q_s, k_s = qf[h], kf[h]
                P.dma("sync", lambda s, q_s=q_s, b=b, h=h: s.dma_start(
                    out=q_s[:], in_=q_d[h * HD:(h + 1) * HD,
                                        b * seq_len:(b + 1) * seq_len]),
                    writes=[K(q_s)], tag=f"{tag}_ldq{b}_{h}")
                P.dma("sync", lambda s, k_s=k_s, b=b, h=h: s.dma_start(
                    out=k_s[:], in_=k_d[h * HD:(h + 1) * HD,
                                        b * seq_len:(b + 1) * seq_len]),
                    writes=[K(k_s)], tag=f"{tag}_ldk{b}_{h}")
                for qt in range(n_qt):
                    nk = qt + 1
                    kact = nk * 128
                    p_s, cst = pb[qt], stt_[qt]
                    sc_writes = [K(ps_s, i) for i in range(ceil_div(nk, 4))]

                    def mm_scores(t, q_s=q_s, k_s=k_s, qt=qt, nk=nk):
                        off = 0
                        while off < nk * 128:
                            w = min(512, nk * 128 - off)
                            inst = t.matmul(
                                ps_s[:, off:off + w],
                                q_s[:, qt * 128:(qt + 1) * 128],
                                k_s[:, off:off + w], start=True, stop=True)
                            off += w
                        return inst
                    P.op("tensor", mm_scores, reads=[K(q_s), K(k_s)],
                         writes=sc_writes, tag=f"{tag}_sc{b}_{h}_{qt}")
                    P.op("vector", lambda v, qt=qt, kact=kact: v.tensor_add(
                        ps_s[:, kact - 128:kact], ps_s[:, kact - 128:kact], mask_add[:]),
                        reads=sc_writes + [K(mask_add)],
                        writes=[K(ps_s, (nk - 1) // 4)],
                        tag=f"{tag}_msk{b}_{h}_{qt}")
                    P.op("vector", lambda v, cst=cst, kact=kact: v.reduce_max(
                        cst[:, 0:1], ps_s[:, 0:kact], axis=AX.X),
                        reads=sc_writes, writes=[K(cst, "m")],
                        tag=f"{tag}_max{b}_{h}_{qt}")
                    P.op("vector", lambda v, cst=cst: v.tensor_scalar_mul(
                        cst[:, 1:2], cst[:, 0:1], -scale),
                        reads=[K(cst, "m")], writes=[K(cst, "nm")])
                    P.op("scalar", lambda s, p_s=p_s, cst=cst, kact=kact: s.activation(
                        p_s[:, 0:kact], ps_s[:, 0:kact], AF.Exp, bias=cst[:, 1:2],
                        scale=scale, accum_out=cst[:, 2:3]),
                        reads=[K(cst, "nm")] + sc_writes,
                        writes=[K(p_s), K(cst, "l")], tag=f"{tag}_exp{b}_{h}_{qt}")
                    P.op("vector", lambda v, cst=cst: v.reciprocal(cst[:, 3:4], cst[:, 2:3]),
                         reads=[K(cst, "l")], writes=[K(cst, "rl")])

                    def mm_tr(t, p_s=p_s, nk=nk):
                        for k2 in range(nk):
                            inst = t.transpose(ps_t[:, k2 * 128:(k2 + 1) * 128],
                                               p_s[:, k2 * 128:(k2 + 1) * 128], ident_b[:])
                        return inst
                    P.op("tensor", mm_tr, reads=[K(p_s), K(ident_b)],
                         writes=[K(ps_t)], tag=f"{tag}_tr{b}_{h}_{qt}")
                    P.op("vector", lambda v, p_s=p_s, kact=kact: v.tensor_copy(
                        p_s[:, 0:kact], ps_t[:, 0:kact]),
                        reads=[K(ps_t)], writes=[K(p_s)], tag=f"{tag}_ev{b}_{h}_{qt}")

                    def mm_pv(t, p_s=p_s, h=h, nk=nk):
                        for k2 in range(nk):
                            inst = t.matmul(
                                ps_o[:, 0:64], p_s[:, k2 * 128:(k2 + 1) * 128],
                                v_b[:, k2 * nheads * HD + h * HD:
                                    k2 * nheads * HD + (h + 1) * HD],
                                start=(k2 == 0), stop=(k2 == nk - 1))
                        return inst
                    P.op("tensor", mm_pv,
                         reads=[K(p_s)] + [K(v_b, k2) for k2 in range(nk)],
                         writes=[K(ps_o)], tag=f"{tag}_pv{b}_{h}_{qt}")
                    o_s = ob[qt]
                    P.op("vector", lambda v, o_s=o_s, cst=cst: v.tensor_scalar(
                        o_s[:], ps_o[:, 0:64], cst[:, 3:4], None, ALU.mult),
                        reads=[K(ps_o), K(cst, "rl")], writes=[K(o_s)],
                        tag=f"{tag}_on{b}_{h}_{qt}")
                    P.op("tensor", lambda t, o_s=o_s: t.transpose(
                        ps_o2[0:64, 0:128], o_s[:, 0:64], ident_b[:]),
                        reads=[K(o_s), K(ident_b)], writes=[K(ps_o2)],
                        tag=f"{tag}_otr{b}_{h}_{qt}")
                    fchunk, poff = h // 2, (h % 2) * 64
                    tok0 = b * seq_len + qt * 128
                    P.op("vector", lambda v, fchunk=fchunk, poff=poff, tok0=tok0: v.tensor_copy(
                        ofm[poff:poff + 64, fchunk * NT + tok0: fchunk * NT + tok0 + 128],
                        ps_o2[0:64, 0:128]),
                        reads=[K(ps_o2)], writes=[K(ofm, (fchunk, poff, tok0))],
                        tag=f"{tag}_oev{b}_{h}_{qt}")
    P.barrier()


def _mha_core_proj(ctx, q_d, k_d, v_d, s_d, wo_ap, scale, nheads, tag,
                   seq_len=T, nbatch=B_LOC):
    P, nc = ctx.P, ctx.nc
    nfc = nheads * 64 // 128
    with ExitStack() as st0:
        wo_s = load_weight_bf16(ctx, st0, f"{tag}_wo", wo_ap, nfc, D)
        ofm = ctx.sbuf(st0, f"{tag}_ofm", [128, nfc * NT], BF16)
        _mha_core(ctx, st0, q_d, k_d, v_d, ofm, scale, nheads, tag,
                  seq_len=seq_len, nbatch=nbatch, v_mode="rows")
        with ExitStack() as st:
            yb = RotBuf(ctx, st, f"{tag}_y", 2, [128, D], F32)
            ps_y = [ctx.psum(st, f"{tag}_psy{i}", [128, 512], F32) for i in range(2)]
            for tt in range(NT // 128):
                y_s = yb[tt]
                tok0 = tt * 128
                for oc in range(2):
                    psb = ps_y[oc]
                    P.op("tensor", lambda t, tt=tt, oc=oc, psb=psb, tok0=tok0: [
                        t.matmul(psb[:],
                                 ofm[:, kc * NT + tok0: kc * NT + tok0 + 128],
                                 wo_s[:, kc * 1024 + oc * 512: kc * 1024 + oc * 512 + 512],
                                 start=(kc == 0), stop=(kc == nfc - 1))
                        for kc in range(nfc)][-1],
                        reads=[], writes=[K(psb)], tag=f"{tag}_ymm{tt}_{oc}")
                    P.op("vector", lambda v, y_s=y_s, oc=oc, psb=psb: v.tensor_copy(
                        y_s[:, oc * 512:(oc + 1) * 512], psb[:]),
                        reads=[K(psb)], writes=[K(y_s, oc)], tag=f"{tag}_yev{tt}_{oc}")
                P.dma("sync", lambda s, y_s=y_s, tok0=tok0: s.dma_start(
                    out=s_d[tok0:tok0 + 128, :], in_=y_s[:]),
                    reads=[K(y_s, 0), K(y_s, 1)], writes=[K(s_d, tt)], tag=f"{tag}_yst{tt}")
    P.barrier()


def l2_attn(ctx, xn_fm_d, s_d, weights):
    P, nc = ctx.P, ctx.nc
    q2_d = ctx.dram_scratch("l2_qfm", [D, NT], BF16)
    k2_d = ctx.dram_scratch("l2_kfm", [D, NT], BF16)
    v2_d = ctx.dram_scratch("l2_vseq", [B_LOC * 16 * T, 64], BF16)

    with ExitStack() as st:
        wq_s = load_weight_bf16(ctx, st, "l2wq", weights["l2_Wq"], 8, D)
        wk_s = load_weight_bf16(ctx, st, "l2wk", weights["l2_Wk"], 8, D)
        wv_s = load_weight_bf16(ctx, st, "l2wv", weights["l2_Wv"], 8, D)
        ident_b = sbuf_const(ctx, st, "ident_bf")
        xnf = RotBuf(ctx, st, "l2a_xn", 2, [128, 8 * 512], BF16)
        outb = RotBuf(ctx, st, "l2a_o", 2, [128, D], BF16)
        seqfm = RotBuf(ctx, st, "l2a_sf", 3, [64, T], BF16)
        vtm = RotBuf(ctx, st, "l2a_vt", 2, [128, 16 * 64], BF16)
        ps = [ctx.psum(st, f"l2a_ps{i}", [128, 512], F32) for i in range(2)]
        ps_t = ctx.psum(st, "l2a_pst", [128, 1024], BF16)
        ps_t2 = ctx.psum(st, "l2a_pst2", [128, 1024], BF16)
        for ti in range(NTILES):
            b, c = ti // 16, ti % 16
            tc, tt = ti // 4, ti % 4
            xn_s = xnf[tc]
            if tt == 0:
                for k in range(8):
                    P.dma("sync", lambda s, xn_s=xn_s, k=k, tc=tc: s.dma_start(
                        out=xn_s[:, k * 512:(k + 1) * 512],
                        in_=xn_fm_d[k * 128:(k + 1) * 128, tc * 512:(tc + 1) * 512]),
                        writes=[K(xn_s, k)], tag=f"l2a_ldxn{tc}_{k}")
            xn_keys = [K(xn_s, k) for k in range(8)]
            for pi, w_s in enumerate((wq_s, wk_s, wv_s)):
                o_s = outb[ti * 3 + pi]
                for oc in range(2):
                    psb = ps[oc]
                    P.op("tensor", lambda t, xn_s=xn_s, w_s=w_s, tt=tt, oc=oc, psb=psb: [
                        t.matmul(psb[:],
                                 xn_s[:, k * 512 + tt * 128: k * 512 + tt * 128 + 128],
                                 w_s[:, k * 1024 + oc * 512: k * 1024 + oc * 512 + 512],
                                 start=(k == 0), stop=(k == 7))
                        for k in range(8)][-1],
                        reads=xn_keys + [K(w_s, k) for k in range(8)],
                        writes=[K(psb)], tag=f"l2a_mm{ti}_{pi}{oc}")
                    P.op("vector", lambda v, o_s=o_s, oc=oc, psb=psb: v.tensor_copy(
                        o_s[:, oc * 512:(oc + 1) * 512], psb[:]),
                        reads=[K(psb)], writes=[K(o_s, oc)], tag=f"l2a_oev{ti}_{pi}{oc}")
                sf = seqfm[ti * 3 + pi]

                def tr16(t, o_s=o_s):
                    for i in range(16):
                        inst = t.transpose(
                            ps_t[(i % 2) * 64:(i % 2) * 64 + 64,
                                 (i // 2) * 128:(i // 2) * 128 + 128],
                            o_s[:, i * 64:(i + 1) * 64], ident_b[:])
                    return inst
                P.op("tensor", tr16, reads=[K(o_s, 0), K(o_s, 1), K(ident_b)],
                     writes=[K(ps_t)], tag=f"l2a_tr{ti}_{pi}")

                def ev16(v, sf=sf):
                    for i in range(16):
                        inst = v.tensor_copy(
                            sf[:, i::16],
                            ps_t[(i % 2) * 64:(i % 2) * 64 + 64,
                                 (i // 2) * 128:(i // 2) * 128 + 128])
                    return inst
                P.op("vector", ev16, reads=[K(ps_t)], writes=[K(sf)],
                     tag=f"l2a_ev{ti}_{pi}")
                if pi < 2:
                    dst = q2_d if pi == 0 else k2_d
                    P.dma("gpsimd", lambda g, sf=sf, dst=dst, b=b, c=c: g.dma_start(
                        out=dst[c * 64:(c + 1) * 64, b * T:(b + 1) * T], in_=sf[:]),
                        reads=[K(sf)], writes=[K(dst, (b, c))], tag=f"l2a_st{ti}_{pi}")
                else:
                    vt = vtm[ti]

                    def tr16b(t, sf=sf):
                        for jt in range(16):
                            inst = t.transpose(
                                ps_t2[:, (jt % 16) * 64:(jt % 16) * 64 + 64],
                                sf[:, jt * 128:(jt + 1) * 128], ident_b[0:64, 0:64])
                        return inst
                    P.op("tensor", tr16b, reads=[K(sf), K(ident_b)],
                         writes=[K(ps_t2)], tag=f"l2a_vtr{ti}")
                    P.op("vector", lambda v, vt=vt: v.tensor_copy(vt[:], ps_t2[:]),
                         reads=[K(ps_t2)], writes=[K(vt)], tag=f"l2a_vev{ti}")
                    for jt in range(16):
                        P.dma("gpsimd", lambda g, vt=vt, b=b, c=c, jt=jt: g.dma_start(
                            out=v2_d[(b * 16 + c) * T + jt * 128:
                                     (b * 16 + c) * T + (jt + 1) * 128, :],
                            in_=vt[:, jt * 64:(jt + 1) * 64]),
                            reads=[K(vt)], writes=[K(v2_d, (ti, jt))],
                            tag=f"l2a_vst{ti}_{jt}")
    P.barrier()

    with ExitStack() as st0:
        idstack = sbuf_const(ctx, st0, "idstack_bf")
        pw_s = load_weight_bf16(ctx, st0, "l2pw", weights["l2_proj_wt"], 8, D)
        ones_f = sbuf_const(ctx, st0, "ones_f32")
        pbrow = ctx.sbuf(st0, "l2pb", [1, D], F32)
        P.dma("sync", lambda s: s.dma_start(
            out=pbrow[:], in_=weights["l2_proj_b"].rearrange("(o d) -> o d", o=1)),
            writes=[K(pbrow)])
        ofm = ctx.sbuf(st0, "l2_ofm", [128, 8 * NT], BF16)
        _mha_core(ctx, st0, q2_d, k2_d, v2_d, ofm, scale=64.0 ** -0.5, nheads=16,
                  tag="l2c", v_mode="seq")
        with ExitStack() as st:
            pbb = bcast_row(ctx, st, "l2pbb", pbrow, D, ones_f)
            stk = RotBuf(ctx, st, "l2_stk", 2, [128, 8 * 128], BF16)
            yb = RotBuf(ctx, st, "l2_y", 2, [128, D], F32)
            ps_st = ctx.psum(st, "l2_psst", [128, 512], F32)
            ps_y = [ctx.psum(st, f"l2_psy{i}", [128, 512], F32) for i in range(2)]
            for ti in range(NTILES):
                b, c = ti // 16, ti % 16
                fchunk, poff = c // 2, (c % 2) * 64
                base = fchunk * NT + b * T
                stk_s = stk[ti]
                for p in range(8):
                    va = ofm[poff:poff + 64, base + 2 * p: base + T: 16]
                    vb_ = ofm[poff:poff + 64, base + 2 * p + 1: base + T: 16]
                    ida = idstack[poff:poff + 64, 0:128]
                    idb = idstack[poff:poff + 64, 128:256]
                    P.op("tensor", lambda t, va=va, vb_=vb_, ida=ida, idb=idb: [
                        t.matmul(ps_st[:, 0:128], ida, va,
                                 start=True, stop=False),
                        t.matmul(ps_st[:, 0:128], idb, vb_,
                                 start=False, stop=True)][-1],
                        reads=[K(idstack)], writes=[K(ps_st)],
                        tag=f"l2p_stk{ti}_{p}")
                    P.op("vector", lambda v, stk_s=stk_s, p=p: v.tensor_copy(
                        stk_s[:, p * 128:(p + 1) * 128], ps_st[:, 0:128]),
                        reads=[K(ps_st)], writes=[K(stk_s, p)],
                        tag=f"l2p_sev{ti}_{p}")
                stk_keys = [K(stk_s, p) for p in range(8)]
                y_s = yb[ti]
                for oc in range(2):
                    psb = ps_y[oc]
                    P.op("tensor", lambda t, stk_s=stk_s, oc=oc, psb=psb: [
                        t.matmul(psb[:], stk_s[:, p * 128:(p + 1) * 128],
                                 pw_s[:, p * 1024 + oc * 512: p * 1024 + oc * 512 + 512],
                                 start=(p == 0), stop=(p == 7))
                        for p in range(8)][-1],
                        reads=stk_keys + [K(pw_s, p) for p in range(8)],
                        writes=[K(psb)], tag=f"l2p_ymm{ti}_{oc}")
                    P.op("vector", lambda v, y_s=y_s, oc=oc, psb=psb: v.tensor_add(
                        y_s[:, oc * 512:(oc + 1) * 512], psb[:],
                        pbb[:, oc * 512:(oc + 1) * 512]),
                        reads=[K(psb), K(pbb)], writes=[K(y_s, oc)],
                        tag=f"l2p_yev{ti}_{oc}")
                P.dma("sync", lambda s, y_s=y_s, ti=ti: s.dma_start(
                    out=s_d[ti * 128:(ti + 1) * 128, :], in_=y_s[:]),
                    reads=[K(y_s, 0), K(y_s, 1)], writes=[K(s_d, ti)],
                    tag=f"l2p_yst{ti}")
    P.barrier()


def l2_ffn(ctx, xn_fm_d, xn_fm32_d, g_d, weights):
    P, nc = ctx.P, ctx.nc
    CAP = int(T * 2 / 8 * 1.5) + 1
    keep_d = ctx.dram_scratch("l2_keep", [B_LOC * 2 * T], F32)
    gall_d = ctx.dram_scratch("l2_gall", [NT, 16], F32)

    with ExitStack() as st:
        ident_f = sbuf_const(ctx, st, "ident_f32")
        rw_s = ctx.sbuf(st, "l2rw", [128, 8 * 8], F32)
        for cc in range(8):
            P.dma("sync", lambda g, cc=cc: g.dma_start(
                out=rw_s[:, cc * 8:(cc + 1) * 8],
                in_=weights["l2_Wrt"][cc * 128:(cc + 1) * 128, :]),
                writes=[K(rw_s, cc)])
        xnf = RotBuf(ctx, st, "l2f_xn", 2, [128, 8 * 128], F32)
        lg = RotBuf(ctx, st, "l2f_lg", 2, [128, 72], F32)
        z128 = ctx.sbuf(st, "l2f_z128", [128, 8], F32)
        ohf = [ctx.sbuf(st, f"l2f_ohf{b}", [8, 2 * T], F32) for b in range(B_LOC)]
        zs8 = ctx.sbuf(st, "l2f_z8", [8, 2 * T], F32)
        cs8 = ctx.sbuf(st, "l2f_cs8", [8, 2 * T], F32)
        cs8b = ctx.sbuf(st, "l2f_cs8b", [8, 2 * T], BF16)
        ones8 = ctx.sbuf(st, "l2f_on8", [8, 1], BF16)
        keeprow = ctx.sbuf(st, "l2f_kr", [1, 2 * T], F32)
        ps_l = ctx.psum(st, "l2f_psl", [128, 512], F32)
        ps_tr = ctx.psum(st, "l2f_pstr", [128, 512], F32)
        ps_kr = ctx.psum(st, "l2f_pskr", [128, 512], F32)
        P.op("vector", lambda v: v.memset(zs8[:], 0.0), writes=[K(zs8)])
        P.op("vector", lambda v: v.memset(z128[:], 0.0), writes=[K(z128)])
        P.op("vector", lambda v: v.memset(ones8[:], 1.0), writes=[K(ones8)])
        for b in range(B_LOC):
            oh = ohf[b]
            for tt in range(16):
                ti = b * 16 + tt
                tc = ti // 4
                xn_s = xnf[tc]
                xn_s = xnf[ti]
                for k in range(8):
                    P.dma("sync", lambda s, xn_s=xn_s, k=k, ti=ti: s.dma_start(
                        out=xn_s[:, k * 128:(k + 1) * 128],
                        in_=xn_fm32_d[k * 128:(k + 1) * 128, ti * 128:(ti + 1) * 128]),
                        writes=[K(xn_s, k)], tag=f"l2f_ldxn{ti}_{k}")
                xn_keys = [K(xn_s, k) for k in range(8)]
                l_ = lg[tt]
                P.op("tensor", lambda t, xn_s=xn_s: [
                    t.matmul(ps_l[:, 0:8],
                             xn_s[:, k * 128:(k + 1) * 128],
                             rw_s[:, k * 8:(k + 1) * 8], start=(k == 0), stop=(k == 7))
                    for k in range(8)][-1],
                    reads=xn_keys + [K(rw_s, k) for k in range(8)],
                    writes=[K(ps_l)], tag=f"l2f_lg{ti}")
                steps = [
                    lambda v, l_: v.tensor_copy(l_[:, 0:8], ps_l[:, 0:8]),
                    lambda v, l_: v.max(l_[:, 8:16], l_[:, 0:8]),
                    lambda v, l_: v.tensor_sub(l_[:, 16:17], l_[:, 9:10], l_[:, 8:9]),
                ]
                P.op("vector", lambda v, l_=l_: steps[0](v, l_),
                     reads=[K(ps_l)], writes=[K(l_)], tag=f"l2f_lev{ti}")
                P.op("vector", lambda v, l_=l_: steps[1](v, l_),
                     reads=[K(l_)], writes=[K(l_)], tag=f"l2f_top{ti}")
                P.op("vector", lambda v, l_=l_: steps[2](v, l_),
                     reads=[K(l_)], writes=[K(l_)])
                P.op("scalar", lambda s, l_=l_: s.activation(l_[:, 17:18], l_[:, 16:17], AF.Exp),
                     reads=[K(l_)], writes=[K(l_)])
                P.op("vector", lambda v, l_=l_: v.tensor_scalar_add(l_[:, 18:19], l_[:, 17:18], 1.0),
                     reads=[K(l_)], writes=[K(l_)])
                P.op("vector", lambda v, l_=l_: v.reciprocal(l_[:, 19:20], l_[:, 18:19]),
                     reads=[K(l_)], writes=[K(l_)])
                P.op("vector", lambda v, l_=l_: v.tensor_mul(l_[:, 20:21], l_[:, 17:18], l_[:, 19:20]),
                     reads=[K(l_)], writes=[K(l_)])
                P.op("vector", lambda v, l_=l_: v.tensor_tensor(
                    l_[:, 21:22], l_[:, 8:9], l_[:, 9:10], ALU.is_equal),
                    reads=[K(l_)], writes=[K(l_)])
                P.op("vector", lambda v, l_=l_: v.tensor_scalar_add(l_[:, 22:23], l_[:, 21:22], 1.0),
                     reads=[K(l_)], writes=[K(l_)])
                P.op("vector", lambda v, l_=l_: v.tensor_scalar(
                    l_[:, 24:32], l_[:, 0:8], l_[:, 8:9], None, ALU.is_equal),
                    reads=[K(l_)], writes=[K(l_)])
                P.op("vector", lambda v, l_=l_: v.tensor_tensor_scan(
                    l_[:, 32:40], l_[:, 24:32], z128[:], 0.0, ALU.add, ALU.add),
                    reads=[K(l_), K(z128)], writes=[K(l_)])
                P.op("vector", lambda v, l_=l_: v.tensor_scalar(
                    l_[:, 40:48], l_[:, 32:40], 1.0, None, ALU.is_equal),
                    reads=[K(l_)], writes=[K(l_)])
                P.op("vector", lambda v, l_=l_: v.tensor_mul(
                    l_[:, 40:48], l_[:, 40:48], l_[:, 24:32]),
                    reads=[K(l_)], writes=[K(l_)])
                P.op("vector", lambda v, l_=l_: v.tensor_scalar(
                    l_[:, 24:32], l_[:, 0:8], l_[:, 9:10], None, ALU.is_equal),
                    reads=[K(l_)], writes=[K(l_)])
                P.op("vector", lambda v, l_=l_: v.tensor_tensor_scan(
                    l_[:, 32:40], l_[:, 24:32], z128[:], 0.0, ALU.add, ALU.add),
                    reads=[K(l_), K(z128)], writes=[K(l_)])
                P.op("vector", lambda v, l_=l_: v.tensor_scalar(
                    l_[:, 32:40], l_[:, 32:40], l_[:, 22:23], None, ALU.is_equal),
                    reads=[K(l_)], writes=[K(l_)])
                P.op("vector", lambda v, l_=l_: v.tensor_mul(
                    l_[:, 48:56], l_[:, 32:40], l_[:, 24:32]),
                    reads=[K(l_)], writes=[K(l_)])
                P.op("tensor", lambda t, l_=l_: [
                    t.transpose(ps_tr[0:8, 0:128], l_[:, 40:48], ident_f[:]),
                    t.transpose(ps_tr[0:8, 128:256], l_[:, 48:56], ident_f[:])][-1],
                    reads=[K(l_), K(ident_f)], writes=[K(ps_tr)], tag=f"l2f_ohtr{ti}")
                P.op("vector", lambda v, oh=oh, tt=tt: [
                    v.tensor_copy(oh[:, 2 * 128 * tt: 2 * 128 * (tt + 1): 2],
                                  ps_tr[0:8, 0:128]),
                    v.tensor_copy(oh[:, 2 * 128 * tt + 1: 2 * 128 * (tt + 1): 2],
                                  ps_tr[0:8, 128:256])][-1],
                    reads=[K(ps_tr)], writes=[K(oh, tt)], tag=f"l2f_ohev{ti}")
                P.op("vector", lambda v, l_=l_: [
                    v.tensor_scalar(l_[:, 56:64], l_[:, 40:48], l_[:, 19:20], None, ALU.mult),
                    v.tensor_scalar(l_[:, 64:72], l_[:, 48:56], l_[:, 20:21], None, ALU.mult)][-1],
                    reads=[K(l_)], writes=[K(l_)], tag=f"l2f_gw{ti}")
                P.dma("sync", lambda s, l_=l_, ti=ti: s.dma_start(
                    out=gall_d[ti * 128:(ti + 1) * 128, :], in_=l_[:, 56:72]),
                    reads=[K(l_)], writes=[K(gall_d, ti)], tag=f"l2f_gst{ti}")
            oh_keys = [K(oh, tt) for tt in range(16)]
            P.op("vector", lambda v, oh=oh: v.tensor_tensor_scan(
                cs8[:], oh[:], zs8[:], 0.0, ALU.add, ALU.add),
                reads=oh_keys + [K(zs8)], writes=[K(cs8)], tag=f"l2f_cs{b}")
            P.op("vector", lambda v, oh=oh: v.tensor_sub(cs8[:], cs8[:], oh[:]),
                 reads=oh_keys, writes=[K(cs8)], tag=f"l2f_pex{b}")
            P.op("vector", lambda v: v.tensor_scalar(
                cs8[:], cs8[:], float(CAP), None, ALU.is_lt),
                reads=[], writes=[K(cs8)], tag=f"l2f_lt{b}")
            P.op("vector", lambda v, oh=oh: v.tensor_mul(cs8[:], cs8[:], oh[:]),
                 reads=oh_keys, writes=[K(cs8)], tag=f"l2f_sel{b}")
            P.op("vector", lambda v: v.tensor_copy(cs8b[:], cs8[:]),
                 reads=[K(cs8)], writes=[K(cs8b)], tag=f"l2f_csb{b}")
            for ch in range(8):
                P.op("tensor", lambda t, ch=ch: t.matmul(
                    ps_kr[0:1, 0:512], ones8[:, 0:1],
                    cs8b[:, ch * 512:(ch + 1) * 512], start=True, stop=True),
                    reads=[K(cs8b), K(ones8)], writes=[K(ps_kr)], tag=f"l2f_krm{b}_{ch}")
                P.op("vector", lambda v, ch=ch: v.tensor_copy(
                    keeprow[0:1, ch * 512:(ch + 1) * 512], ps_kr[0:1, 0:512]),
                    reads=[K(ps_kr)], writes=[K(keeprow, ch)], tag=f"l2f_krev{b}_{ch}")
            P.dma("sync", lambda s, b=b: s.dma_start(
                out=keep_d[b * 2 * T:(b + 1) * 2 * T].rearrange("(o f) -> o f", o=1),
                in_=keeprow[:]),
                reads=[K(keeprow, ch) for ch in range(8)], writes=[K(keep_d, b)],
                tag=f"l2f_krst{b}")
    P.barrier()

    with ExitStack() as st:
        xnf = RotBuf(ctx, st, "l2e_xn", 2, [128, 8 * 512], BF16)
        web = RotBuf(ctx, st, "l2e_we", 2, [128, 8 * 1024], BF16)
        accb = RotBuf(ctx, st, "l2e_acc", 2, [128, 4 * 1024], F32)
        gfb = RotBuf(ctx, st, "l2e_gf", 2, [128, 4 * 8], F32)
        kpb = RotBuf(ctx, st, "l2e_kp", 2, [128, 4 * 2], F32)
        gab = RotBuf(ctx, st, "l2e_ga", 2, [128, 4 * 16], F32)
        yb = RotBuf(ctx, st, "l2e_y", 2, [128, D], F32)
        ps_y = [ctx.psum(st, f"l2e_ps{i}", [128, 512], F32) for i in range(2)]
        for ch in range(8):
            xn_s = xnf[ch]
            for k in range(8):
                P.dma("sync", lambda s, xn_s=xn_s, k=k, ch=ch: s.dma_start(
                    out=xn_s[:, k * 512:(k + 1) * 512],
                    in_=xn_fm_d[k * 128:(k + 1) * 128, ch * 512:(ch + 1) * 512]),
                    writes=[K(xn_s, k)], tag=f"l2e_ldxn{ch}_{k}")
            xn_keys = [K(xn_s, k) for k in range(8)]
            gf_s, kp_s, ga_s = gfb[ch], kpb[ch], gab[ch]
            for tl in range(4):
                ti = ch * 4 + tl
                b = ti // 16
                tw = ti % 16
                P.dma("sync", lambda s, kp_s=kp_s, tl=tl, b=b, tw=tw: s.dma_start(
                    out=kp_s[:, tl * 2:(tl + 1) * 2],
                    in_=keep_d[b * 2 * T + tw * 256: b * 2 * T + (tw + 1) * 256]
                        .rearrange("(t k) -> t k", k=2)),
                    writes=[K(kp_s, tl)], tag=f"l2e_ldkp{ti}")
                P.dma("sync", lambda s, ga_s=ga_s, tl=tl, ti=ti: s.dma_start(
                    out=ga_s[:, tl * 16:(tl + 1) * 16],
                    in_=gall_d[ti * 128:(ti + 1) * 128, :]),
                    writes=[K(ga_s, tl)], tag=f"l2e_ldga{ti}")
            for tl in range(4):
                ti = ch * 4 + tl
                P.op("vector", lambda v, gf_s=gf_s, ga_s=ga_s, kp_s=kp_s, tl=tl:
                     v.tensor_scalar(gf_s[:, tl * 8:(tl + 1) * 8],
                                     ga_s[:, tl * 16:tl * 16 + 8],
                                     kp_s[:, tl * 2:tl * 2 + 1], None, ALU.mult),
                     reads=[K(ga_s, tl), K(kp_s, tl)], writes=[K(gf_s, tl)],
                     tag=f"l2e_gfa{ti}")
                P.op("vector", lambda v, gf_s=gf_s, ga_s=ga_s, kp_s=kp_s, tl=tl:
                     v.scalar_tensor_tensor(gf_s[:, tl * 8:(tl + 1) * 8],
                                            ga_s[:, tl * 16 + 8:tl * 16 + 16],
                                            kp_s[:, tl * 2 + 1:tl * 2 + 2],
                                            gf_s[:, tl * 8:(tl + 1) * 8],
                                            ALU.mult, ALU.add),
                     reads=[K(ga_s, tl), K(kp_s, tl)], writes=[K(gf_s, tl)],
                     tag=f"l2e_gfb{ti}")
            acc = accb[ch]
            for e in range(8):
                we_s = web[e]
                for k in range(8):
                    P.dma("gpsimd", lambda g, we_s=we_s, e=e, k=k: g.dma_start(
                        out=we_s[:, k * 1024:(k + 1) * 1024],
                        in_=weights["l2_We"][e, k * 128:(k + 1) * 128, :]),
                        writes=[K(we_s, k)], tag=f"l2e_ldwe{ch}_{e}_{k}")
                we_keys = [K(we_s, k) for k in range(8)]
                for tl in range(4):
                    ti = ch * 4 + tl
                    for oc in range(2):
                        psb = ps_y[oc]
                        P.op("tensor", lambda t, xn_s=xn_s, we_s=we_s, tl=tl, oc=oc, psb=psb: [
                            t.matmul(psb[:],
                                     xn_s[:, k * 512 + tl * 128: k * 512 + tl * 128 + 128],
                                     we_s[:, k * 1024 + oc * 512: k * 1024 + oc * 512 + 512],
                                     start=(k == 0), stop=(k == 7))
                            for k in range(8)][-1],
                            reads=xn_keys + we_keys, writes=[K(psb)],
                            tag=f"l2e_mm{ch}_{e}_{tl}{oc}")
                        if e == 0:
                            P.op("vector", lambda v, acc=acc, gf_s=gf_s, tl=tl, oc=oc, e=e, psb=psb:
                                 v.tensor_scalar(
                                     acc[:, tl * 1024 + oc * 512: tl * 1024 + (oc + 1) * 512],
                                     psb[:], gf_s[:, tl * 8 + e: tl * 8 + e + 1], None,
                                     ALU.mult),
                                 reads=[K(psb), K(gf_s, tl)], writes=[K(acc, (tl, oc))],
                                 tag=f"l2e_acc{ch}_{e}_{tl}{oc}")
                        else:
                            P.op("vector", lambda v, acc=acc, gf_s=gf_s, tl=tl, oc=oc, e=e, psb=psb:
                                 v.scalar_tensor_tensor(
                                     acc[:, tl * 1024 + oc * 512: tl * 1024 + (oc + 1) * 512],
                                     psb[:], gf_s[:, tl * 8 + e: tl * 8 + e + 1],
                                     acc[:, tl * 1024 + oc * 512: tl * 1024 + (oc + 1) * 512],
                                     ALU.mult, ALU.add),
                                 reads=[K(psb), K(gf_s, tl)], writes=[K(acc, (tl, oc))],
                                 tag=f"l2e_acc{ch}_{e}_{tl}{oc}")
            for tl in range(4):
                ti = ch * 4 + tl
                y_s = yb[tl]
                P.op("scalar", lambda s, y_s=y_s, acc=acc, tl=tl: s.activation(
                    y_s[:], acc[:, tl * 1024:(tl + 1) * 1024], AF.Silu),
                    reads=[K(acc, (tl, 0)), K(acc, (tl, 1))], writes=[K(y_s)],
                    tag=f"l2e_silu{ti}")
                P.dma("sync", lambda s, y_s=y_s, ti=ti: s.dma_start(
                    out=g_d[ti * 128:(ti + 1) * 128, :], in_=y_s[:]),
                    reads=[K(y_s)], writes=[K(g_d, ti)], tag=f"l2e_gst{ti}")
    P.barrier()


def l1_attn(ctx, xn_fm_d, s_d, weights):
    P, nc = ctx.P, ctx.nc
    q_d = ctx.dram_scratch("l1_qfm", [D, NT], BF16)
    k_d = ctx.dram_scratch("l1_kfm", [D, NT], BF16)
    v_d = ctx.dram_scratch("l1_vtm", [NT, D], BF16)

    with ExitStack() as st:
        wq_s = load_weight_bf16(ctx, st, "l1wq", weights["l1_Wq"], 8, D)
        wk_s = load_weight_bf16(ctx, st, "l1wk", weights["l1_Wk"], 8, D)
        wv_s = load_weight_bf16(ctx, st, "l1wv", weights["l1_Wv"], 8, D)
        xnf = RotBuf(ctx, st, "l1a_xn", 2, [128, 8 * 512], BF16)
        qko = RotBuf(ctx, st, "l1a_qko", 3, [128, 512], BF16)
        ps = [ctx.psum(st, f"l1a_ps{i}", [128, 512], F32) for i in range(4)]
        for tc in range(8):
            xn_s = xnf[tc]
            for k in range(8):
                P.dma("sync", lambda s, xn_s=xn_s, k=k, tc=tc: s.dma_start(
                    out=xn_s[:, k * 512:(k + 1) * 512],
                    in_=xn_fm_d[k * 128:(k + 1) * 128, tc * 512:(tc + 1) * 512]),
                    writes=[K(xn_s, k)], tag=f"l1a_ldxn{tc}_{k}")
            xn_keys = [K(xn_s, k) for k in range(8)]
            for pi, (w_s, dst) in enumerate(((wq_s, q_d), (wk_s, k_d))):
                for m in range(8):
                    psb = ps[m % 2 + pi * 2]
                    o_s = qko[(tc * 16 + pi * 8 + m)]
                    P.op("tensor", lambda t, xn_s=xn_s, w_s=w_s, m=m, psb=psb: [
                        t.matmul(psb[:, 0:512],
                                 w_s[:, k * 1024 + m * 128: k * 1024 + m * 128 + 128],
                                 xn_s[:, k * 512:(k + 1) * 512],
                                 start=(k == 0), stop=(k == 7))
                        for k in range(8)][-1],
                        reads=xn_keys + [K(w_s, k) for k in range(8)],
                        writes=[K(psb)], tag=f"l1a_mm{tc}_{pi}_{m}")
                    P.op("scalar", lambda s, o_s=o_s, psb=psb: s.activation(
                        o_s[:], psb[:, 0:512], AF.Silu),
                        reads=[K(psb)], writes=[K(o_s)], tag=f"l1a_silu{tc}_{pi}_{m}")
                    P.dma("sync", lambda g, o_s=o_s, dst=dst, m=m, tc=tc: g.dma_start(
                        out=dst[m * 128:(m + 1) * 128, tc * 512:(tc + 1) * 512],
                        in_=o_s[:]),
                        reads=[K(o_s)], writes=[K(dst, (tc, m))],
                        tag=f"l1a_st{tc}_{pi}_{m}")
            for tt in range(4):
                for oc in range(2):
                    psb = ps[oc]
                    o_s = qko[(tc * 8 + tt * 2 + oc)]
                    P.op("tensor", lambda t, xn_s=xn_s, tt=tt, oc=oc, psb=psb: [
                        t.matmul(psb[:, 0:512],
                                 xn_s[:, k * 512 + tt * 128: k * 512 + tt * 128 + 128],
                                 wv_s[:, k * 1024 + oc * 512: k * 1024 + oc * 512 + 512],
                                 start=(k == 0), stop=(k == 7))
                        for k in range(8)][-1],
                        reads=xn_keys + [K(wv_s, k) for k in range(8)],
                        writes=[K(psb)], tag=f"l1a_vmm{tc}_{tt}{oc}")
                    P.op("vector", lambda v, o_s=o_s, psb=psb: v.tensor_copy(
                        o_s[:], psb[:, 0:512]),
                        reads=[K(psb)], writes=[K(o_s)], tag=f"l1a_vev{tc}_{tt}{oc}")
                    tok0 = tc * 512 + tt * 128
                    P.dma("sync", lambda g, o_s=o_s, tok0=tok0, oc=oc: g.dma_start(
                        out=v_d[tok0:tok0 + 128, oc * 512:(oc + 1) * 512], in_=o_s[:]),
                        reads=[K(o_s)], writes=[K(v_d, (tok0, oc))],
                        tag=f"l1a_vst{tc}_{tt}{oc}")
    P.barrier()

    with ExitStack() as st0:
        wo_s = load_weight_bf16(ctx, st0, "l1wo", weights["l1_Wo"], 8, D)
        ofm = ctx.sbuf(st0, "l1_ofm", [128, 8 * NT], BF16)
        with ExitStack() as st:
            ident_b = sbuf_const(ctx, st, "ident_bf")
            triu01 = sbuf_const(ctx, st, "triu01_bf")
            iota_r = sbuf_const(ctx, st, "iota_row")
            theta_s = ctx.sbuf(st, "l1_theta", [1, D], F32)
            P.dma("sync", lambda s: s.dma_start(
                out=theta_s[:], in_=weights["l1_theta"].rearrange("h o d -> o (h d)")),
                writes=[K(theta_s)])
            vb = ctx.sbuf(st, "l1c_v", [128, 16 * D], BF16)
            qh = RotBuf(ctx, st, "l1c_q", 2, [64, T], BF16)
            kh = RotBuf(ctx, st, "l1c_k", 2, [64, T], BF16)
            cos_s = ctx.sbuf(st, "l1c_cos", [64, T], BF16)
            sin_s = ctx.sbuf(st, "l1c_sin", [64, T], BF16)
            tq = ctx.sbuf(st, "l1c_tq", [64, T], F32)
            tw = ctx.sbuf(st, "l1c_tw", [64, T], F32)
            trnd = ctx.sbuf(st, "l1c_trnd", [64, T], F32)
            ti32 = ctx.sbuf(st, "l1c_i32", [64, T], I32)
            qp = RotBuf(ctx, st, "l1c_qp", 2, [128, T], BF16)
            kp = RotBuf(ctx, st, "l1c_kp", 2, [128, T], BF16)
            e_s = RotBuf(ctx, st, "l1c_e", 2, [128, 128], BF16)
            ktm = RotBuf(ctx, st, "l1c_ktm", 2, [128, 128], BF16)
            sbf = ctx.sbuf(st, "l1c_sbf", [128, 64], BF16)
            s32 = ctx.sbuf(st, "l1c_s32", [128, 64], F32)
            ob = RotBuf(ctx, st, "l1c_o", 2, [128, 64], BF16)
            ps_tab = ctx.psum(st, "l1c_pstab", [64, 2048], F32)
            ps_E = ctx.psum(st, "l1c_psE", [128, 512], F32)
            ps_O = ctx.psum(st, "l1c_psO", [128, 512], F32)
            ps_S = ctx.psum(st, "l1c_psS", [128, 512], F32)
            ps_tr = ctx.psum(st, "l1c_pstr", [128, 1024], BF16)
            inv2pi = float(1.0 / (2 * np.pi))
            for b in range(B_LOC):
                for kt in range(16):
                    P.dma("sync", lambda s, b=b, kt=kt: s.dma_start(
                        out=vb[:, kt * D:(kt + 1) * D],
                        in_=v_d[b * T + kt * 128: b * T + (kt + 1) * 128, :]),
                        writes=[K(vb, kt)], tag=f"l1c_ldv{b}_{kt}")
                for h in range(16):
                    P.op("tensor", lambda t, h=h: [
                        t.matmul(ps_tab[:, c * 512:(c + 1) * 512],
                                 theta_s[0:1, h * 64:(h + 1) * 64],
                                 iota_r[0:1, c * 512:(c + 1) * 512],
                                 start=True, stop=True)
                        for c in range(4)][-1],
                        reads=[K(theta_s), K(iota_r)], writes=[K(ps_tab)],
                        tag=f"l1c_ang{b}_{h}")
                    P.op("vector", lambda v: v.tensor_scalar_mul(tq[:], ps_tab[0:64, :], inv2pi),
                         reads=[K(ps_tab)], writes=[K(tq)], tag=f"l1c_q2p{b}_{h}")
                    for dst, bias in ((sin_s, 0.0), (cos_s, 0.25)):
                        P.op("vector", lambda v, bias=bias: v.tensor_scalar_add(tw[:], tq[:], bias),
                             reads=[K(tq)], writes=[K(tw)], tag=f"l1c_tb{b}_{h}")
                        P.op("vector", lambda v: v.tensor_copy(ti32[:], tw[:]),
                             reads=[K(tw)], writes=[K(ti32)], tag=f"l1c_ti{b}_{h}")
                        P.op("vector", lambda v: v.tensor_copy(trnd[:], ti32[:]),
                             reads=[K(ti32)], writes=[K(trnd)], tag=f"l1c_tr{b}_{h}")
                        P.op("vector", lambda v: v.tensor_sub(tw[:], tw[:], trnd[:]),
                             reads=[K(trnd)], writes=[K(tw)], tag=f"l1c_fr{b}_{h}")
                        P.op("vector", lambda v: v.tensor_scalar(
                            trnd[:], tw[:], 0.5, None, ALU.is_gt),
                            reads=[K(tw)], writes=[K(trnd)], tag=f"l1c_cg{b}_{h}")
                        P.op("vector", lambda v: v.tensor_sub(tw[:], tw[:], trnd[:]),
                             reads=[K(trnd)], writes=[K(tw)], tag=f"l1c_cs{b}_{h}")
                        P.op("vector", lambda v: v.tensor_scalar(
                            trnd[:], tw[:], -0.5, None, ALU.is_lt),
                            reads=[K(tw)], writes=[K(trnd)], tag=f"l1c_cl{b}_{h}")
                        P.op("vector", lambda v: v.tensor_add(tw[:], tw[:], trnd[:]),
                             reads=[K(trnd)], writes=[K(tw)], tag=f"l1c_ca{b}_{h}")
                        P.op("scalar", lambda s, dst=dst: s.activation(
                            dst[:], tw[:], AF.Sin, scale=float(2 * np.pi)),
                            reads=[K(tw)], writes=[K(dst)], tag=f"l1c_sn{b}_{h}")
                    if b == 0 and h == 0 and "l1_sdbg" in ctx.dbg:
                        sdbg_d = ctx.dram_scratch("l1_sdbg", [128, T], F32)
                        ctx.dram["l1_sdbg"] = sdbg_d
                        P.dma("gpsimd", lambda s: s.dma_start(
                            out=sdbg_d[0:64, :], in_=sin_s[:]),
                            reads=[K(sin_s)], writes=[K(sdbg_d, 0)], tag="l1c_sdbg0")
                        P.dma("gpsimd", lambda s: s.dma_start(
                            out=sdbg_d[64:128, :], in_=cos_s[:]),
                            reads=[K(cos_s)], writes=[K(sdbg_d, 1)], tag="l1c_sdbg1")
                    q_s, k_s = qh[b], kh[b]
                    P.dma("sync", lambda s, q_s=q_s, b=b, h=h: s.dma_start(
                        out=q_s[:], in_=q_d[h * 64:(h + 1) * 64, b * T:(b + 1) * T]),
                        writes=[K(q_s)], tag=f"l1c_ldq{b}_{h}")
                    P.dma("sync", lambda s, k_s=k_s, b=b, h=h: s.dma_start(
                        out=k_s[:], in_=k_d[h * 64:(h + 1) * 64, b * T:(b + 1) * T]),
                        writes=[K(k_s)], tag=f"l1c_ldk{b}_{h}")
                    qp_s, kp_s = qp[b], kp[b]
                    for src, dstp in ((q_s, qp_s), (k_s, kp_s)):
                        P.op("vector", lambda v, src=src, dstp=dstp: v.tensor_mul(
                            dstp[0:64, :], src[:], cos_s[:]),
                            reads=[K(src), K(cos_s)], writes=[K(dstp)],
                            tag=f"l1c_pc{b}_{h}")
                        P.op("vector", lambda v, src=src, dstp=dstp: v.tensor_mul(
                            dstp[64:128, :], src[:], sin_s[:]),
                            reads=[K(src), K(sin_s)], writes=[K(dstp)],
                            tag=f"l1c_ps{b}_{h}")
                    P.op("vector", lambda v: v.memset(sbf[:], 0.0),
                         reads=[], writes=[K(sbf)], tag=f"l1c_s0{b}_{h}")
                    P.op("vector", lambda v: v.memset(s32[:], 0.0),
                         reads=[], writes=[K(s32)], tag=f"l1c_s32z{b}_{h}")
                    for ci in range(16):
                        c0 = ci * 128
                        em = e_s[ci]
                        P.op("tensor", lambda t, kp_s=kp_s, qp_s=qp_s, c0=c0: t.matmul(
                            ps_E[:, 0:128], kp_s[:, c0:c0 + 128], qp_s[:, c0:c0 + 128],
                            start=True, stop=True),
                            reads=[K(kp_s), K(qp_s)], writes=[K(ps_E)],
                            tag=f"l1c_E{b}_{h}_{ci}")
                        P.op("vector", lambda v, em=em: v.tensor_mul(
                            em[:], ps_E[:, 0:128], triu01[:]),
                            reads=[K(ps_E), K(triu01)], writes=[K(em)],
                            tag=f"l1c_Em{b}_{h}_{ci}")
                        vslice = ci * D + h * 64
                        P.op("tensor", lambda t, em=em, qp_s=qp_s, c0=c0, vslice=vslice: [
                            t.matmul(ps_O[:, 0:64], em[:],
                                     vb[:, vslice:vslice + 64],
                                     start=True, stop=False),
                            t.matmul(ps_O[:, 0:64], qp_s[:, c0:c0 + 128], sbf[:],
                                     start=False, stop=True)][-1],
                            reads=[K(em), K(qp_s), K(sbf), K(vb, ci)],
                            writes=[K(ps_O)], tag=f"l1c_O{b}_{h}_{ci}")
                        o_s = ob[ci]
                        P.op("vector", lambda v, o_s=o_s: v.tensor_copy(
                            o_s[:], ps_O[:, 0:64]),
                            reads=[K(ps_O)], writes=[K(o_s)], tag=f"l1c_Oe{b}_{h}_{ci}")
                        P.op("tensor", lambda t, o_s=o_s: t.transpose(
                            ps_tr[0:64, 0:128], o_s[:, 0:64], ident_b[:]),
                            reads=[K(o_s), K(ident_b)], writes=[K(ps_tr)],
                            tag=f"l1c_Otr{b}_{h}_{ci}")
                        fchunk, poff = h // 2, (h % 2) * 64
                        tok0 = b * T + c0
                        P.op("vector", lambda v, fchunk=fchunk, poff=poff, tok0=tok0:
                             v.tensor_copy(
                                 ofm[poff:poff + 64,
                                     fchunk * NT + tok0: fchunk * NT + tok0 + 128],
                                 ps_tr[0:64, 0:128]),
                             reads=[K(ps_tr)], writes=[K(ofm, (fchunk, poff, tok0))],
                             tag=f"l1c_Oev{b}_{h}_{ci}")
                        km = ktm[ci]
                        P.op("tensor", lambda t, kp_s=kp_s, c0=c0: t.transpose(
                            ps_tr[:, 128:256], kp_s[:, c0:c0 + 128], ident_b[:]),
                            reads=[K(kp_s), K(ident_b)], writes=[K(ps_tr)],
                            tag=f"l1c_ktr{b}_{h}_{ci}")
                        P.op("vector", lambda v, km=km: v.tensor_copy(
                            km[:], ps_tr[:, 128:256]),
                            reads=[K(ps_tr)], writes=[K(km)],
                            tag=f"l1c_kev{b}_{h}_{ci}")
                        P.op("tensor", lambda t, km=km, vslice=vslice: t.matmul(
                            ps_S[:, 0:64], km[:],
                            vb[:, vslice:vslice + 64],
                            start=True, stop=True),
                            reads=[K(km), K(vb, ci)], writes=[K(ps_S)],
                            tag=f"l1c_Smm{b}_{h}_{ci}")
                        P.op("vector", lambda v: v.tensor_add(s32[:], s32[:], ps_S[:, 0:64]),
                             reads=[K(ps_S)], writes=[K(s32)],
                             tag=f"l1c_Sad{b}_{h}_{ci}")
                        P.op("vector", lambda v: v.tensor_copy(sbf[:], s32[:]),
                             reads=[K(s32)], writes=[K(sbf)],
                             tag=f"l1c_Sev{b}_{h}_{ci}")
        P.barrier()
        with ExitStack() as st:
            ones_f = sbuf_const(ctx, st, "ones_f32")
            ident_b2 = sbuf_const(ctx, st, "ident_bf")
            g1_s = load_weight_bf16(ctx, st, "l1g1", weights["l1_G1"], 8, 64)
            g2_s = ctx.sbuf(st, "l1g2", [64, D], BF16)
            P.dma("gpsimd", lambda g: g.dma_start(out=g2_s[:], in_=weights["l1_G2"][:]),
                  writes=[K(g2_s)])
            gmrow = ctx.sbuf(st, "l1gm", [1, D], F32)
            btrow = ctx.sbuf(st, "l1bt", [1, D], F32)
            P.dma("sync", lambda s: s.dma_start(
                out=gmrow[:], in_=weights["l1_gamma"].rearrange("(o d) -> o d", o=1)),
                writes=[K(gmrow)])
            P.dma("sync", lambda s: s.dma_start(
                out=btrow[:], in_=weights["l1_beta"].rearrange("(o d) -> o d", o=1)),
                writes=[K(btrow)])
            ps_g2 = ctx.psum(st, "l1psg2", [128, 1024], F32)
            gmb = bcast_row(ctx, st, "l1gmb", gmrow, D, ones_f, psum=ps_g2)
            btb = bcast_row(ctx, st, "l1btb", btrow, D, ones_f, psum=ps_g2)
            xnl = RotBuf(ctx, st, "l1x", 2, [128, 8 * 128], BF16)
            yb = RotBuf(ctx, st, "l1y", 2, [128, D], F32)
            gb_ = RotBuf(ctx, st, "l1gt", 2, [128, D], F32)
            g1b = RotBuf(ctx, st, "l1g1b", 2, [128, 64], BF16)
            g1f = RotBuf(ctx, st, "l1g1f", 2, [64, 128], BF16)
            stc = RotBuf(ctx, st, "l1st", 2, [128, 8], F32)
            eps = ctx.sbuf(st, "l1eps", [128, 1], F32)
            sqs = RotBuf(ctx, st, "l1sq", 2, [128, D], F32)
            ps_y = [ctx.psum(st, f"l1psy{i}", [128, 512], F32) for i in range(2)]
            ps_g = ctx.psum(st, "l1psg", [128, 512], F32)
            ps_tr2 = ctx.psum(st, "l1pstr2", [128, 512], BF16)
            P.op("vector", lambda v: v.memset(eps[:], 1e-5), writes=[K(eps)])
            for ti in range(NTILES):
                tok0 = ti * 128
                y_s, gt_s, c = yb[ti], gb_[ti], stc[ti]
                xn_s = xnl[ti]
                for k in range(8):
                    P.dma("sync", lambda s, xn_s=xn_s, k=k, ti=ti: s.dma_start(
                        out=xn_s[:, k * 128:(k + 1) * 128],
                        in_=xn_fm_d[k * 128:(k + 1) * 128, ti * 128:(ti + 1) * 128]),
                        writes=[K(xn_s, k)], tag=f"l1y_ldxn{ti}_{k}")
                for oc in range(2):
                    psb = ps_y[oc]
                    P.op("tensor", lambda t, oc=oc, psb=psb, tok0=tok0: [
                        t.matmul(psb[:],
                                 ofm[:, kc * NT + tok0: kc * NT + tok0 + 128],
                                 wo_s[:, kc * 1024 + oc * 512: kc * 1024 + oc * 512 + 512],
                                 start=(kc == 0), stop=(kc == 7))
                        for kc in range(8)][-1],
                        reads=[], writes=[K(psb)], tag=f"l1y_mm{ti}_{oc}")
                P.op("tensor", lambda t, xn_s=xn_s: [
                    t.matmul(ps_g[:, 0:64],
                             xn_s[:, k * 128:(k + 1) * 128],
                             g1_s[:, k * 64:(k + 1) * 64], start=(k == 0), stop=(k == 7))
                    for k in range(8)][-1],
                    reads=[K(xn_s, k) for k in range(8)] + [K(g1_s, k) for k in range(8)],
                    writes=[K(ps_g)], tag=f"l1y_g1{ti}")
                g1_b = g1b[ti]
                P.op("vector", lambda v, g1_b=g1_b: v.tensor_copy(g1_b[:], ps_g[:, 0:64]),
                     reads=[K(ps_g)], writes=[K(g1_b)], tag=f"l1y_g1e{ti}")
                g1_f = g1f[ti]
                P.op("tensor", lambda t, g1_b=g1_b: t.transpose(
                    ps_tr2[0:64, 0:128], g1_b[:, 0:64], ident_b2[:]),
                    reads=[K(g1_b), K(ident_b2)], writes=[K(ps_tr2)],
                    tag=f"l1y_g1t{ti}")
                P.op("vector", lambda v, g1_f=g1_f: v.tensor_copy(g1_f[:], ps_tr2[0:64, 0:128]),
                     reads=[K(ps_tr2)], writes=[K(g1_f)], tag=f"l1y_g1fe{ti}")
                P.op("tensor", lambda t, g1_f=g1_f: [
                    t.matmul(ps_g2[:, oc * 512:(oc + 1) * 512], g1_f[:],
                             g2_s[:, oc * 512:(oc + 1) * 512], start=True, stop=True)
                    for oc in range(2)][-1],
                    reads=[K(g1_f), K(g2_s)], writes=[K(ps_g2)],
                    tag=f"l1y_g2{ti}")
                P.op("scalar", lambda s, gt_s=gt_s: s.activation(
                    gt_s[:], ps_g2[:, 0:1024], AF.Sigmoid),
                    reads=[K(ps_g2)], writes=[K(gt_s)], tag=f"l1y_sig{ti}")
                for oc in range(2):
                    psb = ps_y[oc]
                    P.op("vector", lambda v, y_s=y_s, gt_s=gt_s, oc=oc, psb=psb:
                         v.tensor_mul(y_s[:, oc * 512:(oc + 1) * 512], psb[:],
                                      gt_s[:, oc * 512:(oc + 1) * 512]),
                         reads=[K(psb), K(gt_s)], writes=[K(y_s)],
                         tag=f"l1y_mul{ti}_{oc}")
                sq_s = sqs[ti]
                P.op("vector", lambda v, y_s=y_s, c=c: v.reduce_sum(
                    c[:, 0:1], y_s[:], axis=AX.X),
                    reads=[K(y_s)], writes=[K(c)], tag=f"l1y_sum{ti}")
                P.op("vector", lambda v, c=c: v.tensor_scalar_mul(c[:, 2:3], c[:, 0:1], 1.0 / D),
                     reads=[K(c)], writes=[K(c)])
                P.op("vector", lambda v, y_s=y_s, c=c: v.tensor_scalar(
                    y_s[:], y_s[:], c[:, 2:3], None, ALU.subtract),
                    reads=[K(c)], writes=[K(y_s)], tag=f"l1y_sub{ti}")
                P.op("scalar", lambda s, y_s=y_s, sq_s=sq_s, c=c: s.activation(
                    sq_s[:], y_s[:], AF.Square, accum_out=c[:, 1:2]),
                    reads=[K(y_s)], writes=[K(sq_s), K(c, "b")], tag=f"l1y_sq{ti}")
                P.op("vector", lambda v, c=c: v.tensor_scalar_mul(c[:, 3:4], c[:, 1:2], 1.0 / D),
                     reads=[K(c, "b")], writes=[K(c, "b")])
                P.op("scalar", lambda s, c=c: s.activation(c[:, 6:7], c[:, 3:4], AF.Sqrt,
                                                           bias=eps[:, 0:1]),
                     reads=[K(c, "b"), K(eps)], writes=[K(c, "b")])
                P.op("vector", lambda v, c=c: v.reciprocal(c[:, 7:8], c[:, 6:7]),
                     reads=[K(c, "b")], writes=[K(c, "b2")])
                P.op("vector", lambda v, y_s=y_s, c=c: v.tensor_scalar(
                    y_s[:], y_s[:], c[:, 7:8], None, ALU.mult),
                    reads=[K(c, "b2")], writes=[K(y_s)], tag=f"l1y_nrm{ti}")
                P.op("vector", lambda v, y_s=y_s: v.tensor_mul(y_s[:], y_s[:], gmb[:]),
                     reads=[K(gmb)], writes=[K(y_s)], tag=f"l1y_gm{ti}")
                P.op("vector", lambda v, y_s=y_s: v.tensor_add(y_s[:], y_s[:], btb[:]),
                     reads=[K(btb)], writes=[K(y_s)], tag=f"l1y_bt{ti}")
                P.dma("sync", lambda s, y_s=y_s, tok0=tok0: s.dma_start(
                    out=s_d[tok0:tok0 + 128, :], in_=y_s[:]),
                    reads=[K(y_s)], writes=[K(s_d, ti)], tag=f"l1y_st{ti}")
    P.barrier()


L1F_HID = 2816
L1F_NH = 8
L1F_W = 512


def l1_ffn(ctx, xn_f32_d, xn_fm_d0, g_d, weights):
    P, nc = ctx.P, ctx.nc
    y_d = ctx.dram_scratch("l1f_y", [NT, D], F32)
    yfm_d = ctx.dram_scratch("l1f_yfm", [D, NT], BF16)
    qf_d = ctx.dram_scratch("l1f_qfm", [512, NT], BF16)
    kf_d = ctx.dram_scratch("l1f_kfm", [512, NT], BF16)
    vt_d = ctx.dram_scratch("l1f_vtm", [NT, 512], BF16)

    with ExitStack() as st:
        for q in range(4):
            P.dma("sync", lambda s, q=q: s.dma_start(
                out=y_d[q * 1024:(q + 1) * 1024, :],
                in_=xn_f32_d[q * 1024:(q + 1) * 1024, :]),
                writes=[K(y_d, q)], tag=f"l1f_iy{q}")
        for q in range(8):
            P.dma("sync", lambda s, q=q: s.dma_start(
                out=yfm_d[q * 128:(q + 1) * 128, :],
                in_=xn_fm_d0[q * 128:(q + 1) * 128, :]),
                writes=[K(yfm_d, q)], tag=f"l1f_iyf{q}")
    P.barrier()

    for l in range(2):
        _l1f_sublayer(ctx, l, y_d, yfm_d, qf_d, kf_d, vt_d, weights)

    with ExitStack() as st:
        ab = RotBuf(ctx, st, "l1f_fa", 2, [128, D], F32)
        gb_ = RotBuf(ctx, st, "l1f_fg", 2, [128, D], F32)
        for ti in range(NTILES):
            a_s, g_s = ab[ti], gb_[ti]
            P.dma("sync", lambda s, a_s=a_s, ti=ti: s.dma_start(
                out=a_s[:], in_=y_d[ti * 128:(ti + 1) * 128, :]),
                writes=[K(a_s)], tag=f"l1f_gld{ti}")
            P.op("scalar", lambda s, a_s=a_s, g_s=g_s: s.activation(g_s[:], a_s[:], AF.Silu),
                 reads=[K(a_s)], writes=[K(g_s)], tag=f"l1f_gsilu{ti}")
            P.dma("sync", lambda s, g_s=g_s, ti=ti: s.dma_start(
                out=g_d[ti * 128:(ti + 1) * 128, :], in_=g_s[:]),
                reads=[K(g_s)], writes=[K(g_d, ti)], tag=f"l1f_gst{ti}")
    P.barrier()


def _l1f_shift_loads(P, dst, yfm_d, k, ti, tag):
    if k < 4 or ti % 16 != 0:
        off = ti * 128 if k < 4 else ti * 128 - 1
        P.dma("sync", lambda s: s.dma_start(
            out=dst[:, k * 128:(k + 1) * 128],
            in_=yfm_d[k * 128:(k + 1) * 128, off:off + 128]),
            writes=[K(dst, k)], tag=tag)
    else:
        P.dma("sync", lambda s: s.dma_start(
            out=dst[:, k * 128 + 1:(k + 1) * 128],
            in_=yfm_d[k * 128:(k + 1) * 128, ti * 128:ti * 128 + 127]),
            writes=[K(dst, k)], tag=tag)
        P.op("vector", lambda v: v.memset(dst[:, k * 128:k * 128 + 1], 0.0),
             reads=[], writes=[K(dst, k)], tag=tag + "z")


def _l1f_sublayer(ctx, l, y_d, yfm_d, qf_d, kf_d, vt_d, weights):
    P, nc = ctx.P, ctx.nc
    HD, NH = 64, L1F_NH
    with ExitStack() as st:
        wqkv_s = load_weight_bf16(ctx, st, f"l1f_wqkv{l}",
                                  weights["l1f_wqkv"][l], 8, 1536)
        ident_b = sbuf_const(ctx, st, "ident_bf")
        qsrow = ctx.sbuf(st, f"l1f_qs{l}", [1, 2 * HD], F32)
        P.dma("sync", lambda s: s.dma_start(
            out=qsrow[:, 0:HD], in_=weights["l1f_q_scale"][l].rearrange("(o d) -> o d", o=1)),
            writes=[K(qsrow, 0)])
        P.dma("sync", lambda s: s.dma_start(
            out=qsrow[:, HD:2 * HD], in_=weights["l1f_k_scale"][l].rearrange("(o d) -> o d", o=1)),
            writes=[K(qsrow, 1)])
        ones_f = sbuf_const(ctx, st, "ones_f32")
        ps_bc = ctx.psum(st, f"l1f_psbc{l}", [128, 512], F32)
        qsb = ctx.sbuf(st, f"l1f_qsb{l}", [128, 512], F32)
        ksb = ctx.sbuf(st, f"l1f_ksb{l}", [128, 512], F32)
        for (dst8, c0) in ((qsb, 0), (ksb, HD)):
            P.op("tensor", lambda t, c0=c0: t.matmul(
                ps_bc[:, 0:HD], ones_f[0:1, 0:128], qsrow[0:1, c0:c0 + HD],
                start=True, stop=True),
                reads=[K(qsrow, 0), K(qsrow, 1), K(ones_f)], writes=[K(ps_bc)],
                tag=f"l1f_qsbc{l}_{c0}")
            def rep8(v, dst8=dst8):
                for hh in range(8):
                    inst = v.tensor_copy(dst8[:, hh * HD:(hh + 1) * HD], ps_bc[:, 0:HD])
                return inst
            P.op("vector", rep8, reads=[K(ps_bc)], writes=[K(dst8)],
                 tag=f"l1f_qsrep{l}_{c0}")
        shb = RotBuf(ctx, st, f"l1f_sh{l}", 2, [128, 8 * 128], BF16)
        qkvb = RotBuf(ctx, st, f"l1f_qkv{l}", 2, [128, 1536], F32)
        sqb = RotBuf(ctx, st, f"l1f_sq{l}", 2, [128, 512], F32)
        nrm = RotBuf(ctx, st, f"l1f_nrm{l}", 2, [128, 16], F32)
        rtb = RotBuf(ctx, st, f"l1f_rt{l}", 2, [128, 512], F32)
        tabb = RotBuf(ctx, st, f"l1f_tab{l}", 2, [128, 4 * 64], F32)
        outb = RotBuf(ctx, st, f"l1f_o{l}", 3, [128, 512], BF16)
        stg = RotBuf(ctx, st, f"l1f_stg{l}", 2, [128, 2 * 4 * 512], BF16)
        ps_qkv = [ctx.psum(st, f"l1f_psq{l}_{i}", [128, 512], F32) for i in range(3)]
        ps_t = ctx.psum(st, f"l1f_pst{l}", [128, 1024], BF16)
        for ti in range(NTILES):
            sh_s = shb[ti]
            for k in range(8):
                _l1f_shift_loads(P, sh_s, yfm_d, k, ti, f"l1f_shl{l}_{ti}_{k}")
            sh_keys = [K(sh_s, k) for k in range(8)]
            qkv_s = qkvb[ti]
            for pi in range(3):
                psb = ps_qkv[pi]
                P.op("tensor", lambda t, sh_s=sh_s, pi=pi, psb=psb: [
                    t.matmul(psb[:],
                             sh_s[:, k * 128:(k + 1) * 128],
                             wqkv_s[:, k * 1536 + pi * 512: k * 1536 + (pi + 1) * 512],
                             start=(k == 0), stop=(k == 7))
                    for k in range(8)][-1],
                    reads=sh_keys + [K(wqkv_s, k) for k in range(8)],
                    writes=[K(psb)], tag=f"l1f_qkvmm{l}_{ti}_{pi}")
                P.op("vector", lambda v, qkv_s=qkv_s, pi=pi, psb=psb: v.tensor_copy(
                    qkv_s[:, pi * 512:(pi + 1) * 512], psb[:]),
                    reads=[K(psb)], writes=[K(qkv_s, pi)], tag=f"l1f_qkvev{l}_{ti}_{pi}")
            trow = (ti % 16) * 128
            tab = tabb[ti]
            for wi, nm in enumerate(("xpos_qc", "xpos_qs", "xpos_kc", "xpos_ks")):
                P.dma("sync", lambda s, tab=tab, wi=wi, nm=nm, trow=trow: s.dma_start(
                    out=tab[:, wi * 64:(wi + 1) * 64],
                    in_=ctx.c[nm][trow:trow + 128, :]),
                    writes=[K(tab, wi)], tag=f"l1f_tab{l}_{ti}_{wi}")
            for pi, (scb, tci, tsi) in enumerate(((qsb, 0, 1), (ksb, 2, 3))):
                sq_s, n_s, rt_s = sqb[ti], nrm[ti], rtb[ti]
                o_s = outb[ti * 3 + pi]
                qv = qkv_s[:, pi * 512:(pi + 1) * 512]
                P.op("scalar", lambda s, sq_s=sq_s, qv=qv: s.activation(
                    sq_s[:], qv, AF.Square),
                    reads=[K(qkv_s, pi)], writes=[K(sq_s)], tag=f"l1f_n2{l}_{ti}_{pi}")
                P.op("vector", lambda v, sq_s=sq_s, n_s=n_s, pi=pi: v.reduce_sum(
                    n_s[:, pi * 8:(pi + 1) * 8],
                    sq_s[:].rearrange("p (h d) -> p h d", d=HD), axis=AX.X),
                    reads=[K(sq_s)], writes=[K(n_s, pi)], tag=f"l1f_nr{l}_{ti}_{pi}")
                P.op("scalar", lambda s, n_s=n_s, pi=pi: s.activation(
                    n_s[:, pi * 8:(pi + 1) * 8], n_s[:, pi * 8:(pi + 1) * 8], AF.Sqrt),
                    reads=[K(n_s, pi)], writes=[K(n_s, pi)], tag=f"l1f_nsq{l}_{ti}_{pi}")
                P.op("vector", lambda v, n_s=n_s, pi=pi: v.tensor_scalar_max(
                    n_s[:, pi * 8:(pi + 1) * 8], n_s[:, pi * 8:(pi + 1) * 8], 1e-12),
                    reads=[K(n_s, pi)], writes=[K(n_s, pi)])
                P.op("vector", lambda v, n_s=n_s, pi=pi: v.reciprocal(
                    n_s[:, pi * 8:(pi + 1) * 8], n_s[:, pi * 8:(pi + 1) * 8]),
                    reads=[K(n_s, pi)], writes=[K(n_s, pi)])
                P.op("vector", lambda v, qv=qv, sq_s=sq_s, n_s=n_s, pi=pi: v.tensor_tensor(
                    sq_s[:].rearrange("p (h d) -> p h d", d=HD),
                    qv.rearrange("p (h d) -> p h d", d=HD),
                    n_s[:, pi * 8:(pi + 1) * 8].rearrange("p (h o) -> p h o", o=1)
                        .broadcast_to([128, 8, HD]),
                    ALU.mult),
                    reads=[K(qkv_s, pi), K(n_s, pi)], writes=[K(sq_s)],
                    tag=f"l1f_nmul{l}_{ti}_{pi}")
                P.op("vector", lambda v, sq_s=sq_s, scb=scb: v.tensor_mul(
                    sq_s[:], sq_s[:], scb[:]),
                    reads=[K(scb)], writes=[K(sq_s)], tag=f"l1f_smul{l}_{ti}_{pi}")
                P.op("vector", lambda v, rt_s=rt_s, sq_s=sq_s: v.tensor_scalar_mul(
                    rt_s[:].rearrange("p (h c d) -> p h c d", c=2, d=32)[:, :, 0, :],
                    sq_s[:].rearrange("p (h c d) -> p h c d", c=2, d=32)[:, :, 1, :], -1.0),
                    reads=[K(sq_s)], writes=[K(rt_s)], tag=f"l1f_rh1{l}_{ti}_{pi}")
                P.op("vector", lambda v, rt_s=rt_s, sq_s=sq_s: v.tensor_copy(
                    rt_s[:].rearrange("p (h c d) -> p h c d", c=2, d=32)[:, :, 1, :],
                    sq_s[:].rearrange("p (h c d) -> p h c d", c=2, d=32)[:, :, 0, :]),
                    reads=[K(sq_s)], writes=[K(rt_s)], tag=f"l1f_rh2{l}_{ti}_{pi}")
                P.op("vector", lambda v, sq_s=sq_s, tab=tab, tci=tci: v.tensor_tensor(
                    sq_s[:].rearrange("p (h d) -> p h d", d=HD),
                    sq_s[:].rearrange("p (h d) -> p h d", d=HD),
                    tab[:, tci * 64:tci * 64 + HD].rearrange("p (o d) -> p o d", o=1)
                        .broadcast_to([128, 8, HD]),
                    ALU.mult),
                    reads=[K(tab, tci)], writes=[K(sq_s)], tag=f"l1f_xc{l}_{ti}_{pi}")
                P.op("vector", lambda v, rt_s=rt_s, tab=tab, tsi=tsi: v.tensor_tensor(
                    rt_s[:].rearrange("p (h d) -> p h d", d=HD),
                    rt_s[:].rearrange("p (h d) -> p h d", d=HD),
                    tab[:, tsi * 64:tsi * 64 + HD].rearrange("p (o d) -> p o d", o=1)
                        .broadcast_to([128, 8, HD]),
                    ALU.mult),
                    reads=[K(tab, tsi)], writes=[K(rt_s)], tag=f"l1f_xs{l}_{ti}_{pi}")
                P.op("vector", lambda v, o_s=o_s, sq_s=sq_s, rt_s=rt_s: v.tensor_add(
                    o_s[:], sq_s[:], rt_s[:]),
                    reads=[K(sq_s), K(rt_s)], writes=[K(o_s)], tag=f"l1f_xa{l}_{ti}_{pi}")
                st_s = stg[ti // 4]

                def tr4(t, o_s=o_s, pi=pi):
                    for j in range(4):
                        inst = t.transpose(
                            ps_t[:, (pi * 4 + j) % 8 * 128:((pi * 4 + j) % 8 + 1) * 128],
                            o_s[:, j * 128:(j + 1) * 128], ident_b[:])
                    return inst
                P.op("tensor", tr4, reads=[K(o_s), K(ident_b)],
                     writes=[K(ps_t, pi)], tag=f"l1f_qtr{l}_{ti}_{pi}")
                P.op("vector", lambda v, st_s=st_s, pi=pi, ti=ti: v.tensor_copy(
                    st_s[:].rearrange("p (x j t) -> p x j t", x=2, j=4)[
                        :, pi, :, (ti % 4) * 128:(ti % 4) * 128 + 128],
                    ps_t[:].rearrange("p (j t) -> p j t", j=8)[
                        :, pi * 4:(pi + 1) * 4, :]),
                    reads=[K(ps_t, pi)], writes=[K(st_s, (pi, ti % 4))],
                    tag=f"l1f_qev{l}_{ti}_{pi}")
            o_s = outb[ti * 3 + 2]
            P.op("vector", lambda v, o_s=o_s, qkv_s=qkv_s: v.tensor_copy(
                o_s[:], qkv_s[:, 1024:1536]),
                reads=[K(qkv_s, 2)], writes=[K(o_s)], tag=f"l1f_vc{l}_{ti}")
            P.dma("gpsimd", lambda g, o_s=o_s, ti=ti: g.dma_start(
                out=vt_d[ti * 128:(ti + 1) * 128, :], in_=o_s[:]),
                reads=[K(o_s)], writes=[K(vt_d, ti)], tag=f"l1f_vst{l}_{ti}")
            if ti % 4 == 3:
                tc = ti // 4
                st_s = stg[ti // 4]
                stg_keys = [K(st_s, (pi2, t2)) for pi2 in range(2) for t2 in range(4)]
                for pi2, dst in ((0, qf_d), (1, kf_d)):
                    for j in range(4):
                        P.dma("gpsimd", lambda g, st_s=st_s, dst=dst, pi2=pi2, j=j, tc=tc:
                              g.dma_start(
                                  out=dst[j * 128:(j + 1) * 128, tc * 512:(tc + 1) * 512],
                                  in_=st_s[:, (pi2 * 4 + j) * 512:(pi2 * 4 + j + 1) * 512]),
                              reads=stg_keys, writes=[K(dst, (tc, j))],
                              tag=f"l1f_fst{l}_{tc}_{pi2}{j}")
    P.barrier()

    with ExitStack() as st0:
        wo_s = load_weight_bf16(ctx, st0, f"l1f_wo{l}", weights["l1f_wo"][l], 4, D)
        ofm = ctx.sbuf(st0, f"l1f_ofm{l}", [128, 4 * NT], BF16)
        with ExitStack() as st:
            ident_b2 = sbuf_const(ctx, st, "ident_bf")
            mask_add = sbuf_const(ctx, st, "mask_tril_add")
            masku = sbuf_const(ctx, st, "mask_triu_add")
            v_b = ctx.sbuf(st, f"l1f_vb{l}", [128, 16 * 512], BF16)
            qw = RotBuf(ctx, st, f"l1f_qw{l}", 2, [64, T], BF16)
            kw = RotBuf(ctx, st, f"l1f_kw{l}", 2, [64, T], BF16)
            pb = RotBuf(ctx, st, f"l1f_p{l}", 2, [128, 1024], BF16)
            cst_ = RotBuf(ctx, st, f"l1f_c{l}", 2, [128, 4], F32)
            ob = RotBuf(ctx, st, f"l1f_ob{l}", 2, [128, 64], BF16)
            ps_s = ctx.psum(st, f"l1f_pss{l}", [128, 1024], F32)
            ps_t2x = ctx.psum(st, f"l1f_pstc{l}", [128, 1024], BF16)
            ps_o = ctx.psum(st, f"l1f_pso{l}", [128, 512], F32)
            ps_o2 = ctx.psum(st, f"l1f_pso2{l}", [128, 512], BF16)
            for b in range(B_LOC):
                for kt in range(16):
                    P.dma("sync", lambda s, b=b, kt=kt: s.dma_start(
                        out=v_b[:, kt * 512:(kt + 1) * 512],
                        in_=vt_d[b * T + kt * 128: b * T + (kt + 1) * 128, :]),
                        writes=[K(v_b, kt)], tag=f"l1f_ldv{l}_{b}_{kt}")
                for h in range(NH):
                    q_s, k_s = qw[h], kw[h]
                    P.dma("sync", lambda s, q_s=q_s, b=b, h=h: s.dma_start(
                        out=q_s[:], in_=qf_d[h * HD:(h + 1) * HD, b * T:(b + 1) * T]),
                        writes=[K(q_s)], tag=f"l1f_ldq{l}_{b}_{h}")
                    P.dma("sync", lambda s, k_s=k_s, b=b, h=h: s.dma_start(
                        out=k_s[:], in_=kf_d[h * HD:(h + 1) * HD, b * T:(b + 1) * T]),
                        writes=[K(k_s)], tag=f"l1f_ldk{l}_{b}_{h}")
                    for w in range(4):
                        for qt in range(4):
                            gq = w * 4 + qt
                            if w == 0:
                                akey0, akey1 = 0, (qt + 1) * 128
                            else:
                                akey0 = (w - 1) * 512 + qt * 128
                                akey1 = w * 512 + (qt + 1) * 128
                            kact = akey1 - akey0
                            nkt = kact // 128
                            p_s, cst = pb[gq], cst_[gq]

                            def mm_sc(t, q_s=q_s, k_s=k_s, gq=gq, akey0=akey0, akey1=akey1):
                                off = akey0
                                while off < akey1:
                                    rel = off - akey0
                                    wdt = min(512 - rel % 512, akey1 - off)
                                    inst = t.matmul(
                                        ps_s[:, rel: rel + wdt],
                                        q_s[:, gq * 128:(gq + 1) * 128],
                                        k_s[:, off:off + wdt],
                                        start=True, stop=True)
                                    off += wdt
                                return inst
                            sc_w = [K(ps_s, i) for i in range(ceil_div(nkt, 4))]
                            P.op("tensor", mm_sc, reads=[K(q_s), K(k_s)],
                                 writes=sc_w, tag=f"l1f_sc{l}_{b}_{h}_{gq}")
                            if w > 0:
                                P.op("vector", lambda v: v.tensor_add(
                                    ps_s[:, 0:128], ps_s[:, 0:128], masku[:]),
                                    reads=sc_w + [K(masku)], writes=[K(ps_s, 0)],
                                    tag=f"l1f_mu{l}_{b}_{h}_{gq}")
                            P.op("vector", lambda v, kact=kact: v.tensor_add(
                                ps_s[:, kact - 128:kact], ps_s[:, kact - 128:kact],
                                mask_add[:]),
                                reads=sc_w + [K(mask_add)],
                                writes=[K(ps_s, (nkt - 1) // 4)],
                                tag=f"l1f_ml{l}_{b}_{h}_{gq}")
                            P.op("vector", lambda v, cst=cst, kact=kact: v.reduce_max(
                                cst[:, 0:1], ps_s[:, 0:kact], axis=AX.X),
                                reads=sc_w, writes=[K(cst, "m")],
                                tag=f"l1f_mx{l}_{b}_{h}_{gq}")
                            P.op("vector", lambda v, cst=cst: v.tensor_scalar_mul(
                                cst[:, 1:2], cst[:, 0:1], -8.0),
                                reads=[K(cst, "m")], writes=[K(cst, "nm")])
                            P.op("scalar", lambda s, p_s=p_s, cst=cst, kact=kact:
                                 s.activation(
                                     p_s[:, 0:kact], ps_s[:, 0:kact], AF.Exp,
                                     bias=cst[:, 1:2], scale=8.0,
                                     accum_out=cst[:, 2:3]),
                                 reads=[K(cst, "nm")] + sc_w,
                                 writes=[K(p_s), K(cst, "l")],
                                 tag=f"l1f_ex{l}_{b}_{h}_{gq}")
                            P.op("vector", lambda v, cst=cst: v.reciprocal(
                                cst[:, 3:4], cst[:, 2:3]),
                                reads=[K(cst, "l")], writes=[K(cst, "rl")])

                            def mm_tr(t, p_s=p_s, nkt=nkt):
                                for k2 in range(nkt):
                                    inst = t.transpose(
                                        ps_t2x[:, k2 * 128:(k2 + 1) * 128],
                                        p_s[:, k2 * 128:(k2 + 1) * 128], ident_b2[:])
                                return inst
                            P.op("tensor", mm_tr, reads=[K(p_s), K(ident_b2)],
                                 writes=[K(ps_t2x)], tag=f"l1f_tr{l}_{b}_{h}_{gq}")
                            P.op("vector", lambda v, p_s=p_s, kact=kact: v.tensor_copy(
                                p_s[:, 0:kact], ps_t2x[:, 0:kact]),
                                reads=[K(ps_t2x)], writes=[K(p_s)],
                                tag=f"l1f_ev{l}_{b}_{h}_{gq}")
                            kt0 = akey0 // 128

                            def mm_pv(t, p_s=p_s, h=h, nkt=nkt, kt0=kt0):
                                for k2 in range(nkt):
                                    inst = t.matmul(
                                        ps_o[:, 0:64], p_s[:, k2 * 128:(k2 + 1) * 128],
                                        v_b[:, (kt0 + k2) * 512 + h * HD:
                                            (kt0 + k2) * 512 + (h + 1) * HD],
                                        start=(k2 == 0), stop=(k2 == nkt - 1))
                                return inst
                            P.op("tensor", mm_pv,
                                 reads=[K(p_s)] + [K(v_b, kt0 + k2) for k2 in range(nkt)],
                                 writes=[K(ps_o)], tag=f"l1f_pv{l}_{b}_{h}_{gq}")
                            o_s = ob[gq]
                            P.op("vector", lambda v, o_s=o_s, cst=cst: v.tensor_scalar(
                                o_s[:], ps_o[:, 0:64], cst[:, 3:4], None, ALU.mult),
                                reads=[K(ps_o), K(cst, "rl")], writes=[K(o_s)],
                                tag=f"l1f_on{l}_{b}_{h}_{gq}")
                            P.op("tensor", lambda t, o_s=o_s: t.transpose(
                                ps_o2[0:64, 0:128], o_s[:, 0:64], ident_b2[:]),
                                reads=[K(o_s), K(ident_b2)], writes=[K(ps_o2)],
                                tag=f"l1f_otr{l}_{b}_{h}_{gq}")
                            fchunk, poff = h // 2, (h % 2) * 64
                            tok0 = b * T + gq * 128
                            P.op("vector", lambda v, fchunk=fchunk, poff=poff, tok0=tok0:
                                 v.tensor_copy(
                                     ofm[poff:poff + 64,
                                         fchunk * NT + tok0: fchunk * NT + tok0 + 128],
                                     ps_o2[0:64, 0:128]),
                                 reads=[K(ps_o2)],
                                 writes=[K(ofm, (fchunk, poff, tok0))],
                                 tag=f"l1f_oev{l}_{b}_{h}_{gq}")
        P.barrier()
        with ExitStack() as st:
            ident_b2p = sbuf_const(ctx, st, "ident_bf")
            yb2 = RotBuf(ctx, st, f"l1f_yo{l}", 2, [128, D], F32)
            ylb2 = RotBuf(ctx, st, f"l1f_yl{l}", 2, [128, D], F32)
            ybf2 = RotBuf(ctx, st, f"l1f_ybf{l}", 2, [128, D], BF16)
            ps_y2 = [ctx.psum(st, f"l1f_psy{l}_{i}", [128, 512], F32) for i in range(2)]
            ps_t2p = ctx.psum(st, f"l1f_psyt{l}", [128, 1024], BF16)
            for ti in range(NTILES):
                tok0 = ti * 128
                y_s, yl_s, yb_s = yb2[ti], ylb2[ti], ybf2[ti]
                P.dma("sync", lambda s, yl_s=yl_s, tok0=tok0: s.dma_start(
                    out=yl_s[:], in_=y_d[tok0:tok0 + 128, :]),
                    writes=[K(yl_s)], tag=f"l1f_yld{l}_{ti}")
                for oc in range(2):
                    psb = ps_y2[oc]
                    P.op("tensor", lambda t, oc=oc, psb=psb, tok0=tok0: [
                        t.matmul(psb[:],
                                 ofm[:, kc * NT + tok0: kc * NT + tok0 + 128],
                                 wo_s[:, kc * 1024 + oc * 512: kc * 1024 + oc * 512 + 512],
                                 start=(kc == 0), stop=(kc == 3))
                        for kc in range(4)][-1],
                        reads=[], writes=[K(psb)], tag=f"l1f_ymm{l}_{ti}_{oc}")
                    P.op("vector", lambda v, y_s=y_s, yl_s=yl_s, oc=oc, psb=psb:
                         v.tensor_add(y_s[:, oc * 512:(oc + 1) * 512], psb[:],
                                      yl_s[:, oc * 512:(oc + 1) * 512]),
                         reads=[K(psb), K(yl_s)], writes=[K(y_s, oc)],
                         tag=f"l1f_yad{l}_{ti}_{oc}")
                P.dma("sync", lambda s, y_s=y_s, tok0=tok0: s.dma_start(
                    out=y_d[tok0:tok0 + 128, :], in_=y_s[:]),
                    reads=[K(y_s, 0), K(y_s, 1)], writes=[K(y_d, ti)],
                    tag=f"l1f_yst{l}_{ti}")
                P.op("scalar", lambda s, yb_s=yb_s, y_s=y_s: s.copy(yb_s[:], y_s[:]),
                     reads=[K(y_s, 0), K(y_s, 1)], writes=[K(yb_s)],
                     tag=f"l1f_ycp{l}_{ti}")

                def tr8(t, yb_s=yb_s):
                    for k in range(8):
                        inst = t.transpose(ps_t2p[:, k * 128:(k + 1) * 128],
                                           yb_s[:, k * 128:(k + 1) * 128], ident_b2p[:])
                    return inst
                P.op("tensor", tr8, reads=[K(yb_s), K(ident_b2p)], writes=[K(ps_t2p)],
                     tag=f"l1f_ytr{l}_{ti}")
                yf_s = ybf2[ti + 1]
                P.op("vector", lambda v, yf_s=yf_s: v.tensor_copy(yf_s[:], ps_t2p[:]),
                     reads=[K(ps_t2p)], writes=[K(yf_s)], tag=f"l1f_yfe{l}_{ti}")
                for k in range(8):
                    P.dma("gpsimd", lambda g, yf_s=yf_s, k=k, tok0=tok0: g.dma_start(
                        out=yfm_d[k * 128:(k + 1) * 128, tok0:tok0 + 128],
                        in_=yf_s[:, k * 128:(k + 1) * 128]),
                        reads=[K(yf_s)], writes=[K(yfm_d, (ti, k))],
                        tag=f"l1f_yfst{l}_{ti}_{k}")
    P.barrier()

    hid_d = ctx.dram_scratch(f"l1f_hid{l}", [L1F_HID, NT], BF16)
    with ExitStack() as st:
        w1_s = load_weight_bf16(ctx, st, f"l1f_w1{l}", weights["l1f_w1p"][l], 8, 2 * L1F_HID)
        ident_f3 = sbuf_const(ctx, st, "ident_f32")
        ones_f3 = sbuf_const(ctx, st, "ones_f32")
        onecol_b = ctx.sbuf(st, f"l1f_ocb{l}", [128, 1], BF16)
        onecol_f = ctx.sbuf(st, f"l1f_ocf{l}", [128, 1], F32)
        P.op("vector", lambda v: v.memset(onecol_b[:], 1.0), writes=[K(onecol_b)])
        P.op("vector", lambda v: v.memset(onecol_f[:], 1.0), writes=[K(onecol_f)])
        lgc = colize(ctx, st, f"l1f_lg{l}", weights["l1f_ln_g"][l], D, ident_f3)
        lbc = colize(ctx, st, f"l1f_lb{l}", weights["l1f_ln_b"][l], D, ident_f3)
        b1c = colize(ctx, st, f"l1f_b1{l}", weights["l1f_b1p"][l], 2 * L1F_HID, ident_f3)
        shb = RotBuf(ctx, st, f"l1f_sh2{l}", 2, [128, 8 * 512], BF16)
        xcf = RotBuf(ctx, st, f"l1f_xc{l}", 2, [128, 8 * 512], F32)
        sqf = RotBuf(ctx, st, f"l1f_sqf{l}", 2, [128, 512], F32)
        rowsb = RotBuf(ctx, st, f"l1f_row{l}", 2, [1, 3 * 512], F32)
        hnb = RotBuf(ctx, st, f"l1f_hn{l}", 2, [128, 8 * 512], BF16)
        afb = RotBuf(ctx, st, f"l1f_ga{l}", 3, [128, 512], F32)
        hhb = RotBuf(ctx, st, f"l1f_hh{l}", 3, [128, 512], BF16)
        ps_r3 = ctx.psum(st, f"l1f_psr{l}", [128, 512], F32)
        ps_b3 = ctx.psum(st, f"l1f_psb{l}", [128, 1024], F32)
        ps_h3 = [ctx.psum(st, f"l1f_psh{l}_{i}", [128, 512], F32) for i in range(2)]
        for tc in range(8):
            sh_s, xc_s = shb[tc], xcf[tc]
            for k in range(8):
                if k < 4:
                    P.dma("sync", lambda s, sh_s=sh_s, k=k, tc=tc: s.dma_start(
                        out=sh_s[:, k * 512:(k + 1) * 512],
                        in_=yfm_d[k * 128:(k + 1) * 128, tc * 512:(tc + 1) * 512]),
                        writes=[K(sh_s, k)], tag=f"l1f_2shl{l}_{tc}_{k}")
                elif tc % 4 != 0:
                    P.dma("sync", lambda s, sh_s=sh_s, k=k, tc=tc: s.dma_start(
                        out=sh_s[:, k * 512:(k + 1) * 512],
                        in_=yfm_d[k * 128:(k + 1) * 128, tc * 512 - 1:(tc + 1) * 512 - 1]),
                        writes=[K(sh_s, k)], tag=f"l1f_2shl{l}_{tc}_{k}")
                else:
                    P.dma("sync", lambda s, sh_s=sh_s, k=k, tc=tc: s.dma_start(
                        out=sh_s[:, k * 512 + 1:(k + 1) * 512],
                        in_=yfm_d[k * 128:(k + 1) * 128, tc * 512:(tc + 1) * 512 - 1]),
                        writes=[K(sh_s, k)], tag=f"l1f_2shl{l}_{tc}_{k}")
                    P.op("vector", lambda v, sh_s=sh_s, k=k: v.memset(
                        sh_s[:, k * 512:k * 512 + 1], 0.0),
                        reads=[], writes=[K(sh_s, k)], tag=f"l1f_2shz{l}_{tc}_{k}")
            sh_keys = [K(sh_s, k) for k in range(8)]
            row_s = rowsb[tc]
            P.op("tensor", lambda t, sh_s=sh_s: [
                t.matmul(ps_r3[0:1, 0:512], onecol_b[:, 0:1],
                         sh_s[:, k * 512:(k + 1) * 512],
                         start=(k == 0), stop=(k == 7))
                for k in range(8)][-1],
                reads=sh_keys + [K(onecol_b)], writes=[K(ps_r3)],
                tag=f"l1f_2sum{l}_{tc}")
            P.op("vector", lambda v, row_s=row_s: v.tensor_scalar_mul(
                row_s[0:1, 0:512], ps_r3[0:1, 0:512], 1.0 / D),
                reads=[K(ps_r3)], writes=[K(row_s, "mu")], tag=f"l1f_2mu{l}_{tc}")
            P.op("tensor", lambda t, row_s=row_s: t.matmul(
                ps_b3[:, 0:512], ones_f3[0:1, 0:128], row_s[0:1, 0:512],
                start=True, stop=True),
                reads=[K(row_s, "mu"), K(ones_f3)], writes=[K(ps_b3, 0)],
                tag=f"l1f_2mb{l}_{tc}")
            sq_s = sqf[tc]
            for k in range(8):
                P.op("vector", lambda v, xc_s=xc_s, sh_s=sh_s, k=k: v.tensor_sub(
                    xc_s[:, k * 512:(k + 1) * 512], sh_s[:, k * 512:(k + 1) * 512],
                    ps_b3[:, 0:512]),
                    reads=[K(ps_b3, 0), K(sh_s, k)], writes=[K(xc_s)],
                    tag=f"l1f_2xc{l}_{tc}_{k}")
            for k in range(8):
                P.op("scalar", lambda s, xc_s=xc_s, sq_s=sq_s, k=k: s.activation(
                    sq_s[:], xc_s[:, k * 512:(k + 1) * 512], AF.Square),
                    reads=[K(xc_s)], writes=[K(sq_s)], tag=f"l1f_2q{l}_{tc}_{k}")
                P.op("tensor", lambda t, sq_s=sq_s, k=k: t.matmul(
                    ps_r3[0:1, 0:512], onecol_f[:, 0:1], sq_s[:],
                    start=(k == 0), stop=(k == 7)),
                    reads=[K(sq_s), K(onecol_f)], writes=[K(ps_r3)],
                    tag=f"l1f_2qs{l}_{tc}_{k}")
            P.op("vector", lambda v, row_s=row_s: v.tensor_scalar(
                row_s[0:1, 512:1024], ps_r3[0:1, 0:512], 1.0 / D, 1e-5,
                ALU.mult, ALU.add),
                reads=[K(ps_r3)], writes=[K(row_s, "var")], tag=f"l1f_2var{l}_{tc}")
            P.op("scalar", lambda s, row_s=row_s: s.activation(
                row_s[0:1, 1024:1536], row_s[0:1, 512:1024], AF.Sqrt),
                reads=[K(row_s, "var")], writes=[K(row_s, "sd")])
            P.op("vector", lambda v, row_s=row_s: v.reciprocal(
                row_s[0:1, 512:1024], row_s[0:1, 1024:1536]),
                reads=[K(row_s, "sd")], writes=[K(row_s, "rstd")])
            P.op("tensor", lambda t, row_s=row_s: t.matmul(
                ps_b3[:, 512:1024], ones_f3[0:1, 0:128], row_s[0:1, 512:1024],
                start=True, stop=True),
                reads=[K(row_s, "rstd"), K(ones_f3)], writes=[K(ps_b3, 1)],
                tag=f"l1f_2rb{l}_{tc}")
            hn_s = hnb[tc]
            for k in range(8):
                P.op("vector", lambda v, xc_s=xc_s, k=k: v.tensor_mul(
                    xc_s[:, k * 512:(k + 1) * 512], xc_s[:, k * 512:(k + 1) * 512],
                    ps_b3[:, 512:1024]),
                    reads=[K(ps_b3, 1)], writes=[K(xc_s)], tag=f"l1f_2rs{l}_{tc}_{k}")
                P.op("vector", lambda v, xc_s=xc_s, hn_s=hn_s, k=k: v.tensor_scalar(
                    hn_s[:, k * 512:(k + 1) * 512], xc_s[:, k * 512:(k + 1) * 512],
                    lgc[:, k:k + 1], lbc[:, k:k + 1], ALU.mult, ALU.add),
                    reads=[K(xc_s), K(lgc), K(lbc)], writes=[K(hn_s, k)],
                    tag=f"l1f_2gn{l}_{tc}_{k}")
            hn_keys = [K(hn_s, k) for k in range(8)]
            for j in range(22):
                a_s, hh_s = afb[j], hhb[j]
                for half, m in ((0, j), (1, 22 + j)):
                    psb = ps_h3[half]
                    P.op("tensor", lambda t, hn_s=hn_s, m=m, psb=psb: [
                        t.matmul(psb[:],
                                 w1_s[:, k * 2 * L1F_HID + m * 128:
                                      k * 2 * L1F_HID + m * 128 + 128],
                                 hn_s[:, k * 512:(k + 1) * 512],
                                 start=(k == 0), stop=(k == 7))
                        for k in range(8)][-1],
                        reads=hn_keys + [K(w1_s, k) for k in range(8)],
                        writes=[K(psb)], tag=f"l1f_3mm{l}_{tc}_{m}")
                P.op("vector", lambda v, a_s=a_s, j=j: v.tensor_scalar(
                    a_s[:], ps_h3[0][:], b1c[:, j:j + 1], None, ALU.add),
                    reads=[K(ps_h3[0]), K(b1c)], writes=[K(a_s)],
                    tag=f"l1f_3a{l}_{tc}_{j}")
                P.op("scalar", lambda s, j=j, hh_s=hh_s: s.activation(
                    hh_s[:], ps_h3[1][:], AF.Gelu, bias=b1c[:, 22 + j:23 + j]),
                    reads=[K(ps_h3[1]), K(b1c)], writes=[K(hh_s)],
                    tag=f"l1f_3g{l}_{tc}_{j}")
                P.op("vector", lambda v, a_s=a_s, hh_s=hh_s: v.tensor_mul(
                    hh_s[:], hh_s[:], a_s[:]),
                    reads=[K(a_s)], writes=[K(hh_s)], tag=f"l1f_3h{l}_{tc}_{j}")
                P.dma("gpsimd", lambda g, hh_s=hh_s, j=j, tc=tc: g.dma_start(
                    out=hid_d[j * 128:(j + 1) * 128, tc * 512:(tc + 1) * 512],
                    in_=hh_s[:]),
                    reads=[K(hh_s)], writes=[K(hid_d, (tc, j))],
                    tag=f"l1f_3st{l}_{tc}_{j}")
    P.barrier()

    with ExitStack() as st:
        w2_s = load_weight_bf16(ctx, st, f"l1f_w2{l}", weights["l1f_w2p"][l], 22, D)
        ident_b4 = sbuf_const(ctx, st, "ident_bf")
        ones_f4 = sbuf_const(ctx, st, "ones_f32")
        b2row4 = ctx.sbuf(st, f"l1f_b2r{l}", [1, D], F32)
        P.dma("sync", lambda s: s.dma_start(
            out=b2row4[:], in_=weights["l1f_b2"][l].rearrange("(o d) -> o d", o=1)),
            writes=[K(b2row4)])
        ps_t4 = ctx.psum(st, f"l1f_ps4t{l}", [128, 1024], BF16)
        ps_y4 = [ctx.psum(st, f"l1f_ps4{l}_{i}", [128, 512], F32) for i in range(2)]
        b2b4 = bcast_row(ctx, st, f"l1f_b2b{l}", b2row4, D, ones_f4)
        hb = RotBuf(ctx, st, f"l1f_4h{l}", 2, [128, 22 * 128], BF16)
        yb4 = RotBuf(ctx, st, f"l1f_4y{l}", 2, [128, D], F32)
        ylb4 = RotBuf(ctx, st, f"l1f_4yl{l}", 2, [128, D], F32)
        ybf4 = RotBuf(ctx, st, f"l1f_4yb{l}", 2, [128, D], BF16)
        for ti in range(NTILES):
            tok0 = ti * 128
            h_s, y_s, yl_s, yb_s = hb[ti], yb4[ti], ylb4[ti], ybf4[ti]
            for kc in range(22):
                P.dma("sync", lambda s, h_s=h_s, kc=kc, tok0=tok0: s.dma_start(
                    out=h_s[:, kc * 128:(kc + 1) * 128],
                    in_=hid_d[kc * 128:(kc + 1) * 128, tok0:tok0 + 128]),
                    writes=[K(h_s, kc)], tag=f"l1f_4lh{l}_{ti}_{kc}")
            P.dma("sync", lambda s, yl_s=yl_s, tok0=tok0: s.dma_start(
                out=yl_s[:], in_=y_d[tok0:tok0 + 128, :]),
                writes=[K(yl_s)], tag=f"l1f_4ly{l}_{ti}")
            h_keys = [K(h_s, kc) for kc in range(22)]
            for oc in range(2):
                psb = ps_y4[oc]
                P.op("tensor", lambda t, h_s=h_s, oc=oc, psb=psb: [
                    t.matmul(psb[:], h_s[:, kc * 128:(kc + 1) * 128],
                             w2_s[:, kc * 1024 + oc * 512: kc * 1024 + oc * 512 + 512],
                             start=(kc == 0), stop=(kc == 21))
                    for kc in range(22)][-1],
                    reads=h_keys + [K(w2_s, kc) for kc in range(22)],
                    writes=[K(psb)], tag=f"l1f_4mm{l}_{ti}_{oc}")
                P.op("vector", lambda v, y_s=y_s, yl_s=yl_s, oc=oc, psb=psb:
                     v.tensor_add(y_s[:, oc * 512:(oc + 1) * 512], psb[:],
                                  yl_s[:, oc * 512:(oc + 1) * 512]),
                     reads=[K(psb), K(yl_s)], writes=[K(y_s, oc)],
                     tag=f"l1f_4ad{l}_{ti}_{oc}")
            P.op("vector", lambda v, y_s=y_s: v.tensor_add(y_s[:], y_s[:], b2b4[:]),
                 reads=[K(b2b4), K(y_s, 0), K(y_s, 1)], writes=[K(y_s, 0), K(y_s, 1)],
                 tag=f"l1f_4b2{l}_{ti}")
            P.dma("sync", lambda s, y_s=y_s, tok0=tok0: s.dma_start(
                out=y_d[tok0:tok0 + 128, :], in_=y_s[:]),
                reads=[K(y_s, 0), K(y_s, 1)], writes=[K(y_d, ti)],
                tag=f"l1f_4st{l}_{ti}")
            if l == 0:
                P.op("scalar", lambda s, yb_s=yb_s, y_s=y_s: s.copy(yb_s[:], y_s[:]),
                     reads=[K(y_s, 0), K(y_s, 1)], writes=[K(yb_s)],
                     tag=f"l1f_4cp{l}_{ti}")

                def tr8b(t, yb_s=yb_s):
                    for k in range(8):
                        inst = t.transpose(ps_t4[:, k * 128:(k + 1) * 128],
                                           yb_s[:, k * 128:(k + 1) * 128], ident_b4[:])
                    return inst
                P.op("tensor", tr8b, reads=[K(yb_s), K(ident_b4)], writes=[K(ps_t4)],
                     tag=f"l1f_4tr{l}_{ti}")
                yf_s = ybf4[ti + 1]
                P.op("vector", lambda v, yf_s=yf_s: v.tensor_copy(yf_s[:], ps_t4[:]),
                     reads=[K(ps_t4)], writes=[K(yf_s)], tag=f"l1f_4fe{l}_{ti}")
                for k in range(8):
                    P.dma("gpsimd", lambda g, yf_s=yf_s, k=k, tok0=tok0: g.dma_start(
                        out=yfm_d[k * 128:(k + 1) * 128, tok0:tok0 + 128],
                        in_=yf_s[:, k * 128:(k + 1) * 128]),
                        reads=[K(yf_s)], writes=[K(yfm_d, (ti, k))],
                        tag=f"l1f_4fs{l}_{ti}_{k}")
    P.barrier()


def l3_ffn(ctx, xn_fm_d, xn_fm32_d, g_d, weights):
    P, nc = ctx.P, ctx.nc
    oh_d = ctx.dram_scratch("l3_oh", [NT, 8], F32)
    acc_d = ctx.dram_scratch("l3_acc", [NT, D], F32)

    with ExitStack() as st:
        ones_f = sbuf_const(ctx, st, "ones_f32")
        gw_s = ctx.sbuf(st, "l3gw", [128, 8 * 8], F32)
        for cc in range(8):
            P.dma("sync", lambda g, cc=cc: g.dma_start(
                out=gw_s[:, cc * 8:(cc + 1) * 8],
                in_=weights["l3f_gwt"][cc * 128:(cc + 1) * 128, :]),
                writes=[K(gw_s, cc)])
        gbrow = ctx.sbuf(st, "l3gb", [1, 8], F32)
        P.dma("sync", lambda s: s.dma_start(
            out=gbrow[:], in_=weights["l3f_gb"].rearrange("(o d) -> o d", o=1)),
            writes=[K(gbrow)])
        gbb = bcast_row(ctx, st, "l3gbb", gbrow, 8, ones_f)
        xnf = RotBuf(ctx, st, "l3g_xn", 2, [128, 8 * 128], F32)
        lg = RotBuf(ctx, st, "l3g_lg", 2, [128, 48], F32)
        z128 = ctx.sbuf(st, "l3g_z", [128, 8], F32)
        ps_l = ctx.psum(st, "l3g_psl", [128, 512], F32)
        P.op("vector", lambda v: v.memset(z128[:], 0.0), writes=[K(z128)])
        for ti in range(NTILES):
            xn_s = xnf[ti]
            for k in range(8):
                P.dma("sync", lambda s, xn_s=xn_s, k=k, ti=ti: s.dma_start(
                    out=xn_s[:, k * 128:(k + 1) * 128],
                    in_=xn_fm32_d[k * 128:(k + 1) * 128, ti * 128:(ti + 1) * 128]),
                    writes=[K(xn_s, k)], tag=f"l3g_ldxn{ti}_{k}")
            xn_keys = [K(xn_s, k) for k in range(8)]
            l_ = lg[ti]
            P.op("tensor", lambda t, xn_s=xn_s: [
                t.matmul(ps_l[:, 0:8],
                         xn_s[:, k * 128:(k + 1) * 128],
                         gw_s[:, k * 8:(k + 1) * 8], start=(k == 0), stop=(k == 7))
                for k in range(8)][-1],
                reads=xn_keys + [K(gw_s, k) for k in range(8)],
                writes=[K(ps_l)], tag=f"l3g_lg{ti}")
            P.op("vector", lambda v, l_=l_: v.tensor_add(l_[:, 0:8], ps_l[:, 0:8], gbb[:, 0:8]),
                 reads=[K(ps_l), K(gbb)], writes=[K(l_)], tag=f"l3g_lev{ti}")
            P.op("vector", lambda v, l_=l_: v.max(l_[:, 8:16], l_[:, 0:8]),
                 reads=[K(l_)], writes=[K(l_)], tag=f"l3g_top{ti}")
            P.op("vector", lambda v, l_=l_: v.tensor_scalar(
                l_[:, 16:24], l_[:, 0:8], l_[:, 8:9], None, ALU.is_equal),
                reads=[K(l_)], writes=[K(l_)])
            P.op("vector", lambda v, l_=l_: v.tensor_tensor_scan(
                l_[:, 24:32], l_[:, 16:24], z128[:], 0.0, ALU.add, ALU.add),
                reads=[K(l_), K(z128)], writes=[K(l_)])
            P.op("vector", lambda v, l_=l_: v.tensor_scalar(
                l_[:, 24:32], l_[:, 24:32], 1.0, None, ALU.is_equal),
                reads=[K(l_)], writes=[K(l_)])
            P.op("vector", lambda v, l_=l_: v.tensor_mul(
                l_[:, 32:40], l_[:, 24:32], l_[:, 16:24]),
                reads=[K(l_)], writes=[K(l_)], tag=f"l3g_oh{ti}")
            P.dma("sync", lambda s, l_=l_, ti=ti: s.dma_start(
                out=oh_d[ti * 128:(ti + 1) * 128, :], in_=l_[:, 32:40]),
                reads=[K(l_)], writes=[K(oh_d, ti)], tag=f"l3g_st{ti}")
    P.barrier()

    for e in range(8):
        with ExitStack() as st:
            w1_s = load_weight_bf16(ctx, st, f"l3w1_{e}", weights["l3f_w1t"][e], 8, 4096)
            w2_s = load_weight_bf16(ctx, st, f"l3w2_{e}", weights["l3f_w2t"][e], 32, D)
            ident_f = sbuf_const(ctx, st, "ident_f32")
            ones_f = sbuf_const(ctx, st, "ones_f32")
            b1c = colize(ctx, st, f"l3b1_{e}", weights["l3f_b1"][e], 4096, ident_f)
            b2row = ctx.sbuf(st, f"l3b2r_{e}", [1, D], F32)
            P.dma("sync", lambda s, e=e, b2row=b2row: s.dma_start(
                out=b2row[:], in_=weights["l3f_b2"][e].rearrange("(o d) -> o d", o=1)),
                writes=[K(b2row)])
            b2b = bcast_row(ctx, st, f"l3b2_{e}", b2row, D, ones_f)
            xnf = RotBuf(ctx, st, "l3e_xn", 2, [128, 8 * 512], BF16)
            hidb = RotBuf(ctx, st, "l3e_hid", 2, [128, 16 * 512], BF16)
            ohb = RotBuf(ctx, st, "l3e_oh", 2, [128, 4 * 8], F32)
            yb = RotBuf(ctx, st, "l3e_y", 4, [128, D], F32)
            ps_h = [ctx.psum(st, f"l3e_psh{i}", [128, 512], F32) for i in range(2)]
            ps_y = [ctx.psum(st, f"l3e_psy{i}", [128, 512], F32) for i in range(2)]
            for ch in range(8):
                xn_s = xnf[ch]
                for k in range(8):
                    P.dma("sync", lambda s, xn_s=xn_s, k=k, ch=ch: s.dma_start(
                        out=xn_s[:, k * 512:(k + 1) * 512],
                        in_=xn_fm_d[k * 128:(k + 1) * 128, ch * 512:(ch + 1) * 512]),
                        writes=[K(xn_s, k)], tag=f"l3e_ldxn{e}_{ch}_{k}")
                oh_s = ohb[ch]
                for tl in range(4):
                    ti = ch * 4 + tl
                    P.dma("sync", lambda s, oh_s=oh_s, tl=tl, ti=ti: s.dma_start(
                        out=oh_s[:, tl * 8:(tl + 1) * 8],
                        in_=oh_d[ti * 128:(ti + 1) * 128, :]),
                        writes=[K(oh_s, tl)], tag=f"l3e_ldoh{e}_{ti}")
                xn_keys = [K(xn_s, k) for k in range(8)]
                for half in range(2):
                    hid_s = hidb[half]
                    for m in range(16):
                        hm = half * 16 + m
                        psb = ps_h[m % 2]
                        P.op("tensor", lambda t, xn_s=xn_s, hm=hm, psb=psb, w1_s=w1_s: [
                            t.matmul(psb[:],
                                     w1_s[:, k * 4096 + hm * 128: k * 4096 + hm * 128 + 128],
                                     xn_s[:, k * 512:(k + 1) * 512],
                                     start=(k == 0), stop=(k == 7))
                            for k in range(8)][-1],
                            reads=xn_keys + [K(w1_s, k) for k in range(8)],
                            writes=[K(psb)], tag=f"l3e_hmm{e}_{ch}_{hm}")
                        P.op("scalar", lambda s, hid_s=hid_s, m=m, hm=hm, psb=psb, b1c=b1c: s.activation(
                            hid_s[:, m * 512:(m + 1) * 512], psb[:], AF.Relu,
                            bias=b1c[:, hm:hm + 1]),
                            reads=[K(psb), K(b1c)], writes=[K(hid_s, m)],
                            tag=f"l3e_relu{e}_{ch}_{hm}")
                    hid_keys = [K(hid_s, m) for m in range(16)]
                    for tl in range(4):
                        ti = ch * 4 + tl
                        y_s = yb[tl]
                        for oc in range(2):
                            psb = ps_y[oc]
                            P.op("tensor", lambda t, hid_s=hid_s, half=half, tl=tl, oc=oc, psb=psb, w2_s=w2_s: [
                                t.matmul(psb[:],
                                         hid_s[:, m * 512 + tl * 128: m * 512 + tl * 128 + 128],
                                         w2_s[:, (half * 16 + m) * 1024 + oc * 512:
                                              (half * 16 + m) * 1024 + oc * 512 + 512],
                                         start=(m == 0), stop=(m == 15))
                                for m in range(16)][-1],
                                reads=hid_keys + [K(w2_s, half * 16 + m) for m in range(16)],
                                writes=[K(psb)], tag=f"l3e_ymm{e}_{ch}_{half}_{tl}{oc}")
                            if half == 0:
                                P.op("vector", lambda v, y_s=y_s, oc=oc, psb=psb: v.tensor_copy(
                                    y_s[:, oc * 512:(oc + 1) * 512], psb[:]),
                                    reads=[K(psb)], writes=[K(y_s, oc)],
                                    tag=f"l3e_yev{e}_{ch}_{half}_{tl}{oc}")
                            else:
                                P.op("vector", lambda v, y_s=y_s, oc=oc, psb=psb: v.tensor_add(
                                    y_s[:, oc * 512:(oc + 1) * 512],
                                    y_s[:, oc * 512:(oc + 1) * 512], psb[:]),
                                    reads=[K(psb)], writes=[K(y_s, oc)],
                                    tag=f"l3e_yad{e}_{ch}_{half}_{tl}{oc}")
                for tl in range(4):
                    ti = ch * 4 + tl
                    y_s = yb[tl]
                    P.op("vector", lambda v, y_s=y_s, b2b=b2b: v.tensor_add(y_s[:], y_s[:], b2b[:]),
                         reads=[K(b2b)], writes=[K(y_s, 0), K(y_s, 1)],
                         tag=f"l3e_b2{e}_{ti}")
                    P.op("vector", lambda v, y_s=y_s, oh_s=oh_s, tl=tl, e=e: v.tensor_scalar(
                        y_s[:], y_s[:], oh_s[:, tl * 8 + e: tl * 8 + e + 1], None, ALU.mult),
                        reads=[K(oh_s, tl)], writes=[K(y_s, 0), K(y_s, 1)],
                        tag=f"l3e_gt{e}_{ti}")
                    if e == 0:
                        P.dma("gpsimd", lambda g, y_s=y_s, ti=ti: g.dma_start(
                            out=acc_d[ti * 128:(ti + 1) * 128, :], in_=y_s[:]),
                            reads=[K(y_s, 0), K(y_s, 1)], writes=[K(acc_d, ti)],
                            tag=f"l3e_st{e}_{ti}")
                    else:
                        P.dma("gpsimd", lambda g, y_s=y_s, ti=ti: g.dma_start(
                            out=acc_d[ti * 128:(ti + 1) * 128, :], in_=y_s[:],
                            accum_op=ALU.add),
                            reads=[K(y_s, 0), K(y_s, 1)], writes=[K(acc_d, ti)],
                            tag=f"l3e_st{e}_{ti}")
        P.barrier()

    with ExitStack() as st:
        ab = RotBuf(ctx, st, "l3s_a", 2, [128, D], F32)
        gb_ = RotBuf(ctx, st, "l3s_g", 2, [128, D], F32)
        for ti in range(NTILES):
            a_s, g_s = ab[ti], gb_[ti]
            P.dma("sync", lambda s, a_s=a_s, ti=ti: s.dma_start(
                out=a_s[:], in_=acc_d[ti * 128:(ti + 1) * 128, :]),
                writes=[K(a_s)], tag=f"l3s_ld{ti}")
            P.op("scalar", lambda s, a_s=a_s, g_s=g_s: s.activation(g_s[:], a_s[:], AF.Silu),
                 reads=[K(a_s)], writes=[K(g_s)], tag=f"l3s_silu{ti}")
            P.dma("sync", lambda s, g_s=g_s, ti=ti: s.dma_start(
                out=g_d[ti * 128:(ti + 1) * 128, :], in_=g_s[:]),
                reads=[K(g_s)], writes=[K(g_d, ti)], tag=f"l3s_st{ti}")
    P.barrier()


L3A_NH = 8


def _l3a_blocks():
    out = []
    for B in range(48):
        j0 = B * 128
        t0 = max(0, -(-(j0 - 2) // 3))
        t1 = -(-(j0 + 126) // 3)
        out.append((j0, t0, t1 - t0, 3 * t0 + 2 - j0))
    return out


def l3_attn(ctx, xn_fm_d, s_d, weights):
    P, nc = ctx.P, ctx.nc
    H, DD, DV = 8, 128, 128
    q_d = ctx.dram_scratch("l3a_q", [1024, NT], BF16)
    k_d = ctx.dram_scratch("l3a_k", [1024, B_LOC * 3 * T], BF16)
    v_d = ctx.dram_scratch("l3a_v", [1024, B_LOC * 3 * T], BF16)
    beta_d = ctx.dram_scratch("l3a_beta", [24, NT], F32)
    g_d2 = ctx.dram_scratch("l3a_g", [8, NT], F32)
    o_d = ctx.dram_scratch("l3a_o", [NT, 1024], BF16)

    xq_d = ctx.dram_scratch("l3a_xq", [1024, NT], F32)
    xk_d = ctx.dram_scratch("l3a_xk", [3072, NT], F32)
    xv_d = ctx.dram_scratch("l3a_xv", [3072, NT], F32)
    with ExitStack() as st:
        xnf = RotBuf(ctx, st, "l3a_xn", 2, [128, 8 * 512], BF16)
        ob = RotBuf(ctx, st, "l3a_po", 3, [128, 512], F32)
        ps = [ctx.psum(st, f"l3a_ps{i}", [128, 512], F32) for i in range(2)]
        wq_s = load_weight_bf16(ctx, st, "l3awq", weights["l3a_Wq"], 8, 1024)
        wk_s = load_weight_bf16(ctx, st, "l3awk", weights["l3a_Wk"], 8, 3072)
        wv_s = load_weight_bf16(ctx, st, "l3awv", weights["l3a_Wv"], 8, 3072)
        wb_s = ctx.sbuf(st, "l3awb", [128, 8 * 24], BF16)
        wa_s = ctx.sbuf(st, "l3awa", [128, 8 * 8], BF16)
        for c in range(8):
            P.dma("gpsimd", lambda g, c=c: g.dma_start(
                out=wb_s[:, c * 24:(c + 1) * 24],
                in_=weights["l3a_Wb"][c * 128:(c + 1) * 128, :]),
                writes=[K(wb_s, c)])
            P.dma("gpsimd", lambda g, c=c: g.dma_start(
                out=wa_s[:, c * 8:(c + 1) * 8],
                in_=weights["l3a_Wa"][c * 128:(c + 1) * 128, :]),
                writes=[K(wa_s, c)])
        dtb = ctx.sbuf(st, "l3adtb", [8, 1], F32)
        P.dma("sync", lambda s: s.dma_start(
            out=dtb[:], in_=weights["l3a_dt_bias"].rearrange("(h o) -> h o", o=1)),
            writes=[K(dtb)])
        alog = ctx.sbuf(st, "l3aal", [8, 1], F32)
        P.dma("sync", lambda s: s.dma_start(
            out=alog[:], in_=weights["l3a_A_log"].rearrange("(h o) -> h o", o=1)),
            writes=[K(alog)])
        nega = ctx.sbuf(st, "l3ana", [8, 1], F32)
        P.op("scalar", lambda s: s.activation(nega[:], alog[:], AF.Exp),
             reads=[K(alog)], writes=[K(nega)])
        P.op("vector", lambda v: v.tensor_scalar_mul(nega[:], nega[:], -1.0),
             reads=[], writes=[K(nega)])
        bsc = RotBuf(ctx, st, "l3a_bs", 2, [24, 512], F32)
        gsc = RotBuf(ctx, st, "l3a_gs", 2, [8, 512], F32)
        for tc in range(8):
            xn_s = xnf[tc]
            for k in range(8):
                P.dma("sync", lambda s, xn_s=xn_s, k=k, tc=tc: s.dma_start(
                    out=xn_s[:, k * 512:(k + 1) * 512],
                    in_=xn_fm_d[k * 128:(k + 1) * 128, tc * 512:(tc + 1) * 512]),
                    writes=[K(xn_s, k)], tag=f"l3a_ldxn{tc}_{k}")
            xn_keys = [K(xn_s, k) for k in range(8)]
            for pi, (w_s, dst, nm) in enumerate(
                    ((wq_s, xq_d, 8), (wk_s, xk_d, 24), (wv_s, xv_d, 24))):
                for m in range(nm):
                    psb = ps[m % 2]
                    o_s = ob[m]
                    P.op("tensor", lambda t, xn_s=xn_s, w_s=w_s, m=m, nm=nm, psb=psb: [
                        t.matmul(psb[:],
                                 w_s[:, k * nm * 128 + m * 128: k * nm * 128 + (m + 1) * 128],
                                 xn_s[:, k * 512:(k + 1) * 512],
                                 start=(k == 0), stop=(k == 7))
                        for k in range(8)][-1],
                        reads=xn_keys + [K(w_s, k) for k in range(8)],
                        writes=[K(psb)], tag=f"l3a_pmm{tc}_{pi}_{m}")
                    P.op("vector", lambda v, o_s=o_s, psb=psb: v.tensor_copy(o_s[:], psb[:]),
                         reads=[K(psb)], writes=[K(o_s)], tag=f"l3a_pev{tc}_{pi}_{m}")
                    P.dma("sync", lambda g, o_s=o_s, dst=dst, m=m, tc=tc: g.dma_start(
                        out=dst[m * 128:(m + 1) * 128, tc * 512:(tc + 1) * 512],
                        in_=o_s[:]),
                        reads=[K(o_s)], writes=[K(dst, (tc, m))],
                        tag=f"l3a_pst{tc}_{pi}_{m}")
            b_s = bsc[tc]
            P.op("tensor", lambda t, xn_s=xn_s: [
                t.matmul(ps[0][0:24, 0:512],
                         wb_s[:, k * 24:(k + 1) * 24],
                         xn_s[:, k * 512:(k + 1) * 512],
                         start=(k == 0), stop=(k == 7))
                for k in range(8)][-1],
                reads=xn_keys + [K(wb_s, k) for k in range(8)],
                writes=[K(ps[0])], tag=f"l3a_bmm{tc}")
            P.op("scalar", lambda s, b_s=b_s: s.activation(
                b_s[:], ps[0][0:24, 0:512], AF.Sigmoid),
                reads=[K(ps[0])], writes=[K(b_s)], tag=f"l3a_bsg{tc}")
            P.op("vector", lambda v, b_s=b_s: v.tensor_scalar_mul(b_s[:], b_s[:], 2.0),
                 reads=[], writes=[K(b_s)], tag=f"l3a_b2{tc}")
            P.dma("sync", lambda s, b_s=b_s, tc=tc: s.dma_start(
                out=beta_d[:, tc * 512:(tc + 1) * 512], in_=b_s[:]),
                reads=[K(b_s)], writes=[K(beta_d, tc)], tag=f"l3a_bst{tc}")
            g_s = gsc[tc]
            P.op("tensor", lambda t, xn_s=xn_s: [
                t.matmul(ps[1][0:8, 0:512], wa_s[:, k * 8:(k + 1) * 8],
                         xn_s[:, k * 512:(k + 1) * 512],
                         start=(k == 0), stop=(k == 7))
                for k in range(8)][-1],
                reads=xn_keys + [K(wa_s, k) for k in range(8)],
                writes=[K(ps[1])], tag=f"l3a_gmm{tc}")
            P.op("vector", lambda v, g_s=g_s: v.tensor_scalar(
                g_s[:], ps[1][0:8, 0:512], dtb[:, 0:1], None, ALU.add),
                reads=[K(ps[1]), K(dtb)], writes=[K(g_s)], tag=f"l3a_gz{tc}")
            gt_s = gsc[tc + 1]
            P.op("scalar", lambda s, g_s=g_s, gt_s=gt_s: s.activation(
                gt_s[:], g_s[:], AF.Abs),
                reads=[K(g_s)], writes=[K(gt_s)])
            P.op("scalar", lambda s, gt_s=gt_s: s.activation(
                gt_s[:], gt_s[:], AF.Exp, scale=-1.0),
                reads=[], writes=[K(gt_s)])
            P.op("scalar", lambda s, gt_s=gt_s: s.activation(
                gt_s[:], gt_s[:], AF.Ln, bias=1.0),
                reads=[], writes=[K(gt_s)])
            P.op("vector", lambda v, g_s=g_s: v.tensor_scalar_max(g_s[:], g_s[:], 0.0),
                 reads=[], writes=[K(g_s)], tag=f"l3a_gmx{tc}")
            P.op("vector", lambda v, g_s=g_s, gt_s=gt_s: v.tensor_add(
                g_s[:], g_s[:], gt_s[:]),
                reads=[K(gt_s)], writes=[K(g_s)], tag=f"l3a_gsp{tc}")
            P.op("vector", lambda v, g_s=g_s: v.tensor_scalar(
                g_s[:], g_s[:], nega[:, 0:1], None, ALU.mult),
                reads=[K(nega)], writes=[K(g_s)], tag=f"l3a_gng{tc}")
            P.dma("sync", lambda s, g_s=g_s, tc=tc: s.dma_start(
                out=g_d2[:, tc * 512:(tc + 1) * 512], in_=g_s[:]),
                reads=[K(g_s)], writes=[K(g_d2, tc)], tag=f"l3a_gst{tc}")
    P.barrier()

    with ExitStack() as st:
        ones_f = sbuf_const(ctx, st, "ones_f32")
        cin = RotBuf(ctx, st, "l3a_ci", 2, [128, 2048 + 3], F32)
        cout = RotBuf(ctx, st, "l3a_co", 2, [128, 2048], F32)
        cwb = RotBuf(ctx, st, "l3a_cw", 2, [128, 4], F32)
        nrow = RotBuf(ctx, st, "l3a_nr", 2, [1, 2048], F32)
        onb = RotBuf(ctx, st, "l3a_ob", 2, [128, 2048], BF16)
        uend = RotBuf(ctx, st, "l3a_ue", 2, [128, 3 * 2048], BF16)
        ps_n = ctx.psum(st, "l3a_psn", [128, 2048], F32)
        onecol_f = ctx.sbuf(st, "l3a_oc", [128, 1], F32)
        P.op("vector", lambda v: v.memset(onecol_f[:], 1.0), writes=[K(onecol_f)])

        def conv_norm(ch, conv_w_ap, src_d, b, do_norm, scale):
            ci, co, cw = cin[ch + b], cout[ch + b], cwb[ch + b]
            P.dma("sync", lambda s: s.dma_start(
                out=cw[:], in_=conv_w_ap[ch * 128:(ch + 1) * 128, :]),
                writes=[K(cw)], tag=f"l3a_cw{ch}_{b}")
            P.dma("sync", lambda s: s.dma_start(
                out=ci[:, 3:2051], in_=src_d[ch * 128:(ch + 1) * 128,
                                             b * T:(b + 1) * T]),
                writes=[K(ci)], tag=f"l3a_cl{ch}_{b}")
            P.op("vector", lambda v: v.memset(ci[:, 0:3], 0.0),
                 reads=[], writes=[K(ci)], tag=f"l3a_cz{ch}_{b}")
            P.op("vector", lambda v: v.tensor_scalar(
                co[:], ci[:, 0:2048], cw[:, 0:1], None, ALU.mult),
                reads=[K(ci), K(cw)], writes=[K(co)], tag=f"l3a_c0{ch}_{b}")
            for i in range(1, 4):
                P.op("vector", lambda v, i=i: v.scalar_tensor_tensor(
                    co[:], ci[:, i:i + 2048], cw[:, i:i + 1], co[:],
                    ALU.mult, ALU.add),
                    reads=[K(ci), K(cw)], writes=[K(co)], tag=f"l3a_c{i}{ch}_{b}")
            P.op("scalar", lambda s: s.activation(co[:], co[:], AF.Silu),
                 reads=[], writes=[K(co)], tag=f"l3a_cs{ch}_{b}")
            ob_ = onb[ch + b]
            if do_norm:
                nr = nrow[ch + b]
                sq = cin[ch + b + 1]
                P.op("scalar", lambda s, sq=sq: s.activation(
                    sq[:, 0:2048], co[:], AF.Square),
                    reads=[K(co)], writes=[K(sq)], tag=f"l3a_nq{ch}_{b}")
                P.op("tensor", lambda t, sq=sq: [
                    t.matmul(ps_n[0:1, c * 512:(c + 1) * 512], onecol_f[:, 0:1],
                             sq[:, c * 512:(c + 1) * 512], start=True, stop=True)
                    for c in range(4)][-1],
                    reads=[K(sq), K(onecol_f)], writes=[K(ps_n)],
                    tag=f"l3a_ns{ch}_{b}")
                P.op("vector", lambda v, nr=nr: v.tensor_scalar_add(
                    nr[:], ps_n[0:1, 0:2048], 1e-6),
                    reads=[K(ps_n)], writes=[K(nr)], tag=f"l3a_ne{ch}_{b}")
                P.op("scalar", lambda s, nr=nr: s.activation(nr[:], nr[:], AF.Sqrt),
                     reads=[], writes=[K(nr)])
                P.op("vector", lambda v, nr=nr: v.reciprocal(nr[:], nr[:]),
                     reads=[], writes=[K(nr)])
                if scale != 1.0:
                    P.op("vector", lambda v, nr=nr: v.tensor_scalar_mul(
                        nr[:], nr[:], scale),
                        reads=[], writes=[K(nr)])
                P.op("tensor", lambda t, nr=nr: [
                    t.matmul(ps_n[:, c * 512:(c + 1) * 512], ones_f[0:1, 0:128],
                             nr[0:1, c * 512:(c + 1) * 512], start=True, stop=True)
                    for c in range(4)][-1],
                    reads=[K(nr), K(ones_f)], writes=[K(ps_n)],
                    tag=f"l3a_nb{ch}_{b}")
                P.op("vector", lambda v, ob_=ob_: v.tensor_mul(
                    ob_[:], co[:], ps_n[:, 0:2048]),
                    reads=[K(co), K(ps_n)], writes=[K(ob_)],
                    tag=f"l3a_nm{ch}_{b}")
            else:
                P.op("vector", lambda v, ob_=ob_: v.tensor_copy(ob_[:], co[:]),
                     reads=[K(co)], writes=[K(ob_)], tag=f"l3a_nc{ch}_{b}")
            return ob_

        for b in range(B_LOC):
            for h in range(H):
                ob_ = conv_norm(h, weights["l3a_conv_q"], xq_d, b, True,
                                float(DD) ** -0.5)
                P.dma("gpsimd", lambda g, ob_=ob_, h=h, b=b: g.dma_start(
                    out=q_d[h * 128:(h + 1) * 128, b * T:(b + 1) * T], in_=ob_[:]),
                    reads=[K(ob_)], writes=[K(q_d, (b, h))], tag=f"l3a_qst{b}_{h}")
            for pi, (wap, srcd, dstd, do_norm) in enumerate(
                    ((weights["l3a_conv_k"], xk_d, k_d, True),
                     (weights["l3a_conv_v"], xv_d, v_d, False))):
                for h in range(H):
                    ue = uend[h]
                    for i in range(3):
                        ob_ = conv_norm(i * 8 + h, wap, srcd, b, do_norm, 1.0)
                        P.op("vector", lambda v, ue=ue, ob_=ob_, i=i: v.tensor_copy(
                            ue[:, i::3], ob_[:]),
                            reads=[K(ob_)], writes=[K(ue, i)],
                            tag=f"l3a_ui{pi}_{b}_{h}_{i}")
                    P.dma("gpsimd", lambda g, ue=ue, dstd=dstd, h=h, b=b: g.dma_start(
                        out=dstd[h * 128:(h + 1) * 128,
                                 b * 3 * T:(b + 1) * 3 * T], in_=ue[:]),
                        reads=[K(ue, 0), K(ue, 1), K(ue, 2)],
                        writes=[K(dstd, (b, h))], tag=f"l3a_ust{pi}_{b}_{h}")
    P.barrier()

    gall_d = ctx.dram_scratch("l3a_gall", [16, 6144], F32)
    bupd_d = ctx.dram_scratch("l3a_bupd", [16, 6144], F32)
    with ExitStack() as st:
        rrow = RotBuf(ctx, st, "l3a_rr", 2, [1, 2048], F32)
        urow = RotBuf(ctx, st, "l3a_ur", 2, [1, 6144], F32)
        zrow = ctx.sbuf(st, "l3a_zr", [1, 6144], F32)
        P.op("vector", lambda v: v.memset(zrow[:], 0.0), writes=[K(zrow)])
        for b in range(B_LOC):
            for h in range(8):
                r = b * 8 + h
                ur = urow[r]
                for i in range(3):
                    rr = rrow[i]
                    P.dma("sync", lambda s, rr=rr, b=b, h=h, i=i: s.dma_start(
                        out=rr[:], in_=beta_d[i * 8 + h:i * 8 + h + 1,
                                              b * T:(b + 1) * T]),
                        writes=[K(rr)], tag=f"l3a_r1{r}_{i}")
                    P.op("vector", lambda v, rr=rr, ur=ur, i=i: v.tensor_copy(
                        ur[:, i::3], rr[:]),
                        reads=[K(rr)], writes=[K(ur)], tag=f"l3a_r2{r}_{i}")
                P.dma("sync", lambda s, ur=ur, r=r: s.dma_start(
                    out=bupd_d[r:r + 1, :], in_=ur[:]),
                    reads=[K(ur)], writes=[K(bupd_d, r)], tag=f"l3a_r3{r}")
                ur2 = urow[r + 1]
                rr = rrow[0]
                P.dma("sync", lambda s, rr=rr, b=b, h=h: s.dma_start(
                    out=rr[:], in_=g_d2[h:h + 1, b * T:(b + 1) * T]),
                    writes=[K(rr)], tag=f"l3a_r4{r}")
                P.op("vector", lambda v, ur2=ur2: v.memset(ur2[:], 0.0),
                     reads=[], writes=[K(ur2)], tag=f"l3a_r5{r}")
                P.op("vector", lambda v, rr=rr, ur2=ur2: v.tensor_copy(
                    ur2[:, 0::3], rr[:]),
                    reads=[K(rr)], writes=[K(ur2)], tag=f"l3a_r6{r}")
                P.op("vector", lambda v, ur2=ur2: v.tensor_tensor_scan(
                    ur2[:], ur2[:], zrow[:], 0.0, ALU.add, ALU.add),
                    reads=[K(zrow)], writes=[K(ur2)], tag=f"l3a_r7{r}")
                P.dma("sync", lambda s, ur2=ur2, r=r: s.dma_start(
                    out=gall_d[r:r + 1, :], in_=ur2[:]),
                    reads=[K(ur2)], writes=[K(gall_d, r)], tag=f"l3a_r8{r}")
    P.barrier()

    blocks = _l3a_blocks()
    with ExitStack() as st:
        ident_p2 = sbuf_const(ctx, st, "ident_bf")
        ones_p2 = sbuf_const(ctx, st, "ones_f32")
        triu_s = sbuf_const(ctx, st, "triu01s_bf")
        tril_s = sbuf_const(ctx, st, "tril01s_bf")
        stair = [sbuf_const(ctx, st, f"stair{r}_bf") for r in range(3)]
        W = H * 128
        q_sb = ctx.sbuf(st, "l3a2_q", [128, H * T], BF16)
        S32 = ctx.sbuf(st, "l3a2_S32", [128, W], F32)
        Sbf = ctx.sbuf(st, "l3a2_Sbf", [128, W], BF16)
        Kb = RotBuf(ctx, st, "l3a2_K", 2, [128, W], BF16)
        Vb = RotBuf(ctx, st, "l3a2_V", 2, [128, W], BF16)
        Ktil = RotBuf(ctx, st, "l3a2_Kt", 2, [128, W], BF16)
        Kbe = RotBuf(ctx, st, "l3a2_Kb", 2, [128, W], BF16)
        Vtm = RotBuf(ctx, st, "l3a2_Vt", 2, [128, W], BF16)
        expD = RotBuf(ctx, st, "l3a2_eD", 2, [128, W], F32)
        expDb = RotBuf(ctx, st, "l3a2_eDn", 2, [128, W], F32)
        abar = [RotBuf(ctx, st, f"l3a2_ab{i}", 2, [128, W], BF16) for i in range(2)]
        aplain = [RotBuf(ctx, st, f"l3a2_ap{i}", 2, [128, W], BF16) for i in range(2)]
        Xf = ctx.sbuf(st, "l3a2_X", [128, W], F32)
        Xb = ctx.sbuf(st, "l3a2_Xb", [128, W], BF16)
        Mbar = RotBuf(ctx, st, "l3a2_M", 2, [128, W], BF16)
        Osb = RotBuf(ctx, st, "l3a2_O", 2, [128, W], BF16)
        Qhat = RotBuf(ctx, st, "l3a2_Qh", 2, [128, W], BF16)
        scw = RotBuf(ctx, st, "l3a2_sc", 2, [128, 3 * H], F32)
        gslb = RotBuf(ctx, st, "l3a2_gs", 2, [1, 8 * 132], F32)
        ngslb = RotBuf(ctx, st, "l3a2_ngs", 2, [128, 8 * 132], F32)
        for i in range(2):
            P.op("vector", lambda v, t=ngslb[i]: v.memset(t[:], 0.0),
                 writes=[K(ngslb[i])], tag=f"l3a2_ngz{i}")
            P.op("vector", lambda v, t=gslb[i]: v.memset(t[:], 0.0),
                 writes=[K(gslb[i], h) for h in range(8)], tag=f"l3a2_gz0{i}")
        bslb = RotBuf(ctx, st, "l3a2_bs", 2, [1, 8 * 128], F32)
        erow = RotBuf(ctx, st, "l3a2_er", 2, [1, 128 * H], F32)
        ps_a = ctx.psum(st, "l3a2_psa", [128, 1024], F32)
        ps_b2 = ctx.psum(st, "l3a2_psb", [128, 1024], F32)
        ps_x = ctx.psum(st, "l3a2_psx", [128, 1024], F32)
        ps_tp2 = ctx.psum(st, "l3a2_pst", [128, 1024], BF16)
        ps_s2 = ctx.psum(st, "l3a2_pss", [128, 512], F32)
        for b in range(B_LOC):
            for h in range(H):
                P.dma("sync", lambda s, b=b, h=h: s.dma_start(
                    out=q_sb[:, h * T:(h + 1) * T],
                    in_=q_d[h * 128:(h + 1) * 128, b * T:(b + 1) * T]),
                    writes=[K(q_sb, h)], tag=f"l3a2_lq{b}_{h}")
            P.op("vector", lambda v: v.memset(S32[:], 0.0),
                 reads=[], writes=[K(S32)], tag=f"l3a2_s0{b}")
            P.op("vector", lambda v: v.memset(Sbf[:], 0.0),
                 reads=[], writes=[K(Sbf)], tag=f"l3a2_sb0{b}")
            for Bi, (j0, t0, nt, off) in enumerate(blocks):
                K_s, V_s = Kb[Bi], Vb[Bi]
                Kt_s, Kbe_s, Vt_s = Ktil[Bi], Kbe[Bi], Vtm[Bi]
                eD_s, eDn_s = expD[Bi], expDb[Bi]
                M_s, O_s, Qh_s = Mbar[Bi], Osb[Bi], Qhat[Bi]
                sc = scw[Bi]
                er = erow[Bi]
                gsl, ngsl, bsl = gslb[Bi], ngslb[Bi], bslb[Bi]
                ab0, ab1 = abar[0][Bi], abar[1][Bi]
                ap0, ap1 = aplain[0][Bi], aplain[1][Bi]
                GS = lambda h, a, bnd, step=1, gsl=gsl: gsl[0:1, h * 132 + a: h * 132 + bnd: step]
                NGS = lambda h, a, bnd, step=1, ngsl=ngsl: ngsl[0:1, h * 132 + a: h * 132 + bnd: step]
                for h in range(H):
                    if j0 == 0:
                        P.op("vector", lambda v, gsl=gsl, h=h: v.memset(
                            gsl[0:1, h * 132:h * 132 + 1], 0.0),
                            reads=[], writes=[K(gsl, h)], tag=f"l3a2_gz{b}_{Bi}_{h}")
                        P.dma("sync", lambda s, gsl=gsl, b=b, h=h: s.dma_start(
                            out=gsl[0:1, h * 132 + 1:h * 132 + 129],
                            in_=gall_d[b * 8 + h:b * 8 + h + 1, 0:128]),
                            writes=[K(gsl, h)], tag=f"l3a2_gl{b}_{Bi}_{h}")
                    else:
                        P.dma("sync", lambda s, gsl=gsl, b=b, h=h, j0=j0: s.dma_start(
                            out=gsl[0:1, h * 132:h * 132 + 129],
                            in_=gall_d[b * 8 + h:b * 8 + h + 1, j0 - 1:j0 + 128]),
                            writes=[K(gsl, h)], tag=f"l3a2_gl{b}_{Bi}_{h}")
                    P.dma("sync", lambda s, bsl=bsl, b=b, h=h, j0=j0: s.dma_start(
                        out=bsl[0:1, h * 128:(h + 1) * 128],
                        in_=bupd_d[b * 8 + h:b * 8 + h + 1, j0:j0 + 128]),
                        writes=[K(bsl, h)], tag=f"l3a2_bl{b}_{Bi}_{h}")
                    P.dma("sync", lambda s, K_s=K_s, b=b, h=h, j0=j0: s.dma_start(
                        out=K_s[:, h * 128:(h + 1) * 128],
                        in_=k_d[h * 128:(h + 1) * 128,
                                b * 3 * T + j0: b * 3 * T + j0 + 128]),
                        writes=[K(K_s, h)], tag=f"l3a2_lk{b}_{Bi}_{h}")
                    P.dma("sync", lambda s, V_s=V_s, b=b, h=h, j0=j0: s.dma_start(
                        out=V_s[:, h * 128:(h + 1) * 128],
                        in_=v_d[h * 128:(h + 1) * 128,
                                b * 3 * T + j0: b * 3 * T + j0 + 128]),
                        writes=[K(V_s, h)], tag=f"l3a2_lv{b}_{Bi}_{h}")
                gsl_keys = [K(gsl, h) for h in range(H)]
                bsl_keys = [K(bsl, h) for h in range(H)]
                P.op("vector", lambda v, ngsl=ngsl, gsl=gsl: v.tensor_scalar_mul(
                    ngsl[0:1, :], gsl[:], -1.0),
                    reads=gsl_keys, writes=[K(ngsl)], tag=f"l3a2_ng{b}_{Bi}")
                K_keys = [K(K_s, h) for h in range(H)]
                V_keys = [K(V_s, h) for h in range(H)]
                P.op("tensor", lambda t, bsl=bsl: [
                    t.matmul(ps_b2[:, h * 128:(h + 1) * 128], ones_p2[0:1, 0:128],
                             bsl[0:1, h * 128:(h + 1) * 128], start=True, stop=True)
                    for h in range(H)][-1],
                    reads=bsl_keys + [K(ones_p2)],
                    writes=[K(ps_b2)], tag=f"l3a2_bb{b}_{Bi}")
                P.op("vector", lambda v, Kt_s=Kt_s, K_s=K_s: v.tensor_mul(
                    Kt_s[:], K_s[:], ps_b2[:, 0:W]),
                    reads=K_keys + [K(ps_b2)], writes=[K(Kt_s)],
                    tag=f"l3a2_kt{b}_{Bi}")
                P.op("vector", lambda v, V_s=V_s: v.tensor_mul(
                    V_s[:], V_s[:], ps_b2[:, 0:W]),
                    reads=[K(ps_b2)] + V_keys,
                    writes=[K(V_s, hh) for hh in range(H)],
                    tag=f"l3a2_vb{b}_{Bi}")
                P.op("tensor", lambda t, GS=GS, NGS=NGS: [
                    [t.matmul(ps_b2[:, h * 128:(h + 1) * 128], ones_p2[0:1, 0:128],
                              GS(h, 1, 129), start=True, stop=False),
                     t.matmul(ps_b2[:, h * 128:(h + 1) * 128],
                              NGS(h, 1, 129), ones_p2[0:1, 0:128],
                              start=False, stop=True)][-1]
                    for h in range(H)][-1],
                    reads=gsl_keys + [K(ngsl), K(ones_p2)],
                    writes=[K(ps_b2)], tag=f"l3a2_D{b}_{Bi}")
                P.op("vector", lambda v, eD_s=eD_s: v.tensor_scalar(
                    eD_s[:], ps_b2[:, 0:W], 0.0, None, ALU.min),
                    reads=[K(ps_b2)], writes=[K(eD_s)], tag=f"l3a2_eDc{b}_{Bi}")
                P.op("scalar", lambda s, eD_s=eD_s: s.activation(
                    eD_s[:], eD_s[:], AF.Exp),
                    reads=[], writes=[K(eD_s)], tag=f"l3a2_eD{b}_{Bi}")
                P.op("vector", lambda v, eDn_s=eDn_s: v.tensor_scalar(
                    eDn_s[:], ps_b2[:, 0:W], 0.0, None, ALU.max),
                    reads=[K(ps_b2)], writes=[K(eDn_s)], tag=f"l3a2_eDnc{b}_{Bi}")
                P.op("scalar", lambda s, eDn_s=eDn_s: s.activation(
                    eDn_s[:], eDn_s[:], AF.Exp, scale=-1.0),
                    reads=[], writes=[K(eDn_s)], tag=f"l3a2_eDn{b}_{Bi}")
                P.op("tensor", lambda t, K_s=K_s, Kt_s=Kt_s: [
                    t.matmul(ps_a[:, h * 128:(h + 1) * 128],
                             K_s[:, h * 128:(h + 1) * 128],
                             Kt_s[:, h * 128:(h + 1) * 128], start=True, stop=True)
                    for h in range(H)][-1],
                    reads=K_keys + [K(Kt_s)], writes=[K(ps_a)],
                    tag=f"l3a2_kk{b}_{Bi}")
                P.op("vector", lambda v, ab0=ab0, eD_s=eD_s: v.tensor_mul(
                    ab0[:], ps_a[:, 0:W], eD_s[:]),
                    reads=[K(ps_a), K(eD_s)], writes=[K(ab0)],
                    tag=f"l3a2_ab1{b}_{Bi}")
                P.op("vector", lambda v, ab0=ab0: v.tensor_tensor(
                    ab0[:].rearrange("p (h j) -> p h j", h=H),
                    ab0[:].rearrange("p (h j) -> p h j", h=H),
                    triu_s[:].rearrange("p (o j) -> p o j", o=1)
                        .broadcast_to([128, H, 128]),
                    ALU.mult),
                    reads=[K(triu_s)], writes=[K(ab0)], tag=f"l3a2_ab2{b}_{Bi}")
                P.op("tensor", lambda t, K_s=K_s, Kt_s=Kt_s: [
                    t.matmul(ps_a[:, h * 128:(h + 1) * 128],
                             Kt_s[:, h * 128:(h + 1) * 128],
                             K_s[:, h * 128:(h + 1) * 128], start=True, stop=True)
                    for h in range(H)][-1],
                    reads=K_keys + [K(Kt_s)], writes=[K(ps_a)],
                    tag=f"l3a2_kk2{b}_{Bi}")
                P.op("vector", lambda v, ap0=ap0, eDn_s=eDn_s: v.tensor_mul(
                    ap0[:], ps_a[:, 0:W], eDn_s[:]),
                    reads=[K(ps_a), K(eDn_s)], writes=[K(ap0)],
                    tag=f"l3a2_ap1{b}_{Bi}")
                P.op("vector", lambda v, ap0=ap0: v.tensor_tensor(
                    ap0[:].rearrange("p (h j) -> p h j", h=H),
                    ap0[:].rearrange("p (h j) -> p h j", h=H),
                    tril_s[:].rearrange("p (o j) -> p o j", o=1)
                        .broadcast_to([128, H, 128]),
                    ALU.mult),
                    reads=[K(tril_s)], writes=[K(ap0)], tag=f"l3a2_ap2{b}_{Bi}")
                P.op("scalar", lambda s, er=er, GS=GS, NGS=NGS: [
                    s.activation(er[0:1, h * 128:(h + 1) * 128],
                                 GS(h, 1, 129), AF.Exp,
                                 bias=NGS(h, 0, 1))
                    for h in range(H)][-1],
                    reads=gsl_keys + [K(ngsl)],
                    writes=[K(er)], tag=f"l3a2_eg{b}_{Bi}")
                P.op("tensor", lambda t, er=er: [
                    t.matmul(ps_b2[:, h * 128:(h + 1) * 128], ones_p2[0:1, 0:128],
                             er[0:1, h * 128:(h + 1) * 128], start=True, stop=True)
                    for h in range(H)][-1],
                    reads=[K(er), K(ones_p2)], writes=[K(ps_b2)],
                    tag=f"l3a2_egb{b}_{Bi}")
                P.op("vector", lambda v, Kbe_s=Kbe_s, Kt_s=Kt_s: v.tensor_mul(
                    Kbe_s[:], Kt_s[:], ps_b2[:, 0:W]),
                    reads=[K(Kt_s), K(ps_b2)], writes=[K(Kbe_s)],
                    tag=f"l3a2_kbe{b}_{Bi}")
                P.op("tensor", lambda t, Kbe_s=Kbe_s: [
                    t.matmul(ps_x[:, h * 128:(h + 1) * 128],
                             Kbe_s[:, h * 128:(h + 1) * 128],
                             Sbf[:, h * 128:(h + 1) * 128], start=True, stop=True)
                    for h in range(H)][-1],
                    reads=[K(Kbe_s), K(Sbf)], writes=[K(ps_x)],
                    tag=f"l3a2_ks{b}_{Bi}")
                P.op("tensor", lambda t, V_s=V_s: [
                    t.transpose(ps_tp2[:, h * 128:(h + 1) * 128],
                                V_s[:, h * 128:(h + 1) * 128], ident_p2[:])
                    for h in range(H)][-1],
                    reads=V_keys + [K(ident_p2)], writes=[K(ps_tp2)],
                    tag=f"l3a2_vt{b}_{Bi}")
                P.op("vector", lambda v, Vt_s=Vt_s: v.tensor_copy(Vt_s[:], ps_tp2[:, 0:W]),
                     reads=[K(ps_tp2)], writes=[K(Vt_s)], tag=f"l3a2_vte{b}_{Bi}")
                P.op("vector", lambda v, Vt_s=Vt_s: v.tensor_sub(
                    Xf[:], Vt_s[:], ps_x[:, 0:W]),
                    reads=[K(Vt_s), K(ps_x)], writes=[K(Xf)],
                    tag=f"l3a2_B{b}_{Bi}")
                P.op("vector", lambda v: v.tensor_copy(Xb[:], Xf[:]),
                     reads=[K(Xf)], writes=[K(Xb)], tag=f"l3a2_Bb{b}_{Bi}")
                cur_ab, cur_ap = ab0, ap0
                for lev in range(7):
                    P.op("tensor", lambda t, cur_ab=cur_ab: [
                        t.matmul(ps_x[:, h * 128:(h + 1) * 128],
                                 cur_ab[:, h * 128:(h + 1) * 128],
                                 Xb[:, h * 128:(h + 1) * 128], start=True, stop=True)
                        for h in range(H)][-1],
                        reads=[K(cur_ab), K(Xb)], writes=[K(ps_x)],
                        tag=f"l3a2_fx{b}_{Bi}_{lev}")
                    if lev == 0:
                        P.op("vector", lambda v: v.tensor_sub(Xf[:], Xf[:], ps_x[:, 0:W]),
                             reads=[K(ps_x)], writes=[K(Xf)],
                             tag=f"l3a2_fs{b}_{Bi}_{lev}")
                    else:
                        P.op("vector", lambda v: v.tensor_add(Xf[:], Xf[:], ps_x[:, 0:W]),
                             reads=[K(ps_x)], writes=[K(Xf)],
                             tag=f"l3a2_fs{b}_{Bi}_{lev}")
                    P.op("vector", lambda v: v.tensor_copy(Xb[:], Xf[:]),
                         reads=[K(Xf)], writes=[K(Xb)], tag=f"l3a2_fb{b}_{Bi}_{lev}")
                    if lev < 6:
                        nab = abar[(lev + 1) % 2][Bi]
                        nap = aplain[(lev + 1) % 2][Bi]
                        P.op("tensor", lambda t, cur_ab=cur_ab, cur_ap=cur_ap: [
                            t.matmul(ps_a[:, h * 128:(h + 1) * 128],
                                     cur_ap[:, h * 128:(h + 1) * 128],
                                     cur_ab[:, h * 128:(h + 1) * 128],
                                     start=True, stop=True)
                            for h in range(H)][-1],
                            reads=[K(cur_ab), K(cur_ap)], writes=[K(ps_a)],
                            tag=f"l3a2_sq1{b}_{Bi}_{lev}")
                        P.op("vector", lambda v, nab=nab: v.tensor_copy(
                            nab[:], ps_a[:, 0:W]),
                            reads=[K(ps_a)], writes=[K(nab)],
                            tag=f"l3a2_sqe1{b}_{Bi}_{lev}")
                        P.op("tensor", lambda t, cur_ab=cur_ab, cur_ap=cur_ap: [
                            t.matmul(ps_a[:, h * 128:(h + 1) * 128],
                                     cur_ab[:, h * 128:(h + 1) * 128],
                                     cur_ap[:, h * 128:(h + 1) * 128],
                                     start=True, stop=True)
                            for h in range(H)][-1],
                            reads=[K(cur_ab), K(cur_ap)], writes=[K(ps_a)],
                            tag=f"l3a2_sq2{b}_{Bi}_{lev}")
                        P.op("vector", lambda v, nap=nap: v.tensor_copy(
                            nap[:], ps_a[:, 0:W]),
                            reads=[K(ps_a)], writes=[K(nap)],
                            tag=f"l3a2_sqe2{b}_{Bi}_{lev}")
                        cur_ab, cur_ap = nab, nap
                P.op("tensor", lambda t, K_s=K_s, t0=t0, nt=nt: [
                    t.matmul(ps_a[:, h * 128:h * 128 + nt],
                             K_s[:, h * 128:(h + 1) * 128],
                             q_sb[:, h * T + t0: h * T + t0 + nt],
                             start=True, stop=True)
                    for h in range(H)][-1],
                    reads=K_keys + [K(q_sb, h) for h in range(H)],
                    writes=[K(ps_a)], tag=f"l3a2_kq{b}_{Bi}")
                P.op("tensor", lambda t, gsl=gsl, NGS=NGS, nt=nt, off=off: [
                    [t.matmul(ps_b2[:, h * 128:h * 128 + nt], ones_p2[0:1, 0:128],
                              gsl[0:1, h * 132 + 1 + off:
                                  h * 132 + 1 + off + 3 * (nt - 1) + 1: 3],
                              start=True, stop=False),
                     t.matmul(ps_b2[:, h * 128:h * 128 + nt],
                              NGS(h, 1, 129), ones_p2[0:1, 0:nt],
                              start=False, stop=True)][-1]
                    for h in range(H)][-1],
                    reads=gsl_keys + [K(ngsl), K(ones_p2)],
                    writes=[K(ps_b2)], tag=f"l3a2_Dq{b}_{Bi}")
                P.op("vector", lambda v, eD_s=eD_s, nt=nt: [
                    v.tensor_scalar(eD_s[:, h * 128:h * 128 + nt],
                                    ps_b2[:, h * 128:h * 128 + nt],
                                    0.0, None, ALU.min)
                    for h in range(H)][-1],
                    reads=[K(ps_b2)], writes=[K(eD_s)], tag=f"l3a2_eDqc{b}_{Bi}")
                P.op("scalar", lambda s, eD_s=eD_s, nt=nt: [
                    s.activation(eD_s[:, h * 128:h * 128 + nt],
                                 eD_s[:, h * 128:h * 128 + nt], AF.Exp)
                    for h in range(H)][-1],
                    reads=[], writes=[K(eD_s)], tag=f"l3a2_eDq{b}_{Bi}")
                P.op("vector", lambda v, M_s=M_s, eD_s=eD_s, nt=nt: [
                    v.tensor_mul(M_s[:, h * 128:h * 128 + nt],
                                 ps_a[:, h * 128:h * 128 + nt],
                                 eD_s[:, h * 128:h * 128 + nt])
                    for h in range(H)][-1],
                    reads=[K(ps_a), K(eD_s)], writes=[K(M_s)],
                    tag=f"l3a2_M1{b}_{Bi}")
                P.op("vector", lambda v, M_s=M_s, nt=nt, off=off: [
                    v.tensor_mul(M_s[:, h * 128:h * 128 + nt],
                                 M_s[:, h * 128:h * 128 + nt],
                                 stair[off][:, 0:nt])
                    for h in range(H)][-1],
                    reads=[K(stair[off])], writes=[K(M_s)],
                    tag=f"l3a2_M2{b}_{Bi}")
                P.op("scalar", lambda s, er=er, gsl=gsl, NGS=NGS, nt=nt, off=off: [
                    s.activation(er[0:1, h * 128:h * 128 + nt],
                                 gsl[0:1, h * 132 + 1 + off:
                                     h * 132 + 1 + off + 3 * (nt - 1) + 1: 3],
                                 AF.Exp, bias=NGS(h, 0, 1))
                    for h in range(H)][-1],
                    reads=gsl_keys + [K(ngsl)],
                    writes=[K(er)], tag=f"l3a2_egq{b}_{Bi}")
                P.op("tensor", lambda t, er=er, nt=nt: [
                    t.matmul(ps_b2[:, h * 128:h * 128 + nt], ones_p2[0:1, 0:128],
                             er[0:1, h * 128:h * 128 + nt], start=True, stop=True)
                    for h in range(H)][-1],
                    reads=[K(er), K(ones_p2)], writes=[K(ps_b2)],
                    tag=f"l3a2_egqb{b}_{Bi}")
                P.op("vector", lambda v, Qh_s=Qh_s, t0=t0, nt=nt: [
                    v.tensor_mul(Qh_s[:, h * 128:h * 128 + nt],
                                 q_sb[:, h * T + t0: h * T + t0 + nt],
                                 ps_b2[:, h * 128:h * 128 + nt])
                    for h in range(H)][-1],
                    reads=[K(q_sb, h) for h in range(H)] + [K(ps_b2)],
                    writes=[K(Qh_s)], tag=f"l3a2_qh{b}_{Bi}")
                P.op("tensor", lambda t, M_s=M_s, Qh_s=Qh_s, nt=nt: [
                    [t.matmul(ps_x[0:nt, h * 128:(h + 1) * 128],
                              M_s[:, h * 128:h * 128 + nt],
                              Xb[:, h * 128:(h + 1) * 128], start=True, stop=False),
                     t.matmul(ps_x[0:nt, h * 128:(h + 1) * 128],
                              Qh_s[:, h * 128:h * 128 + nt],
                              Sbf[:, h * 128:(h + 1) * 128],
                              start=False, stop=True)][-1]
                    for h in range(H)][-1],
                    reads=[K(M_s), K(Qh_s), K(Xb), K(Sbf)],
                    writes=[K(ps_x)], tag=f"l3a2_O{b}_{Bi}")
                P.op("vector", lambda v, O_s=O_s, nt=nt: v.tensor_copy(
                    O_s[0:nt, 0:W], ps_x[0:nt, 0:W]),
                    reads=[K(ps_x)], writes=[K(O_s)], tag=f"l3a2_Oe{b}_{Bi}")
                for h in range(H):
                    P.dma("gpsimd", lambda g, O_s=O_s, b=b, h=h, t0=t0, nt=nt:
                          g.dma_start(
                              out=o_d[b * T + t0: b * T + t0 + nt,
                                      h * 128:(h + 1) * 128],
                              in_=O_s[0:nt, h * 128:(h + 1) * 128]),
                          reads=[K(O_s)], writes=[K(o_d, (b, Bi, h))],
                          tag=f"l3a2_Os{b}_{Bi}_{h}")
                P.op("scalar", lambda s, er=er, GS=GS, NGS=NGS: [
                    s.activation(er[0:1, h * 128:(h + 1) * 128],
                                 NGS(h, 1, 129), AF.Exp,
                                 bias=GS(h, 128, 129))
                    for h in range(H)][-1],
                    reads=gsl_keys + [K(ngsl)],
                    writes=[K(er)], tag=f"l3a2_ese{b}_{Bi}")
                P.op("tensor", lambda t, er=er: [
                    t.matmul(ps_b2[:, h * 128:(h + 1) * 128], ones_p2[0:1, 0:128],
                             er[0:1, h * 128:(h + 1) * 128], start=True, stop=True)
                    for h in range(H)][-1],
                    reads=[K(er), K(ones_p2)], writes=[K(ps_b2)],
                    tag=f"l3a2_eseb{b}_{Bi}")
                P.op("vector", lambda v, Kbe_s=Kbe_s, K_s=K_s: v.tensor_mul(
                    Kbe_s[:], K_s[:], ps_b2[:, 0:W]),
                    reads=K_keys + [K(ps_b2)], writes=[K(Kbe_s)],
                    tag=f"l3a2_kh{b}_{Bi}")
                P.op("tensor", lambda t, Kbe_s=Kbe_s: [
                    t.transpose(ps_tp2[:, h * 128:(h + 1) * 128],
                                Kbe_s[:, h * 128:(h + 1) * 128], ident_p2[:])
                    for h in range(H)][-1],
                    reads=[K(Kbe_s), K(ident_p2)], writes=[K(ps_tp2)],
                    tag=f"l3a2_kht{b}_{Bi}")
                P.op("vector", lambda v, Kt_s=Kt_s: v.tensor_copy(Kt_s[:], ps_tp2[:, 0:W]),
                     reads=[K(ps_tp2)], writes=[K(Kt_s)], tag=f"l3a2_khte{b}_{Bi}")
                P.op("tensor", lambda t, Kt_s=Kt_s: [
                    t.matmul(ps_x[:, h * 128:(h + 1) * 128],
                             Kt_s[:, h * 128:(h + 1) * 128],
                             Xb[:, h * 128:(h + 1) * 128], start=True, stop=True)
                    for h in range(H)][-1],
                    reads=[K(Kt_s), K(Xb)], writes=[K(ps_x)],
                    tag=f"l3a2_sd{b}_{Bi}")
                P.op("scalar", lambda s, sc=sc, GS=GS, NGS=NGS: [
                    s.activation(sc[0:1, h:h + 1],
                                 GS(h, 128, 129), AF.Exp,
                                 bias=NGS(h, 0, 1))
                    for h in range(H)][-1],
                    reads=gsl_keys + [K(ngsl)],
                    writes=[K(sc)], tag=f"l3a2_lam{b}_{Bi}")
                P.op("tensor", lambda t, sc=sc: [
                    t.matmul(ps_s2[:, h:h + 1], ones_p2[0:1, 0:128],
                             sc[0:1, h:h + 1], start=True, stop=True)
                    for h in range(H)][-1],
                    reads=[K(sc), K(ones_p2)], writes=[K(ps_s2)],
                    tag=f"l3a2_lamb{b}_{Bi}")
                P.op("vector", lambda v, sc=sc: v.tensor_copy(
                    sc[:, H:2 * H], ps_s2[:, 0:H]),
                    reads=[K(ps_s2)], writes=[K(sc, "c")], tag=f"l3a2_lame{b}_{Bi}")
                P.op("vector", lambda v, sc=sc: [
                    v.scalar_tensor_tensor(
                        S32[:, h * 128:(h + 1) * 128],
                        S32[:, h * 128:(h + 1) * 128],
                        sc[:, H + h:H + h + 1],
                        ps_x[:, h * 128:(h + 1) * 128],
                        ALU.mult, ALU.add)
                    for h in range(H)][-1],
                    reads=[K(sc, "c"), K(ps_x)], writes=[K(S32)],
                    tag=f"l3a2_su{b}_{Bi}")
                P.op("vector", lambda v: v.tensor_copy(Sbf[:], S32[:]),
                     reads=[K(S32)], writes=[K(Sbf)], tag=f"l3a2_sbe{b}_{Bi}")
    P.barrier()

    with ExitStack() as st0:
        wo_s = load_weight_bf16(ctx, st0, "l3awo", weights["l3a_Wo"], 8, D)
        wg_s = load_weight_bf16(ctx, st0, "l3awg", weights["l3a_Wg"], 8, D)
        ofm = ctx.sbuf(st0, "l3a_ofm", [128, 8 * NT], BF16)
        with ExitStack() as st:
            ident_p3 = sbuf_const(ctx, st, "ident_bf")
            ones_p3 = sbuf_const(ctx, st, "ones_f32")
            rw_row = ctx.sbuf(st, "l3a_rw", [1, 128], F32)
            P.dma("sync", lambda s: s.dma_start(
                out=rw_row[:], in_=weights["l3a_rmsw"].rearrange("(o d) -> o d", o=1)),
                writes=[K(rw_row)])
            ps_rb = ctx.psum(st, "l3a3_psrb", [128, 1024], F32)
            rwb = ctx.sbuf(st, "l3a3_rwb", [128, 1024], F32)
            P.op("tensor", lambda t: [
                t.matmul(ps_rb[:, h * 128:(h + 1) * 128], ones_p3[0:1, 0:128],
                         rw_row[0:1, 0:128], start=True, stop=True)
                for h in range(8)][-1],
                reads=[K(rw_row), K(ones_p3)], writes=[K(ps_rb)], tag="l3a3_rwb")
            P.op("vector", lambda v: v.tensor_copy(rwb[:], ps_rb[:, 0:1024]),
                 reads=[K(ps_rb)], writes=[K(rwb)], tag="l3a3_rwe")
            xnl = RotBuf(ctx, st, "l3a3_xn", 2, [128, 8 * 128], BF16)
            ol = RotBuf(ctx, st, "l3a3_o", 2, [128, 1024], BF16)
            of = RotBuf(ctx, st, "l3a3_of", 2, [128, 1024], F32)
            sq = RotBuf(ctx, st, "l3a3_sq", 2, [128, 1024], F32)
            nr8 = RotBuf(ctx, st, "l3a3_nr", 2, [128, 16], F32)
            gt = RotBuf(ctx, st, "l3a3_gt", 2, [128, 1024], F32)
            obf = RotBuf(ctx, st, "l3a3_obf", 2, [128, 1024], BF16)
            ps_g = [ctx.psum(st, f"l3a3_psg{i}", [128, 512], F32) for i in range(2)]
            ps_t = ctx.psum(st, "l3a3_pst", [128, 1024], BF16)
            for ti in range(NTILES):
                tok0 = ti * 128
                o_s, of_s, sq_s, n_s, gt_s, ob_s = (ol[ti], of[ti], sq[ti],
                                                    nr8[ti], gt[ti], obf[ti])
                xn_s = xnl[ti]
                P.dma("sync", lambda s, o_s=o_s, tok0=tok0: s.dma_start(
                    out=o_s[:], in_=o_d[tok0:tok0 + 128, :]),
                    writes=[K(o_s)], tag=f"l3a3_lo{ti}")
                for k in range(8):
                    P.dma("sync", lambda s, xn_s=xn_s, k=k, ti=ti: s.dma_start(
                        out=xn_s[:, k * 128:(k + 1) * 128],
                        in_=xn_fm_d[k * 128:(k + 1) * 128, ti * 128:(ti + 1) * 128]),
                        writes=[K(xn_s, k)], tag=f"l3a3_lx{ti}_{k}")
                P.op("scalar", lambda s, o_s=o_s, sq_s=sq_s: s.activation(
                    sq_s[:], o_s[:], AF.Square),
                    reads=[K(o_s)], writes=[K(sq_s)], tag=f"l3a3_sq{ti}")
                P.op("vector", lambda v, sq_s=sq_s, n_s=n_s: v.reduce_sum(
                    n_s[:, 0:8], sq_s[:].rearrange("p (h d) -> p h d", d=128),
                    axis=AX.X),
                    reads=[K(sq_s)], writes=[K(n_s)], tag=f"l3a3_rs{ti}")
                P.op("vector", lambda v, n_s=n_s: v.tensor_scalar(
                    n_s[:, 0:8], n_s[:, 0:8], 1.0 / 128.0, 1e-5, ALU.mult, ALU.add),
                    reads=[], writes=[K(n_s)])
                P.op("scalar", lambda s, n_s=n_s: s.activation(
                    n_s[:, 8:16], n_s[:, 0:8], AF.Sqrt),
                    reads=[K(n_s)], writes=[K(n_s)])
                P.op("vector", lambda v, n_s=n_s: v.reciprocal(
                    n_s[:, 0:8], n_s[:, 8:16]),
                    reads=[], writes=[K(n_s)])
                P.op("vector", lambda v, o_s=o_s, of_s=of_s, n_s=n_s: v.tensor_tensor(
                    of_s[:].rearrange("p (h d) -> p h d", d=128),
                    o_s[:].rearrange("p (h d) -> p h d", d=128),
                    n_s[:, 0:8].rearrange("p (h o) -> p h o", o=1)
                        .broadcast_to([128, 8, 128]),
                    ALU.mult),
                    reads=[K(o_s), K(n_s)], writes=[K(of_s)], tag=f"l3a3_nm{ti}")
                P.op("vector", lambda v, of_s=of_s: v.tensor_mul(
                    of_s[:], of_s[:], rwb[:]),
                    reads=[K(rwb)], writes=[K(of_s)], tag=f"l3a3_rw{ti}")
                xn_keys = [K(xn_s, k) for k in range(8)]
                for oc in range(2):
                    psb = ps_g[oc]
                    P.op("tensor", lambda t, xn_s=xn_s, oc=oc, psb=psb: [
                        t.matmul(psb[:],
                                 xn_s[:, k * 128:(k + 1) * 128],
                                 wg_s[:, k * 1024 + oc * 512: k * 1024 + oc * 512 + 512],
                                 start=(k == 0), stop=(k == 7))
                        for k in range(8)][-1],
                        reads=xn_keys + [K(wg_s, k) for k in range(8)],
                        writes=[K(psb)], tag=f"l3a3_gmm{ti}_{oc}")
                    P.op("scalar", lambda s, gt_s=gt_s, oc=oc, psb=psb: s.activation(
                        gt_s[:, oc * 512:(oc + 1) * 512], psb[:], AF.Silu),
                        reads=[K(psb)], writes=[K(gt_s)], tag=f"l3a3_gsl{ti}_{oc}")
                P.op("vector", lambda v, of_s=of_s, gt_s=gt_s, ob_s=ob_s: v.tensor_mul(
                    ob_s[:], of_s[:], gt_s[:]),
                    reads=[K(of_s), K(gt_s)], writes=[K(ob_s)], tag=f"l3a3_og{ti}")
                def tr8(t, ob_s=ob_s):
                    for k in range(8):
                        inst = t.transpose(ps_t[:, k * 128:(k + 1) * 128],
                                           ob_s[:, k * 128:(k + 1) * 128], ident_p3[:])
                    return inst
                P.op("tensor", tr8, reads=[K(ob_s), K(ident_p3)], writes=[K(ps_t)],
                     tag=f"l3a3_tr{ti}")
                P.op("vector", lambda v, ti=ti: [
                    v.tensor_copy(ofm[:, k * NT + ti * 128: k * NT + (ti + 1) * 128],
                                  ps_t[:, k * 128:(k + 1) * 128])
                    for k in range(8)][-1],
                    reads=[K(ps_t)], writes=[K(ofm, ti)], tag=f"l3a3_oev{ti}")
        P.barrier()
        with ExitStack() as st:
            yb = RotBuf(ctx, st, "l3a3_y", 2, [128, D], F32)
            ps_y = [ctx.psum(st, f"l3a3_psy{i}", [128, 512], F32) for i in range(2)]
            for ti in range(NTILES):
                tok0 = ti * 128
                y_s = yb[ti]
                for oc in range(2):
                    psb = ps_y[oc]
                    P.op("tensor", lambda t, oc=oc, psb=psb, tok0=tok0: [
                        t.matmul(psb[:],
                                 ofm[:, kc * NT + tok0: kc * NT + tok0 + 128],
                                 wo_s[:, kc * 1024 + oc * 512: kc * 1024 + oc * 512 + 512],
                                 start=(kc == 0), stop=(kc == 7))
                        for kc in range(8)][-1],
                        reads=[], writes=[K(psb)], tag=f"l3a3_ymm{ti}_{oc}")
                    P.op("vector", lambda v, y_s=y_s, oc=oc, psb=psb: v.tensor_copy(
                        y_s[:, oc * 512:(oc + 1) * 512], psb[:]),
                        reads=[K(psb)], writes=[K(y_s, oc)], tag=f"l3a3_yev{ti}_{oc}")
                P.dma("sync", lambda s, y_s=y_s, tok0=tok0: s.dma_start(
                    out=s_d[tok0:tok0 + 128, :], in_=y_s[:]),
                    reads=[K(y_s, 0), K(y_s, 1)], writes=[K(s_d, ti)],
                    tag=f"l3a3_yst{ti}")
    P.barrier()


def build_core_program(layers=(0, 1, 2, 3), dbg=()):
    nc = bass.Bass("TRN2", target_bir_lowering=False)
    consts_np = make_consts()
    weights_names = _weight_decls()

    ins = {}
    for name, (shape, dtype) in weights_names.items():
        ins[name] = nc.dram_tensor(name, list(shape), dtype, kind="ExternalInput").ap()
    consts = {}
    for name, arr in consts_np.items():
        dt = BF16 if arr.dtype == ml_dtypes.bfloat16 else F32
        consts[name] = nc.dram_tensor(f"c_{name}", list(arr.shape), dt,
                                      kind="ExternalInput").ap()

    out_d = nc.dram_tensor("out", [NT, D], F32, kind="ExternalOutput").ap()

    prog = Prog(nc)
    stack = ExitStack()
    ctx = Ctx(nc, prog, consts, stack)
    ctx.dbg = set(dbg)
    ctx.ins = ins

    def scratch(name, shape, dtype):
        kind = "ExternalOutput" if name in ctx.dbg else "Internal"
        if name not in ctx.dram:
            ctx.dram[name] = nc.dram_tensor(name, shape, dtype, kind=kind).ap()
        return ctx.dram[name]
    ctx.dram_scratch = scratch

    x_src = ins["x_in"].rearrange("b t d -> (b t) d")
    xn_fm_d = scratch("xn_fm", [D, NT], BF16)
    xn_fm32_d = scratch("xn_fm32", [D, NT], F32)
    xn_f32_d = scratch("xn_f32", [NT, D], F32)
    s_d = scratch("s_tm", [NT, D], F32)
    g_d = scratch("g_tm", [NT, D], F32)
    x_a = scratch("x_a", [NT, D], F32)
    x_b = scratch("x_b", [NT, D], F32)

    cur = x_src
    nxt = [x_a, x_b]
    for li in layers:
        need_f32 = (li == 1) and "no_xnf32" not in DEV_SKIP
        need_fm32 = (li in (2, 3))
        phase_ln(ctx, cur, xn_fm_d, xn_f32_d if need_f32 else None,
                 xn_fm32_d if need_fm32 else None, tag=f"ln{li}")
        if li == 0:
            l0_attn(ctx, xn_fm_d, s_d, ins)
            l0_ffn(ctx, xn_fm_d, g_d, ins)
        elif li == 1:
            skip_ffn = "l1_ffn" in DEV_SKIP
            l1_attn(ctx, xn_fm_d, s_d, ins)
            if not skip_ffn:
                l1_ffn(ctx, xn_f32_d, xn_fm_d, g_d, ins)
            else:
                cur = None
                continue
        elif li == 2:
            l2_attn(ctx, xn_fm_d, s_d, ins)
            l2_ffn(ctx, xn_fm_d, xn_fm32_d, g_d, ins)
        elif li == 3:
            skip_attn = "l3_attn" in DEV_SKIP
            if not skip_attn:
                l3_attn(ctx, xn_fm_d, s_d, ins)
            l3_ffn(ctx, xn_fm_d, xn_fm32_d, g_d, ins)
            if skip_attn:
                cur = None
                continue
        else:
            raise NotImplementedError(f"layer {li}")
        dst = out_d if li == layers[-1] else nxt[li % 2]
        phase_combine(ctx, cur, s_d, g_d, dst, tag=f"cmb{li}")
        cur = dst

    prog.barrier()
    prog.finalize()
    stack.close()
    input_names = list(ins.keys()) + [f"c_{n}" for n in consts_np]
    return nc, consts_np


def _weight_decls():
    d = {"x_in": ((B_LOC, T, D), F32)}
    for w in ("Wqt", "Wkt", "Wvt", "Wot"):
        d[f"l0_{w}"] = ((D, D), F32)
    d["l0_router_w"] = ((D, 8), F32)
    d["l0_w1"] = ((D, 4096), F32)
    d["l0_b1"] = ((4096,), F32)
    d["l0_w2"] = ((4096, D), F32)
    d["l0_b2"] = ((D,), F32)
    for w in ("Wq", "Wk", "Wv"):
        d[f"l2_{w}"] = ((D, D), F32)
    d["l2_proj_wt"] = ((D, D), F32)
    d["l2_proj_b"] = ((D,), F32)
    d["l2_Wrt"] = ((D, 8), F32)
    d["l2_We"] = ((8, D, D), F32)
    for w in ("Wq", "Wk", "Wv", "Wo"):
        d[f"l1_{w}"] = ((D, D), F32)
    d["l1_G1"] = ((D, 64), F32)
    d["l1_G2"] = ((64, D), F32)
    d["l1_gamma"] = ((D,), F32)
    d["l1_beta"] = ((D,), F32)
    d["l1_theta"] = ((16, 1, 64), F32)
    d["l1f_wqkv"] = ((2, D, 1536), F32)
    d["l1f_q_scale"] = ((2, 64), F32)
    d["l1f_k_scale"] = ((2, 64), F32)
    d["l1f_wo"] = ((2, 512, D), F32)
    d["l1f_ln_g"] = ((2, D), F32)
    d["l1f_ln_b"] = ((2, D), F32)
    d["l1f_w1p"] = ((2, D, 5632), F32)
    d["l1f_b1p"] = ((2, 5632), F32)
    d["l1f_w2p"] = ((2, 2816, D), F32)
    d["l1f_b2"] = ((2, D), F32)
    d["l3a_Wq"] = ((D, D), F32)
    d["l3a_Wk"] = ((D, 3072), F32)
    d["l3a_Wv"] = ((D, 3072), F32)
    d["l3a_Wb"] = ((D, 24), F32)
    d["l3a_Wa"] = ((D, 8), F32)
    d["l3a_Wg"] = ((D, D), F32)
    d["l3a_Wo"] = ((D, D), F32)
    d["l3a_conv_q"] = ((D, 4), F32)
    d["l3a_conv_k"] = ((3072, 4), F32)
    d["l3a_conv_v"] = ((3072, 4), F32)
    d["l3a_A_log"] = ((8,), F32)
    d["l3a_dt_bias"] = ((8,), F32)
    d["l3a_rmsw"] = ((128,), F32)
    d["l3f_gwt"] = ((D, 8), F32)
    d["l3f_gb"] = ((8,), F32)
    d["l3f_w1t"] = ((8, D, 4096), F32)
    d["l3f_b1"] = ((8, 4096), F32)
    d["l3f_w2t"] = ((8, 4096, D), F32)
    d["l3f_b2"] = ((8, D), F32)
    return d


def prep_all(inp):
    return prep_inputs(inp)


def run_cores(inp, layers=(0, 1, 2, 3), dbg=(), cores=None):
    if cores is None:
        cores = list(range(NCORES))
    nc, consts_np = build_core_program(layers=layers, dbg=dbg)
    prepped = prep_all(inp)
    in_maps = []
    for ci in cores:
        m = {}
        for name in prepped:
            if name == "x_in":
                m[name] = np.ascontiguousarray(prepped[name][2 * ci:2 * ci + 2])
            else:
                m[name] = prepped[name]
        for cname, arr in consts_np.items():
            m[f"c_{cname}"] = np.asarray(arr)
        in_maps.append(m)
    res = run_bass_kernel_spmd(nc, in_maps, core_ids=list(range(len(cores))))
    return res.results


def kernel(**inputs):
    results = run_cores(inputs, layers=(0, 1, 2, 3))
    out = np.concatenate([np.asarray(r["out"]).reshape(B_LOC, T, D) for r in results],
                         axis=0)
    return out
